# version 15
# baseline (speedup 1.0000x reference)
import contextlib
import numpy as np
import concourse.bass as bass
import concourse.mybir as mybir
from concourse.bass_utils import run_bass_kernel_spmd
from concourse.alu_op_type import AluOpType as AO

F32 = mybir.dt.float32
BF16 = mybir.dt.bfloat16
I32 = mybir.dt.int32
AF = mybir.ActivationFunctionType
AX = mybir.AxisListType

D = 1024
C = 8
NE = 8
HC = 2
NCORES = 8

CEPS, CINV, C30, CHALF, CZERO, CNEG1, CJ0, CM05 = 128, 129, 130, 131, 132, 133, 134, 142
NCONST = 144

B1_0, BBASE0, B2_0, B3_0, BENC0, BDEC0, BDOWN0, BUP0 = 0, 8, 16, 17, 18, 34, 98, 106
NBIAS = 176


class Pipe:
    ENGS = ("sync", "gpsimd", "tensor", "scalar", "vector")
    COMPUTE = ("t", "s", "v", "q")

    def __init__(self, sems):
        self.sems = sems
        self.ops = []
        self.cnt = {k: 0 for k in sems}
        self.chain = {}

    def is_dma(self, sem):
        return sem is not None and sem not in self.COMPUTE and sem != "c"

    def _flat(self, waits):
        for w in waits:
            if w is None:
                continue
            if isinstance(w, list):
                yield from self._flat(w)
            else:
                yield w

    def op(self, eng, emit, waits=(), sem=None, n=1):
        waits = list(waits)
        if sem in self.chain and self.chain[sem] is not None:
            waits.append(self.chain[sem])
        wmax = {}
        for (s, v, peng) in self._flat(waits):
            if v <= 0:
                continue
            assert v <= self.cnt[s], f"wait on future event {s}={v} (cnt={self.cnt[s]})"
            if v > wmax.get(s, 0):
                wmax[s] = v
        ws = list(wmax.items())
        post = None
        if sem is not None:
            self.cnt[sem] += 16 * n if self.is_dma(sem) else n
            post = (sem, self.cnt[sem], eng)
            if sem in self.chain:
                self.chain[sem] = post
        self.ops.append((eng, tuple(ws), emit, sem, n))
        return post

    def wait_only(self, eng, waits):
        self.op(eng, None, waits=waits)

    def emit_into(self, block):
        for eng_name in self.ENGS:
            def mk(eng_name=eng_name):
                def f(eng):
                    lw = {k: 0 for k in self.sems}
                    for (e, ws, emit, sem, n) in self.ops:
                        if e != eng_name:
                            continue
                        for (s, v) in ws:
                            if v > lw[s]:
                                eng.wait_ge(self.sems[s], v)
                                lw[s] = v
                        if emit is None:
                            continue
                        last = emit(eng)
                        if sem is not None:
                            last.then_inc(self.sems[sem],
                                          16 * n if self.is_dma(sem) else n)
                return f
            getattr(block, eng_name)(mk())


class Ctx:
    pass


def F(z, key):
    return z.free.get(key, [])



def plan_prelude(P, z):
    e_con = P.op("gpsimd", lambda e: e.dma_start(out=z.consts[:], in_=z.d_consts), sem="g")
    e_bias = P.op("gpsimd", lambda e: e.dma_start(out=z.bias[:], in_=z.d_biases), sem="g")
    e_ci = P.op("gpsimd", lambda e: e.dma_start(out=z.ci[:], in_=z.d_ci), sem="g")
    nt, ttot = z.nt, z.ttot
    TT = ttot // nt
    xd3 = z.d_x.rearrange("p (c t) -> p c t", c=C)
    xs3 = z.x[:].rearrange("p (c t) -> p c t", c=C)
    z.ev_xtile = []
    for i in range(nt):
        z.ev_xtile.append(P.op(
            "gpsimd", lambda e, i=i: e.dma_start(
                out=xs3[:, :, i * TT:(i + 1) * TT], in_=xd3[:, :, i * TT:(i + 1) * TT]),
            sem="g"))
    z.ev_x = z.ev_xtile[-1]
    z.ev_con = e_con
    z.ev_bias = e_bias
    z.ev_cif = P.op("vector", lambda e: e.tensor_copy(z.cif[:], z.ci[:]),
                    waits=[e_ci], sem="v")
    z.ev_halfbase = P.op(
        "vector", lambda e: e.tensor_scalar(
            z.halfbase[:, 0:8], z.bias[:, BBASE0:BBASE0 + 8],
            z.consts[:, CHALF:CHALF + 1], None, op0=AO.mult),
        waits=[e_bias, e_con], sem="v")
    z.ev_rb0 = P.op(
        "vector", lambda e: e.tensor_scalar(
            z.rowbuf[0:1, 0:64], z.consts[0:1, 0:64],
            z.consts[0:1, CZERO:CZERO + 1], None, op0=AO.mult),
        waits=[e_con], sem="v")


def plan_stats_xn(P, z, gate):
    ttot, nt = z.ttot, z.nt
    TT = ttot // nt
    x = z.x
    free_p0 = F(z, "p0")
    free_p1 = F(z, "p1")
    ev_tile = []
    e6 = None
    ev_Q_prev = None
    ev_qmm = {}
    sc3_guard = F(z, "sc3")
    per_tile = bool(gate) and isinstance(gate[0], list) and len(gate) == nt
    gates = gate if per_tile else [gate] * nt
    for i in range(nt):
        gate = gates[i]
        t0 = i * TT

        def emit_S(e, t0=t0, TT=TT):
            last = None
            for c in range(C):
                last = e.matmul(z.p[0][:, 0:TT], z.consts[:, 0:128],
                                x[:, c * ttot + t0: c * ttot + t0 + TT],
                                start=(c == 0), stop=(c == C - 1))
            return last
        ev_S = P.op("tensor", emit_S,
                    waits=[gate, z.ev_xtile[i], z.ev_con, free_p0], sem="t")

        ev_sq = [None, None]
        for c in range(C):
            scr = (z.sc0, z.sc1)[c % 2]
            if c >= 2:
                w_extra = [ev_qmm.get(c % 2)]
            else:
                w_extra = [ev_qmm.get(c % 2), F(z, ("sc0", "sc1")[c % 2]), free_p1]
            ev_sq[c % 2] = P.op(
                "scalar",
                lambda e, c=c, scr=scr, t0=t0, TT=TT: e.activation(
                    out=scr[:, 0:TT], in_=x[:, c * ttot + t0: c * ttot + t0 + TT],
                    func=AF.Square),
                waits=[gate, z.ev_xtile[i]] + w_extra, sem="s")
            ev_qmm[c % 2] = P.op(
                "tensor",
                lambda e, c=c, scr=scr, TT=TT: e.matmul(
                    z.p[1][:, 0:TT], z.consts[:, 0:128], scr[:, 0:TT],
                    start=(c == 0), stop=(c == C - 1)),
                waits=[ev_sq[c % 2]], sem="t")
        ev_Q = ev_qmm[(C - 1) % 2]
        ev_Q_prev = ev_Q

        e1 = P.op("vector", lambda e, TT=TT: e.tensor_scalar(
            z.scrM[:, 0:TT], z.p[0][:, 0:TT], z.consts[:, CINV:CINV + 1], None,
            op0=AO.mult), waits=[ev_S, ev_tile[-1] if ev_tile else None], sem="v")
        e2 = P.op("vector", lambda e, TT=TT: e.tensor_scalar(
            z.sc2[:, 0:TT], z.p[1][:, 0:TT], z.consts[:, CINV:CINV + 1], None,
            op0=AO.mult), waits=[ev_Q, F(z, "sc2")], sem="v")
        free_p0, free_p1 = [e1], [e2]
        e3 = P.op("vector", lambda e, TT=TT: e.tensor_tensor(
            z.scrR[:, 0:TT], z.scrM[:, 0:TT], z.scrM[:, 0:TT], op=AO.mult),
            waits=[e1, e6], sem="v")
        e4 = P.op("vector", lambda e, TT=TT: e.tensor_tensor(
            z.sc2[:, 0:TT], z.sc2[:, 0:TT], z.scrR[:, 0:TT], op=AO.subtract),
            waits=[e3], sem="v")
        e5 = P.op("scalar", lambda e, TT=TT: e.activation(
            out=z.sc3[:, 0:TT], in_=z.sc2[:, 0:TT], func=AF.Ln,
            bias=z.consts[:, CEPS:CEPS + 1]),
            waits=[e4, sc3_guard, ev_tile[-1] if ev_tile else None], sem="s")
        e6 = P.op("scalar", lambda e, TT=TT: e.activation(
            out=z.scrR[:, 0:TT], in_=z.sc3[:, 0:TT], func=AF.Exp,
            scale=z.consts[:, CM05:CM05 + 1]), waits=[e5], sem="s")

        elast = None
        for c in range(C):
            off = c * ttot + t0
            e7 = P.op("vector", lambda e, off=off, TT=TT: e.tensor_tensor(
                z.sc3[:, 0:TT], x[:, off: off + TT], z.scrM[:, 0:TT],
                op=AO.subtract), waits=[e1, elast, e6], sem="v")
            e8 = P.op("vector", lambda e, TT=TT: e.tensor_tensor(
                z.sc3[:, 0:TT], z.sc3[:, 0:TT], z.scrR[:, 0:TT],
                op=AO.mult), waits=[e6, e7], sem="v")
            e9 = P.op("vector", lambda e, off=off, TT=TT: e.tensor_copy(
                z.xnh[:, off: off + TT], z.sc3[:, 0:TT]), waits=[e8], sem="v")
            elast = P.op("vector", lambda e, off=off, TT=TT: e.tensor_tensor(
                z.xnl[:, off: off + TT], z.sc3[:, 0:TT], z.xnh[:, off: off + TT],
                op=AO.subtract), waits=[e9], sem="v")
        ev_tile.append(elast)
    z.free["p0"] = free_p0
    z.free["p1"] = free_p1
    z.free["sc0"] = [ev_qmm[0]]
    z.free["sc1"] = [ev_qmm[1]]
    z.free["sc2"] = [e6]
    z.free["sc3"] = [ev_tile[-1]]
    return ev_tile


def _mm3(e, psum_ap, wsp, whoff, wloff, wslice, xh, xl, xslices, first):
    last = None
    passes = [(whoff, xh), (whoff, xl), (wloff, xh)]
    for pi, (woff, xs) in enumerate(passes):
        for c in range(C):
            w0, w1 = wslice(c)
            x0, x1 = xslices(c)
            last = e.matmul(psum_ap, wsp[:, woff + w0: woff + w1],
                            xs[:, x0:x1],
                            start=(pi == 0 and c == 0 and first),
                            stop=(pi == 2 and c == C - 1))
    return last


def plan_wstream(P, z, dram_src, target, tgt_guard):
    ev_dma = P.op("sync", lambda e: e.dma_start(out=z.wtrans[:], in_=dram_src),
                  waits=[F(z, "wtrans")], sem="we0")
    ev_hi = P.op("vector", lambda e: e.tensor_copy(target[:, 0:2048], z.wtrans[:]),
                 waits=[ev_dma, tgt_guard], sem="v")
    ev_lo = P.op("vector", lambda e: e.tensor_tensor(
        target[:, 2048:4096], z.wtrans[:], target[:, 0:2048], op=AO.subtract),
        waits=[ev_hi], sem="v")
    z.free["wtrans"] = [ev_lo]
    return ev_lo


def plan_experts(P, z, ev_xn, tail_hook=None, pre=None):
    ttot, nt = z.ttot, z.nt
    TT = ttot // nt
    ev_enc_sp = [None, None]
    ev_dec_sp = None
    enc_guard = [F(z, "wencsp0"), F(z, "wencsp1")]
    dec_guard = F(z, "wdecsp")
    ev_enc_mm_last = [None, None]
    ev_dec_mm_last = None

    def load_enc(n):
        b = n % 2
        ev_enc_sp[b] = plan_wstream(P, z, z.d_wenc[n], z.wencsp[b],
                                    [enc_guard[b], ev_enc_mm_last[b]])

    def load_dec(n):
        nonlocal ev_dec_sp
        ev_dec_sp = plan_wstream(P, z, z.d_wdec[n], z.wdecsp,
                                 [dec_guard, ev_dec_mm_last])

    if pre is not None:
        ev_enc_sp[0], ev_dec_sp, ev_enc_sp[1] = pre
    else:
        load_enc(0)
        load_dec(0)
        load_enc(1)

    hp_free = {(0, 0): F(z, "p2"), (0, 1): F(z, "p3"),
               (1, 0): F(z, "p6"), (1, 1): F(z, "p7")}
    rbank = (z.p[4], z.p[5], z.p[0], z.p[1])
    rc_free = {0: F(z, "p4"), 1: F(z, "p5"), 2: F(z, "p0"), 3: F(z, "p1")}
    h32_guard = {0: F(z, "sc4"), 1: F(z, "sc5")}
    hsp_guard = {0: F(z, "hsp0"), 1: F(z, "hsp1")}
    ev_d = {0: F(z, "sc0"), 1: F(z, "sc1")}
    ev_sq = {0: F(z, "sc2"), 1: F(z, "sc3")}
    slots_guard = F(z, "slots")
    ev_hsplit = {}

    state = {"last_enc_mm": None, "last_dec_mm": None}

    def enc_tile(n, b, i, my_enc_sp):
        t0 = i * TT
        hspb = (z.hsp0, z.hsp1)[i % 2]
        pbank = (2, 6)[i % 2]
        for hc in range(HC):
            def emit_enc(e, b=b, hc=hc, t0=t0, pbank=pbank, TT=TT):
                return _mm3(
                    e, z.p[pbank + hc][:, 0:TT], z.wencsp[b], 0, 2048,
                    lambda c, hc=hc: (c * 256 + hc * 128, c * 256 + hc * 128 + 128),
                    z.xnh, z.xnl,
                    lambda c, t0=t0: (c * ttot + t0, c * ttot + t0 + TT),
                    first=True)
            ev_mm = P.op("tensor", emit_enc,
                         waits=[ev_xn[i], my_enc_sp, hp_free[(i % 2, hc)]],
                         sem="t")
            state["last_enc_mm"] = ev_mm
            h32 = (z.sc4, z.sc5)[hc]
            ev_relu = P.op(
                "scalar",
                lambda e, n=n, hc=hc, h32=h32, pbank=pbank, TT=TT: e.activation(
                    out=h32[:, 0:TT], in_=z.p[pbank + hc][:, 0:TT],
                    func=AF.Relu,
                    bias=z.bias[:, BENC0 + n * 2 + hc: BENC0 + n * 2 + hc + 1]),
                waits=[ev_mm, z.ev_bias, h32_guard[hc]], sem="s")
            hp_free[(i % 2, hc)] = [ev_relu]
            ev_hi = P.op("vector", lambda e, hc=hc, h32=h32, hspb=hspb, TT=TT:
                         e.tensor_copy(hspb[:, hc * TT: hc * TT + TT],
                                       h32[:, 0:TT]),
                         waits=[ev_relu, hsp_guard[i % 2]], sem="v")
            ev_lo = P.op("vector", lambda e, hc=hc, h32=h32, hspb=hspb, TT=TT:
                         e.tensor_tensor(
                             hspb[:, 2 * TT + hc * TT: 2 * TT + hc * TT + TT],
                             h32[:, 0:TT], hspb[:, hc * TT: hc * TT + TT],
                             op=AO.subtract),
                         waits=[ev_hi], sem="v")
            h32_guard[hc] = [ev_lo]
            ev_hsplit[(i % 2, hc)] = ev_lo

    pend = [None]

    def dec_tile(n, i, my_dec_sp):
        t0 = i * TT
        hspb = (z.hsp0, z.hsp1)[i % 2]
        for c in range(C):
            cb = c % 2
            cb4 = c % 4
            bank = rbank[cb4]
            scrRb = (z.sc0, z.sc1)[cb]
            scrDb = (z.sc2, z.sc3)[cb]

            def emit_dec(e, c=c, bank=bank, hspb=hspb, TT=TT):
                last = None
                first = True
                for (woff, hoff) in ((0, 0), (0, 2 * TT), (2048, 0)):
                    for hc in range(HC):
                        last = e.matmul(
                            bank[:, 0:TT],
                            z.wdecsp[:, woff + hc * 1024 + c * 128:
                                     woff + hc * 1024 + c * 128 + 128],
                            hspb[:, hoff + hc * TT: hoff + hc * TT + TT],
                            start=first, stop=(woff == 2048 and hc == HC - 1))
                        first = False
                return last
            ev_dmm = P.op(
                "tensor", emit_dec,
                waits=[ev_hsplit[(i % 2, 0)], ev_hsplit[(i % 2, 1)], my_dec_sp,
                       rc_free[cb4]], sem="t")
            state["last_dec_mm"] = ev_dmm
            ev_rb = P.op(
                "scalar",
                lambda e, n=n, c=c, bank=bank, scrRb=scrRb, TT=TT: e.activation(
                    out=scrRb[:, 0:TT], in_=bank[:, 0:TT], func=AF.Identity,
                    bias=z.bias[:, BDEC0 + n * 8 + c: BDEC0 + n * 8 + c + 1]),
                waits=[ev_dmm, z.ev_bias, ev_d[cb], ev_sq[cb]], sem="s")
            rc_free[cb4] = [ev_rb]
            ev_d1 = P.op(
                "vector",
                lambda e, c=c, t0=t0, scrRb=scrRb, scrDb=scrDb, TT=TT:
                e.tensor_tensor(
                    scrDb[:, 0:TT], scrRb[:, 0:TT],
                    z.xnh[:, c * ttot + t0: c * ttot + t0 + TT], op=AO.subtract),
                waits=[ev_rb, ev_sq[cb]], sem="v")
            ev_d[cb] = P.op(
                "vector",
                lambda e, c=c, t0=t0, scrDb=scrDb, TT=TT: e.tensor_tensor(
                    scrDb[:, 0:TT], scrDb[:, 0:TT],
                    z.xnl[:, c * ttot + t0: c * ttot + t0 + TT], op=AO.subtract),
                waits=[ev_d1], sem="v")
            col = n * (nt * C) + i * C + c
            if pend[0] is not None:
                (pcb, pcol, pRb, pDb, pdev) = pend[0]
                ev_sq[pcb] = P.op(
                    "scalar",
                    lambda e, pRb=pRb, pDb=pDb, pcol=pcol, TT=TT: e.activation(
                        out=pRb[:, 0:TT], in_=pDb[:, 0:TT], func=AF.Square,
                        accum_out=z.slots[:, pcol:pcol + 1]),
                    waits=[pdev, slots_guard], sem="s")
            pend[0] = (cb, col, scrRb, scrDb, ev_d[cb])
        (pcb, pcol, pRb, pDb, pdev) = pend[0]
        ev_sq[pcb] = P.op(
            "scalar",
            lambda e, pRb=pRb, pDb=pDb, pcol=pcol, TT=TT: e.activation(
                out=pRb[:, 0:TT], in_=pDb[:, 0:TT], func=AF.Square,
                accum_out=z.slots[:, pcol:pcol + 1]),
            waits=[pdev, slots_guard], sem="s")
        pend[0] = None

    for n in range(NE):
        b = n % 2
        my_enc_sp = ev_enc_sp[b]
        my_dec_sp = ev_dec_sp
        for step in range(nt + 1):
            if n == NE - 1 and step == nt and tail_hook is not None:
                tail_hook(ev_enc_mm_last[0], state["last_enc_mm"])
            if step < nt:
                enc_tile(n, b, step, my_enc_sp)
            if step >= 1:
                dec_tile(n, step - 1, my_dec_sp)
        ev_enc_mm_last[b] = state["last_enc_mm"]
        ev_dec_mm_last = state["last_dec_mm"]
        if n + 1 < NE:
            load_dec(n + 1)
        if n + 2 < NE:
            load_enc(n + 2)
    last_dec_mm = state["last_dec_mm"]

    z.free["p2"] = hp_free[(0, 0)]
    z.free["p3"] = hp_free[(0, 1)]
    z.free["p6"] = hp_free[(1, 0)]
    z.free["p7"] = hp_free[(1, 1)]
    z.free["p4"] = rc_free[0]
    z.free["p5"] = rc_free[1]
    z.free["p0"] = rc_free[2]
    z.free["p1"] = rc_free[3]
    z.free["hsp0"] = [last_dec_mm]
    z.free["hsp1"] = [last_dec_mm]
    z.free["sc4"] = h32_guard[0]
    z.free["sc5"] = h32_guard[1]
    z.free["wencsp0"] = [ev_enc_mm_last[0]]
    z.free["wencsp1"] = [ev_enc_mm_last[1]]
    z.free["wdecsp"] = [ev_dec_mm_last]
    z.free["sc0"] = [ev_d[0], ev_sq[0]]
    z.free["sc1"] = [ev_d[1], ev_sq[1]]
    z.free["sc2"] = [ev_d[0], ev_sq[0]]
    z.free["sc3"] = [ev_d[1], ev_sq[1]]
    return [ev_sq[0], ev_sq[1]]


def plan_loss_ar_issue(P, z, call, ev_sq_pair):
    nslot = z.nt * C
    ev_red = None
    for n in range(NE):
        ev_red = P.op("vector", lambda e, n=n, nslot=nslot: e.tensor_reduce(
            z.lacc[:, n:n + 1], z.slots[:, n * nslot:(n + 1) * nslot], axis=AX.X,
            op=AO.add), waits=[ev_sq_pair, F(z, "lacc")], sem="v")
    z.free["slots"] = [ev_red]
    ev_pl = P.op("tensor", lambda e: e.matmul(
        z.p[6][0:1, 0:8], z.consts[:, 0:1], z.lacc[:, 0:8], start=True, stop=True),
        waits=[ev_red, F(z, "p6")], sem="t")
    ev_cp = P.op("vector", lambda e: e.tensor_copy(z.arst[0:1, 0:8], z.p[6][0:1, 0:8]),
                 waits=[ev_pl, F(z, "arst")], sem="v")
    z.free["lacc"] = [ev_pl]
    ev_in = P.op("gpsimd", lambda e, call=call: e.dma_start(
        out=z.d_arin[call], in_=z.arst[0:1, 0:8]), waits=[ev_cp], sem="g")
    z.free["arst"] = [ev_in]
    ev_cc = P.op("gpsimd", lambda e, call=call: e.collective_compute(
        "AllReduce", AO.add,
        replica_groups=[[0, 1], [2, 3], [4, 5], [6, 7]],
        ins=[z.d_arin[call].opt()], outs=[z.d_arout[call].opt()]),
        waits=[ev_in], sem="c")
    ev_out = P.op("gpsimd", lambda e, call=call: e.dma_start(
        out=z.lrow[0:1, 0:8], in_=z.d_arout[call]),
        waits=[ev_cc, F(z, "lrow")], sem="g")
    z.ar_evcp = ev_cp
    return ev_out


def plan_mask(P, z, ev_out):
    ev_cp = z.ar_evcp
    rb = z.rowbuf
    ev_min = P.op("vector", lambda e: e.tensor_reduce(
        rb[0:1, 8:9], z.lrow[0:1, 0:8], axis=AX.X, op=AO.min),
        waits=[ev_out, z.ev_rb0, F(z, "rowbuf")], sem="v")
    ev_eq = P.op("vector", lambda e: e.tensor_scalar(
        rb[0:1, 16:24], z.lrow[0:1, 0:8], rb[0:1, 8:9], None, op0=AO.is_le),
        waits=[ev_min], sem="v")
    z.free["lrow"] = [ev_eq]
    ev_run = P.op("vector", lambda e: e.tensor_copy(
        rb[0:1, 24:25], z.consts[0:1, CZERO:CZERO + 1]), waits=[z.ev_con, ev_eq],
        sem="v")
    ev_nr = None
    for n in range(NE):
        ev_nr = P.op("vector", lambda e, n=n: e.tensor_scalar(
            rb[0:1, 56:57], rb[0:1, 24 + n:25 + n], z.consts[0:1, CNEG1:CNEG1 + 1],
            z.consts[0:1, 0:1], op0=AO.mult, op1=AO.add), waits=[ev_run, ev_eq], sem="v")
        ev_mk = P.op("vector", lambda e, n=n: e.tensor_tensor(
            rb[0:1, 33 + n:34 + n], rb[0:1, 16 + n:17 + n], rb[0:1, 56:57],
            op=AO.mult), waits=[ev_nr], sem="v")
        ev_run = P.op("vector", lambda e, n=n: e.tensor_tensor(
            rb[0:1, 25 + n:26 + n], rb[0:1, 24 + n:25 + n], rb[0:1, 33 + n:34 + n],
            op=AO.add), waits=[ev_mk], sem="v")
    ev_adp = None
    for j in range(NE):
        ev_e = P.op("vector", lambda e, j=j: e.tensor_scalar(
            rb[0:1, 48:56], z.cif[0:1, 0:8], z.consts[0:1, CJ0 + j:CJ0 + j + 1], None,
            op0=AO.is_equal), waits=[z.ev_cif, ev_run, ev_adp, ev_nr], sem="v")
        ev_p = P.op("vector", lambda e: e.tensor_tensor(
            rb[0:1, 48:56], rb[0:1, 48:56], rb[0:1, 33:41], op=AO.mult),
            waits=[ev_e], sem="v")
        ev_adp = P.op("vector", lambda e, j=j: e.tensor_reduce(
            rb[0:1, 41 + j:42 + j], rb[0:1, 48:56], axis=AX.X, op=AO.add),
            waits=[ev_p], sem="v")
    z.free["rowbuf"] = [ev_adp]
    z.ar_evadp = ev_adp
    return ev_adp


def plan_mask_bc(P, z):
    rb = z.rowbuf
    ev_bc = P.op("tensor", lambda e: e.matmul(
        z.p[6][:, 0:8], z.consts[0:1, 0:128], rb[0:1, 41:49], start=True, stop=True),
        waits=[z.ar_evadp, z.ar_evcp, F(z, "p6")], sem="t")
    ev_mskb = P.op("vector", lambda e: e.tensor_copy(z.mskb[:, 0:8], z.p[6][:, 0:8]),
                   waits=[ev_bc, F(z, "mskb")], sem="v")
    z.free["p6"] = [ev_mskb]
    return ev_mskb


def plan_select(P, z, ev_mskb):
    gd = F(z, "adapters")
    ev_wd = gd
    ev_wu = gd
    ev_bd = gd
    ev_bu = gd
    sc4_guard = F(z, "sc4")
    sc5_guard = F(z, "sc5")
    hsp0f = z.hsp0[:].bitcast(F32)
    hsp1f = z.hsp1[:].bitcast(F32)
    bufs = [(z.selbuf[:, 0:512], z.selbuf[0:64, 512:1536]),
            (hsp1f[:, 0:512], hsp0f[0:64, 0:1024])]
    ev_free = [F(z, "selbuf"), F(z, "selodd") + F(z, "hsp0") + F(z, "hsp1")]
    ev_loads = {}

    def load(n):
        b = n % 2
        wd_ap, wu_ap = bufs[b]
        sem = "ws" if b == 0 else "ws2"
        P.op("sync", lambda e, n=n, wd_ap=wd_ap: e.dma_start(
            out=wd_ap, in_=z.d_wdown[n]), waits=[ev_free[b]], sem=sem)
        ev_loads[n] = P.op("sync", lambda e, n=n, wu_ap=wu_ap: e.dma_start(
            out=wu_ap, in_=z.d_wup[n]), sem=sem)

    load(0)
    load(1)
    for n in range(NE):
        sb_wd, sb_wu = bufs[n % 2]
        ev_load = ev_loads[n]
        if n == 0:
            ev_wd = P.op("vector", lambda e, sb_wd=sb_wd: e.tensor_scalar(
                z.sc2[:, 0:512], sb_wd, z.mskb[:, 0:1], None, op0=AO.mult),
                waits=[ev_mskb, ev_load, ev_wd, F(z, "sc2")], sem="v")
            ev_wu = P.op("vector", lambda e, sb_wu=sb_wu: e.tensor_scalar(
                z.wupsel[0:64, :], sb_wu, z.mskb[0:64, 0:1], None,
                op0=AO.mult), waits=[ev_mskb, ev_load, ev_wu], sem="v")
            ev_bd = P.op("vector", lambda e: e.tensor_scalar(
                z.bsel[0:64, 0:1], z.bias[0:64, BDOWN0:BDOWN0 + 1],
                z.mskb[0:64, 0:1], None, op0=AO.mult),
                waits=[ev_mskb, z.ev_bias, ev_bd], sem="v")
            ev_bu = P.op("vector", lambda e: e.tensor_scalar(
                z.bsel[:, 2:10], z.bias[:, BUP0:BUP0 + 8],
                z.mskb[:, 0:1], None, op0=AO.mult),
                waits=[ev_mskb, z.ev_bias, ev_bu], sem="v")
        else:
            ev_s1 = P.op("vector", lambda e, sb_wd=sb_wd, n=n: e.tensor_scalar(
                z.sc4[:, 0:512], sb_wd, z.mskb[:, n:n + 1], None, op0=AO.mult),
                waits=[ev_mskb, ev_load, ev_wd, sc4_guard], sem="v")
            ev_wd = P.op("vector", lambda e: e.tensor_tensor(
                z.sc2[:, 0:512], z.sc2[:, 0:512], z.sc4[:, 0:512], op=AO.add),
                waits=[ev_s1], sem="v")
            for half in range(2):
                ev_s2 = P.op("vector", lambda e, sb_wu=sb_wu, n=n, half=half:
                             e.tensor_scalar(
                                 z.sc5[0:64, 0:512],
                                 sb_wu[0:64, half * 512:512 + half * 512],
                                 z.mskb[0:64, n:n + 1], None, op0=AO.mult),
                    waits=[ev_load, ev_wu, sc5_guard], sem="v")
                ev_wu = P.op("vector", lambda e, half=half: e.tensor_tensor(
                    z.wupsel[0:64, half * 512:512 + half * 512],
                    z.wupsel[0:64, half * 512:512 + half * 512],
                    z.sc5[0:64, 0:512], op=AO.add), waits=[ev_s2], sem="v")
            ev_b1 = P.op("vector", lambda e, n=n: e.tensor_scalar(
                z.bsel[0:64, 34:35], z.bias[0:64, BDOWN0 + n:BDOWN0 + n + 1],
                z.mskb[0:64, n:n + 1], None, op0=AO.mult), waits=[ev_bd], sem="v")
            ev_bd = P.op("vector", lambda e: e.tensor_tensor(
                z.bsel[0:64, 0:1], z.bsel[0:64, 0:1],
                z.bsel[0:64, 34:35], op=AO.add), waits=[ev_b1], sem="v")
            ev_b2 = P.op("vector", lambda e, n=n: e.tensor_scalar(
                z.bsel[:, 26:34], z.bias[:, BUP0 + n * 8:BUP0 + n * 8 + 8],
                z.mskb[:, n:n + 1], None, op0=AO.mult), waits=[ev_bu], sem="v")
            ev_bu = P.op("vector", lambda e: e.tensor_tensor(
                z.bsel[:, 2:10], z.bsel[:, 2:10],
                z.bsel[:, 26:34], op=AO.add), waits=[ev_b2], sem="v")
        ev_free[n % 2] = [ev_wu]
        if n + 2 < NE:
            load(n + 2)
    ev_bush = P.op("vector", lambda e: e.tensor_scalar(
        z.bsel[:, 18:26], z.bsel[:, 2:10],
        z.consts[:, CHALF:CHALF + 1], None, op0=AO.mult), waits=[ev_bu], sem="v")
    ev_wdh = P.op("vector", lambda e: e.tensor_copy(z.wdselh[:], z.sc2[:, 0:512]),
                  waits=[ev_wd, F(z, "wdselsp")], sem="v")
    ev_wdl = P.op("vector", lambda e: e.tensor_tensor(
        z.wdsell[:], z.sc2[:, 0:512], z.wdselh[:], op=AO.subtract),
        waits=[ev_wdh], sem="v")
    ev_wuh = P.op("vector", lambda e: e.tensor_copy(
        z.wupselh[0:64, :], z.wupsel[0:64, :]), waits=[ev_wu, F(z, "wupselsp")], sem="v")
    ev_wul = P.op("vector", lambda e: e.tensor_tensor(
        z.wupsell[0:64, :], z.wupsel[0:64, :], z.wupselh[0:64, :], op=AO.subtract),
        waits=[ev_wuh], sem="v")
    z.free["selbuf"] = ev_free[0]
    z.free["selodd"] = ev_free[1]
    z.free["hsp0"] = F(z, "hsp0") + ev_free[1]
    z.free["hsp1"] = F(z, "hsp1") + ev_free[1]
    z.free["sc2"] = [ev_wdl]
    z.free["sc4"] = [ev_wdh]
    z.free["sc5"] = [ev_wul]
    return ev_wdl, ev_wul, ev_bd, ev_bush


def plan_base(P, z, ev_xn, pre=None):
    ttot, nt = z.ttot, z.nt
    TT = ttot // nt
    ev_sp = [None, None]
    sp_guard = [F(z, "wencsp0"), F(z, "wencsp1")]
    ev_mm_free = [None, None]
    quarters = [(i, q) for i in range(nt) for q in range(4)]

    def load(k):
        i, q = quarters[k]
        b = k % 2
        ev_sp[b] = plan_wstream(
            P, z, z.d_wbase[:, q * 2048:(q + 1) * 2048], z.wencsp[b],
            [sp_guard[b], ev_mm_free[b]])
        sp_guard[b] = None

    if pre is not None:
        ev_sp[0], ev_sp[1] = pre
        sp_guard = [None, None]
    else:
        load(0)
        load(1)
    ev_evac = {0: F(z, "p2"), 1: F(z, "p3"), 2: F(z, "p4"), 3: F(z, "p5")}
    ev_add = {0: F(z, "sc0"), 1: F(z, "sc1")}
    ev_last_add = None
    tile_adds = []
    for k, (i, q) in enumerate(quarters):
        b = k % 2
        t0 = i * TT
        for eo2 in range(2):
            eo = q * 2 + eo2
            pb = 2 + 2 * (q % 2) + eo2

            def emit(e, b=b, eo2=eo2, t0=t0, pb=pb, TT=TT):
                return _mm3(
                    e, z.p[pb][:, 0:TT], z.wencsp[b], 0, 2048,
                    lambda c, eo2=eo2: (c * 256 + eo2 * 128, c * 256 + eo2 * 128 + 128),
                    z.xnh, z.xnl,
                    lambda c, t0=t0: (c * ttot + t0, c * ttot + t0 + TT),
                    first=True)
            ev_mm = P.op("tensor", emit,
                         waits=[ev_xn[i], ev_sp[b], ev_evac[pb - 2]], sem="t")
            if eo2 == 1:
                ev_mm_free[b] = ev_mm
            scr = (z.sc0, z.sc1)[eo % 2]
            ev_evac[pb - 2] = P.op(
                "scalar", lambda e, eo=eo, pb=pb, scr=scr, TT=TT: e.activation(
                    out=scr[:, 0:TT], in_=z.p[pb][:, 0:TT], func=AF.Identity,
                    scale=z.consts[:, CHALF:CHALF + 1],
                    bias=z.halfbase[:, eo:eo + 1]),
                waits=[ev_mm, z.ev_halfbase, ev_add[eo % 2]], sem="s")
            off = eo * ttot + t0
            ev_add[eo % 2] = P.op(
                "gpsimd", lambda e, off=off, scr=scr, pb=pb, TT=TT: e.tensor_tensor(
                    z.x[:, off:off + TT], z.x[:, off:off + TT], scr[:, 0:TT],
                    op=AO.add), waits=[ev_evac[pb - 2], ev_add[eo % 2]], sem="q")
            ev_last_add = ev_add[eo % 2]
        if q == 3:
            tile_adds.append([ev_add[0], ev_add[1]])
        if k + 2 < len(quarters):
            load(k + 2)
    z.free["p2"], z.free["p3"] = [ev_evac[0]], [ev_evac[1]]
    z.free["p4"], z.free["p5"] = [ev_evac[2]], [ev_evac[3]]
    z.free["wencsp0"] = [ev_mm_free[0]]
    z.free["wencsp1"] = [ev_mm_free[1]]
    z.free["sc0"] = [ev_add[0]]
    z.free["sc1"] = [ev_add[1]]
    return ev_last_add, [ev_add[0], ev_add[1]], tile_adds


def plan_adapter(P, z, ev_xn, ev_sel):
    ttot, nt = z.ttot, z.nt
    TT = ttot // nt
    ev_wdl, ev_wul, ev_bd, ev_bush = ev_sel
    mid_guard = F(z, "sc4")
    msp_guard = F(z, "msp")
    ev_up_evac = {0: F(z, "p3"), 1: F(z, "p4")}
    ev_add = {0: F(z, "sc2"), 1: F(z, "sc3")}
    ev_mid_free = F(z, "p2")
    ev_up_last = None
    ev_msp = None
    ev_mid_relu = None
    ev_last = None
    tile_adds = []
    for i in range(nt):
        t0 = i * TT

        def emit_mid(e, t0=t0, TT=TT):
            last = None
            first = True
            for (wt, xt) in ((z.wdselh, z.xnh), (z.wdselh, z.xnl), (z.wdsell, z.xnh)):
                for c in range(C):
                    last = e.matmul(
                        z.p[2][0:64, 0:TT], wt[:, c * 64:(c + 1) * 64],
                        xt[:, c * ttot + t0: c * ttot + t0 + TT],
                        start=first, stop=(wt is z.wdsell and c == C - 1))
                    first = False
            return last
        ev_mid = P.op("tensor", emit_mid,
                      waits=[ev_xn[i], ev_wdl, ev_mid_free], sem="t")
        ev_mid_relu = P.op("scalar", lambda e, TT=TT: e.activation(
            out=z.sc4[0:64, 0:TT], in_=z.p[2][0:64, 0:TT], func=AF.Relu,
            bias=z.bsel[0:64, 0:1]),
            waits=[ev_mid, ev_bd, mid_guard, ev_msp], sem="s")
        ev_mid_free = [ev_mid_relu]
        ev_mh = P.op("vector", lambda e, TT=TT: e.tensor_copy(
            z.msph[0:64, 0:TT], z.sc4[0:64, 0:TT]),
            waits=[ev_mid_relu, msp_guard, ev_up_last], sem="v")
        ev_msp = P.op("vector", lambda e, TT=TT: e.tensor_tensor(
            z.mspl[0:64, 0:TT], z.sc4[0:64, 0:TT], z.msph[0:64, 0:TT],
            op=AO.subtract), waits=[ev_mh], sem="v")
        mid_guard = [ev_msp]
        for eo in range(C):
            pb = 3 + eo % 2

            def emit_up(e, eo=eo, pb=pb, TT=TT):
                last = None
                first = True
                for (wt, mt) in ((z.wupselh, z.msph), (z.wupselh, z.mspl),
                                 (z.wupsell, z.msph)):
                    last = e.matmul(
                        z.p[pb][:, 0:TT], wt[0:64, eo * 128:(eo + 1) * 128],
                        mt[0:64, 0:TT], start=first,
                        stop=(wt is z.wupsell))
                    first = False
                return last
            ev_up = P.op("tensor", emit_up,
                         waits=[ev_msp, ev_wul, ev_up_evac[eo % 2]], sem="t")
            ev_up_last = ev_up
            scr = (z.sc2, z.sc3)[eo % 2]
            ev_up_evac[eo % 2] = P.op(
                "scalar", lambda e, eo=eo, pb=pb, scr=scr, TT=TT: e.activation(
                    out=scr[:, 0:TT], in_=z.p[pb][:, 0:TT], func=AF.Identity,
                    scale=z.consts[:, CHALF:CHALF + 1],
                    bias=z.bsel[:, 18 + eo:19 + eo]),
                waits=[ev_up, ev_bush, ev_add[eo % 2]], sem="s")
            off = eo * ttot + t0
            ev_add[eo % 2] = P.op(
                "gpsimd", lambda e, off=off, scr=scr, TT=TT: e.tensor_tensor(
                    z.x[:, off:off + TT], z.x[:, off:off + TT], scr[:, 0:TT],
                    op=AO.add), waits=[ev_up_evac[eo % 2], ev_add[eo % 2]], sem="q")
            ev_last = ev_add[eo % 2]
        tile_adds.append([ev_add[0], ev_add[1]])
    z.free["p2"] = ev_mid_free
    z.free["p3"] = [ev_up_evac[0]]
    z.free["p4"] = [ev_up_evac[1]]
    z.free["sc4"] = [ev_msp]
    z.free["msp"] = [ev_up_last]
    z.free["adapters"] = [ev_up_last, ev_up_evac[0], ev_up_evac[1], ev_mid_relu]
    z.free["wdselsp"] = [ev_mid]
    z.free["wupselsp"] = [ev_up_last]
    z.free["sc2"] = [ev_add[0]]
    z.free["sc3"] = [ev_add[1]]
    return ev_last, [ev_add[0], ev_add[1]], tile_adds


def plan_w23(P, z):
    ev_w2 = P.op("sync", lambda e: e.dma_start(out=z.wtrans[:, 0:128], in_=z.d_w2),
                 waits=[F(z, "wtrans")], sem="we0")
    ev_w3 = P.op("sync", lambda e: e.dma_start(out=z.wtrans[:, 128:256], in_=z.d_w3),
                 sem="we0")
    ev_w23h = P.op("vector", lambda e: e.tensor_copy(
        z.w23sp[:, 0:256], z.wtrans[:, 0:256]),
        waits=[ev_w2, ev_w3, F(z, "w23sp")], sem="v")
    ev_w23l = P.op("vector", lambda e: e.tensor_tensor(
        z.w23sp[:, 256:512], z.wtrans[:, 0:256], z.w23sp[:, 0:256], op=AO.subtract),
        waits=[ev_w23h], sem="v")
    z.free["wtrans"] = [ev_w23l]
    return ev_w23l


def plan_mixer(P, z, ev_xn, pre=None):
    ttot, nt = z.ttot, z.nt
    TT = ttot // nt
    ev_sp = [None, None]
    sp_guard = [F(z, "wencsp0"), F(z, "wencsp1")]
    ev_mm_free = [None, None]
    quarters = [(i, q) for i in range(nt) for q in range(4)]

    def load(k):
        i, q = quarters[k]
        b = k % 2
        ev_sp[b] = plan_wstream(
            P, z, z.d_w1[:, q * 2048:(q + 1) * 2048], z.wencsp[b],
            [sp_guard[b], ev_mm_free[b]])
        sp_guard[b] = None

    ev_w2 = P.op("sync", lambda e: e.dma_start(out=z.wtrans[:, 0:128], in_=z.d_w2),
                 waits=[F(z, "wtrans")], sem="we0")
    ev_w3 = P.op("sync", lambda e: e.dma_start(out=z.wtrans[:, 128:256], in_=z.d_w3),
                 sem="we0")
    ev_w23h = P.op("vector", lambda e: e.tensor_copy(
        z.w23sp[:, 0:256], z.wtrans[:, 0:256]),
        waits=[ev_w2, ev_w3, F(z, "w23sp")], sem="v")
    ev_w23l = P.op("vector", lambda e: e.tensor_tensor(
        z.w23sp[:, 256:512], z.wtrans[:, 0:256], z.w23sp[:, 0:256], op=AO.subtract),
        waits=[ev_w23h], sem="v")
    z.free["wtrans"] = [ev_w23l]

    load(0)
    load(1)

    ev_BCe = {0: F(z, "p4"), 1: F(z, "p5")}
    ev_sbc_used = F(z, "p6")
    ev_sbc = None
    ev_sbc_prev = None
    ev_evac = {0: F(z, "p2"), 1: F(z, "p3")}
    ev_u0_guard = {0: F(z, "sc0"), 1: F(z, "sc1")}
    ev_dd_guard = {0: F(z, "sc2"), 1: F(z, "sc3")}
    sc4_guard = F(z, "sc4")
    sc5_guard = F(z, "sc5")
    ev_BCp_prev = None
    ev_Bm = ev_Cm = None
    ev_last = None
    tile_adds = []
    for k, (i, q) in enumerate(quarters):
        b = k % 2
        t0 = i * TT
        if q == 0:
            def emit_B(e, t0=t0, TT=TT):
                return _mm3(
                    e, z.p[4][0:16, 0:TT], z.w23sp, 0, 256,
                    lambda c: (c * 16, (c + 1) * 16),
                    z.xnh, z.xnl,
                    lambda c, t0=t0: (c * ttot + t0, c * ttot + t0 + TT),
                    first=True)

            def emit_Cm(e, t0=t0, TT=TT):
                return _mm3(
                    e, z.p[5][0:16, 0:TT], z.w23sp, 128, 384,
                    lambda c: (c * 16, (c + 1) * 16),
                    z.xnh, z.xnl,
                    lambda c, t0=t0: (c * ttot + t0, c * ttot + t0 + TT),
                    first=True)
            ev_Bm = P.op("tensor", emit_B, waits=[ev_xn[i], ev_w23l, ev_BCe[0]],
                         sem="t")
            ev_Cm = P.op("tensor", emit_Cm, waits=[ev_w23l, ev_BCe[1]], sem="t")
            ev_Be = P.op("scalar", lambda e, TT=TT: e.activation(
                out=z.sc4[0:16, 0:TT], in_=z.p[4][0:16, 0:TT], func=AF.Identity,
                bias=z.bias[0:16, B2_0:B2_0 + 1]),
                waits=[ev_Bm, z.ev_bias, ev_sbc_prev, sc4_guard], sem="s")
            ev_Ce = P.op("scalar", lambda e, TT=TT: e.activation(
                out=z.sc5[0:16, 0:TT], in_=z.p[5][0:16, 0:TT], func=AF.Identity,
                bias=z.bias[0:16, B3_0:B3_0 + 1]),
                waits=[ev_Cm, z.ev_bias, ev_BCp_prev, sc5_guard], sem="s")
            ev_BCe[0], ev_BCe[1] = [ev_Be], [ev_Ce]
            ev_BCp = P.op("vector", lambda e, TT=TT: e.tensor_tensor(
                z.sc4[0:16, 0:TT], z.sc4[0:16, 0:TT], z.sc5[0:16, 0:TT], op=AO.mult),
                waits=[ev_Be, ev_Ce], sem="v")
            ev_BCp_prev = ev_BCp
            ev_sbc = P.op("tensor", lambda e, TT=TT: e.matmul(
                z.p[6][:, 0:TT], z.consts[0:16, 0:128], z.sc4[0:16, 0:TT],
                start=True, stop=True), waits=[ev_BCp, ev_sbc_used], sem="t")
            ev_sbc_prev = ev_sbc
        for eo2 in range(2):
            eo = q * 2 + eo2
            pb = 2 + eo2

            def emit_w1(e, b=b, eo2=eo2, t0=t0, pb=pb, TT=TT):
                return _mm3(
                    e, z.p[pb][:, 0:TT], z.wencsp[b], 0, 2048,
                    lambda c, eo2=eo2: (c * 256 + eo2 * 128, c * 256 + eo2 * 128 + 128),
                    z.xnh, z.xnl,
                    lambda c, t0=t0: (c * ttot + t0, c * ttot + t0 + TT),
                    first=True)
            ev_mm = P.op("tensor", emit_w1,
                         waits=[ev_xn[i], ev_sp[b], ev_evac[eo2]], sem="t")
            if eo2 == 1:
                ev_mm_free[b] = ev_mm
            u0 = (z.sc0, z.sc1)[eo2]
            dd = (z.sc2, z.sc3)[eo2]
            ev_u0 = P.op("scalar", lambda e, eo=eo, pb=pb, u0=u0, TT=TT: e.activation(
                out=u0[:, 0:TT], in_=z.p[pb][:, 0:TT], func=AF.Identity,
                bias=z.bias[:, B1_0 + eo:B1_0 + eo + 1]),
                waits=[ev_mm, z.ev_bias, ev_u0_guard[eo2]], sem="s")
            ev_evac[eo2] = [ev_u0]
            ev_uc = P.op("vector", lambda e, u0=u0, dd=dd, TT=TT: e.tensor_scalar(
                dd[:, 0:TT], u0[:, 0:TT], z.consts[:, C30:C30 + 1], None, op0=AO.min),
                waits=[ev_u0, ev_dd_guard[eo2]], sem="v")
            ev_ex = P.op("scalar", lambda e, dd=dd, TT=TT: e.activation(
                out=dd[:, 0:TT], in_=dd[:, 0:TT], func=AF.Exp), waits=[ev_uc], sem="s")
            ev_ln = P.op("scalar", lambda e, dd=dd, TT=TT: e.activation(
                out=dd[:, 0:TT], in_=dd[:, 0:TT], func=AF.Ln,
                bias=z.consts[:, 0:1]), waits=[ev_ex], sem="s")
            ev_dl = P.op("vector", lambda e, u0=u0, dd=dd, TT=TT: e.tensor_tensor(
                dd[:, 0:TT], u0[:, 0:TT], dd[:, 0:TT], op=AO.max),
                waits=[ev_ln], sem="v")
            off = eo * ttot + t0
            ev_xnf = P.op("vector", lambda e, off=off, u0=u0, TT=TT: e.tensor_tensor(
                u0[:, 0:TT], z.xnh[:, off:off + TT], z.xnl[:, off:off + TT],
                op=AO.add), waits=[ev_dl], sem="v")
            ev_p1 = P.op("vector", lambda e, u0=u0, dd=dd, TT=TT: e.tensor_tensor(
                dd[:, 0:TT], u0[:, 0:TT], dd[:, 0:TT], op=AO.mult),
                waits=[ev_xnf], sem="v")
            ev_u0_guard[eo2] = [ev_p1]
            ev_p2 = P.op("vector", lambda e, dd=dd, TT=TT: e.tensor_tensor(
                dd[:, 0:TT], dd[:, 0:TT], z.p[6][:, 0:TT], op=AO.mult),
                waits=[ev_p1, ev_sbc], sem="v")
            ev_add = P.op("vector", lambda e, off=off, dd=dd, TT=TT: e.tensor_tensor(
                z.x[:, off:off + TT], z.x[:, off:off + TT], dd[:, 0:TT], op=AO.add),
                waits=[ev_p2], sem="v")
            ev_dd_guard[eo2] = [ev_add]
            ev_last = ev_add
            if eo == 7:
                ev_sbc_used = [ev_p2]
        if q == 3:
            tile_adds.append([ev_dd_guard[0], ev_dd_guard[1]])
        if k + 2 < len(quarters):
            load(k + 2)
    z.free["p2"], z.free["p3"] = ev_evac[0], ev_evac[1]
    z.free["p4"], z.free["p5"] = ev_BCe[0], ev_BCe[1]
    z.free["p6"] = ev_sbc_used
    z.free["wencsp0"] = [ev_mm_free[0]]
    z.free["wencsp1"] = [ev_mm_free[1]]
    z.free["w23sp"] = [ev_Bm, ev_Cm]
    z.free["sc0"] = ev_u0_guard[0]
    z.free["sc1"] = ev_u0_guard[1]
    z.free["sc2"] = ev_dd_guard[0]
    z.free["sc3"] = ev_dd_guard[1]
    z.free["sc4"] = [ev_sbc]
    z.free["sc5"] = [ev_BCp_prev]
    return ev_last, tile_adds



def build_core_program(ttot=2048, nt=4, dbg=False):
    TT = ttot // nt
    assert TT <= 512
    nc = bass.Bass("TRN2", target_bir_lowering=False)
    z = Ctx()
    z.ttot, z.nt = ttot, nt
    z.free = {}
    z.d_x = nc.dram_tensor("x", [128, C * ttot], F32, kind="ExternalInput").ap()
    z.d_wenc = nc.dram_tensor("wenc", [NE, 128, 2048], F32, kind="ExternalInput").ap()
    z.d_wdec = nc.dram_tensor("wdec", [NE, 128, 2048], F32, kind="ExternalInput").ap()
    z.d_wbase = nc.dram_tensor("wbase", [128, 8192], F32, kind="ExternalInput").ap()
    z.d_w1 = nc.dram_tensor("w1", [128, 8192], F32, kind="ExternalInput").ap()
    z.d_w2 = nc.dram_tensor("w2", [128, 128], F32, kind="ExternalInput").ap()
    z.d_w3 = nc.dram_tensor("w3", [128, 128], F32, kind="ExternalInput").ap()
    z.d_wdown = nc.dram_tensor("wdown", [NE, 128, 512], F32, kind="ExternalInput").ap()
    z.d_wup = nc.dram_tensor("wup", [NE, 64, 1024], F32, kind="ExternalInput").ap()
    z.d_biases = nc.dram_tensor("biases", [128, NBIAS], F32, kind="ExternalInput").ap()
    z.d_consts = nc.dram_tensor("consts", [128, NCONST], F32, kind="ExternalInput").ap()
    z.d_ci = nc.dram_tensor("ci", [1, 8], I32, kind="ExternalInput").ap()
    z.d_y = nc.dram_tensor("y", [128, C * ttot], F32, kind="ExternalOutput").ap()
    z.d_arin = [nc.dram_tensor(f"arin{k}", [1, 8], F32).ap() for k in range(2)]
    z.d_arout = [nc.dram_tensor(f"arout{k}", [1, 8], F32).ap() for k in range(2)]
    if dbg:
        z.d_dbg_xn = nc.dram_tensor("dbg_xn", [128, C * ttot], BF16,
                                    kind="ExternalOutput").ap()
        z.d_dbg_xnl = nc.dram_tensor("dbg_xnl", [128, C * ttot], BF16,
                                     kind="ExternalOutput").ap()
        z.d_dbg_row = nc.dram_tensor("dbg_row", [1, 64], F32, kind="ExternalOutput").ap()
        z.d_dbg_lacc = nc.dram_tensor("dbg_lacc", [128, 8], F32,
                                      kind="ExternalOutput").ap()
        z.d_dbg_sel = nc.dram_tensor("dbg_sel", [128, 512], BF16,
                                     kind="ExternalOutput").ap()
        z.d_dbg_x1 = nc.dram_tensor("dbg_x1", [128, C * ttot], F32,
                                    kind="ExternalOutput").ap()
        z.d_dbg_x2 = nc.dram_tensor("dbg_x2", [128, C * ttot], F32,
                                    kind="ExternalOutput").ap()

    es = contextlib.ExitStack()
    sb = lambda name, shape, dt=F32: es.enter_context(nc.sbuf_tensor(name, shape, dt))
    with es:
        z.consts = sb("consts_sb", [128, NCONST])
        z.bias = sb("bias_sb", [128, NBIAS])
        z.halfbase = sb("halfbase", [128, 8])
        z.ci = sb("ci_sb", [1, 8], I32)
        z.cif = sb("cif", [1, 8])
        z.x = sb("x_sb", [128, C * ttot])
        z.xnh = sb("xnh", [128, C * ttot], BF16)
        z.xnl = sb("xnl", [128, C * ttot], BF16)
        z.wtrans = sb("wtrans", [128, 2048])
        z.wencsp = [sb("wencsp0", [128, 4096], BF16), sb("wencsp1", [128, 4096], BF16)]
        z.wdecsp = sb("wdecsp", [128, 4096], BF16)
        z.hsp0 = sb("hsp0", [128, 4 * TT], BF16)
        z.hsp1 = sb("hsp1", [128, 4 * TT], BF16)
        z.scrM = sb("scrM", [128, TT])
        z.scrR = sb("scrR", [128, TT])
        z.sc0 = sb("sc0", [128, 512])
        z.sc1 = sb("sc1", [128, 512])
        z.sc2 = sb("sc2", [128, 512])
        z.sc3 = sb("sc3", [128, 512])
        z.sc4 = sb("sc4", [128, 512])
        z.sc5 = sb("sc5", [128, 512])
        z.slots = sb("slots", [128, NE * nt * C])
        z.lacc = sb("lacc", [128, 8])
        z.rowbuf = sb("rowbuf", [1, 64])
        z.arst = sb("arst", [1, 8])
        z.lrow = sb("lrow", [1, 8])
        z.mskb = sb("mskb", [128, 8])
        z.selbuf = sb("selbuf", [128, 1536])
        z.wupsel = sb("wupsel", [64, 1024])
        z.wdselh = sb("wdselh", [128, 512], BF16)
        z.wdsell = sb("wdsell", [128, 512], BF16)
        z.wupselh = sb("wupselh", [64, 1024], BF16)
        z.wupsell = sb("wupsell", [64, 1024], BF16)
        z.msph = sb("msph", [64, 512], BF16)
        z.mspl = sb("mspl", [64, 512], BF16)
        z.bsel = sb("bsel", [128, 35])
        z.w23sp = sb("w23sp", [128, 512], BF16)
        z.p = [es.enter_context(nc.psum_tensor(f"pb{k}", [128, 512], F32))
               for k in range(8)]

        sems = {name: es.enter_context(nc.semaphore(name))
                for name in ("we0", "ws", "ws2", "g", "t", "s", "v", "q", "c")}
        P = Pipe(sems)
        P.chain["g"] = None

        plan_prelude(P, z)
        gate = []
        base_tiles = ad_tiles = None
        base_adds = ad_adds = []
        for call in range(2):
            exp_pre = (
                plan_wstream(P, z, z.d_wenc[0], z.wencsp[0], [F(z, "wencsp0")]),
                plan_wstream(P, z, z.d_wdec[0], z.wdecsp, [F(z, "wdecsp")]),
                plan_wstream(P, z, z.d_wenc[1], z.wencsp[1], [F(z, "wencsp1")]),
            )
            ev_xn = plan_stats_xn(P, z, gate)
            pre_box = {}

            def tail_hook(g0, g1, pre_box=pre_box):
                pre_box["sp0"] = plan_wstream(P, z, z.d_wbase[:, 0:2048],
                                              z.wencsp[0], [g0])
                pre_box["sp1"] = plan_wstream(P, z, z.d_wbase[:, 2048:4096],
                                              z.wencsp[1], [g1])
            ev_sq_pair = plan_experts(P, z, ev_xn, tail_hook, pre=exp_pre)
            ev_lrow = plan_loss_ar_issue(P, z, call, ev_sq_pair)
            plan_mask(P, z, ev_lrow)
            ev_base, base_adds, base_tiles = plan_base(
                P, z, ev_xn, pre=(pre_box["sp0"], pre_box["sp1"]))
            ev_mskb = plan_mask_bc(P, z)
            ev_sel = plan_select(P, z, ev_mskb)
            ev_ad, ad_adds, ad_tiles = plan_adapter(P, z, ev_xn, ev_sel)
            z.free["mskb"] = [ev_sel[3]]
            dbg_evs = []
            if dbg and call == 0:
                dbg_evs.append(P.op(
                    "gpsimd", lambda e: e.dma_start(out=z.d_dbg_xn, in_=z.xnh[:]),
                    waits=[ev_sq_pair], sem="g"))
                dbg_evs.append(P.op(
                    "gpsimd", lambda e: e.dma_start(out=z.d_dbg_xnl, in_=z.xnl[:]),
                    waits=[ev_sq_pair], sem="g"))
                dbg_evs.append(P.op(
                    "gpsimd", lambda e: e.dma_start(out=z.d_dbg_lacc, in_=z.lacc[:]),
                    waits=[z.free["lacc"]], sem="g"))
                dbg_evs.append(P.op(
                    "gpsimd", lambda e: e.dma_start(
                        out=z.d_dbg_row, in_=z.rowbuf[0:1, 0:64]),
                    waits=[ev_mskb], sem="g"))
                dbg_evs.append(P.op(
                    "gpsimd", lambda e: e.dma_start(out=z.d_dbg_sel, in_=z.wdselh[:]),
                    waits=[ev_sel[0]], sem="g"))
                dbg_evs.append(P.op(
                    "gpsimd", lambda e: e.dma_start(out=z.d_dbg_x1, in_=z.x[:]),
                    waits=[base_adds, ad_adds], sem="g"))
            if call == 0:
                mix_pre = (
                    plan_wstream(P, z, z.d_w1[:, 0:2048], z.wencsp[0],
                                 [F(z, "wencsp0")]),
                    plan_wstream(P, z, z.d_w1[:, 2048:4096], z.wencsp[1],
                                 [F(z, "wencsp1")]),
                    plan_w23(P, z),
                )
                gates = [base_tiles[i] + ad_tiles[i] + dbg_evs for i in range(nt)]
                ev_mix, mix_tiles = plan_mixer(P, z, plan_stats_xn(P, z, gates),
                                               pre=mix_pre)
                gate = [mix_tiles[i] for i in range(nt)]
                if dbg:
                    ev_dx2 = P.op(
                        "gpsimd", lambda e: e.dma_start(out=z.d_dbg_x2, in_=z.x[:]),
                        waits=[ev_mix], sem="g")
                    gate = [g + [ev_dx2] for g in gate]
        yd3 = z.d_y.rearrange("p (c t) -> p c t", c=C)
        ys3 = z.x[:].rearrange("p (c t) -> p c t", c=C)
        ev_out = None
        for i in range(nt):
            ev_out = P.op(
                "gpsimd", lambda e, i=i: e.dma_start(
                    out=yd3[:, :, i * TT:(i + 1) * TT],
                    in_=ys3[:, :, i * TT:(i + 1) * TT]),
                waits=[base_tiles[i], ad_tiles[i]], sem="g")
        P.wait_only("gpsimd", [ev_out])
        with nc.Block() as block:
            P.emit_into(block)
    return nc, z


_BUILD_CACHE = {}


def _get_program(ttot=2048, nt=4, dbg=False):
    key = (ttot, nt, dbg)
    if key not in _BUILD_CACHE:
        _BUILD_CACHE[key] = build_core_program(ttot, nt, dbg)
    return _BUILD_CACHE[key]



def _fm(a):
    Dd, T = a.shape
    return np.ascontiguousarray(
        a.reshape(Dd // 128, 128, T).transpose(1, 0, 2).reshape(128, (Dd // 128) * T))


def make_in_maps(inputs, ttot=2048):
    f32 = lambda v: np.asarray(v, dtype=np.float32)
    x = f32(inputs["L0_attn_x"]).reshape(-1, D)
    W1 = f32(inputs["L0_attn_W1"])
    b1 = f32(inputs["L0_attn_b1"])
    W2 = f32(inputs["L0_attn_W2"])
    b2 = f32(inputs["L0_attn_b2"])
    W3 = f32(inputs["L0_attn_W3"])
    b3 = f32(inputs["L0_attn_b3"])
    Wb = f32(inputs["L0_ffn_W_base"])
    bb = f32(inputs["L0_ffn_b_base"])
    We = f32(inputs["L0_ffn_W_enc"])
    be = f32(inputs["L0_ffn_b_enc"])
    Wd = f32(inputs["L0_ffn_W_dec"])
    bd = f32(inputs["L0_ffn_b_dec"])
    Wdn = f32(inputs["L0_ffn_W_down"])
    bdn = f32(inputs["L0_ffn_b_down"])
    Wup = f32(inputs["L0_ffn_W_up"])
    bup = f32(inputs["L0_ffn_b_up"])
    ci = np.asarray(inputs["L0_ffn_connected_idx"], dtype=np.int32).reshape(1, 8)

    wenc = np.stack([_fm(We[n]) for n in range(NE)])
    wdec = np.stack([_fm(Wd[n]) for n in range(NE)])

    def _quarter_layout(W):
        a = W.reshape(C, 128, 2, 2, 2, 128)
        a = a.transpose(1, 2, 3, 0, 4, 5)
        return np.ascontiguousarray(a.reshape(128, 8192))
    wbase = _quarter_layout(Wb)
    w1 = _quarter_layout(W1)
    w2 = _fm(W2)
    w3 = _fm(W3)
    wdown = np.stack([_fm(Wdn[n]) for n in range(NE)])
    wupl = np.ascontiguousarray(Wup)

    biases = np.zeros((128, NBIAS), np.float32)
    biases[:, B1_0:B1_0 + 8] = b1.reshape(C, 128).T
    biases[:, BBASE0:BBASE0 + 8] = bb.reshape(C, 128).T
    biases[0:16, B2_0] = b2
    biases[0:16, B3_0] = b3
    for n in range(NE):
        biases[:, BENC0 + n * 2:BENC0 + n * 2 + 2] = be[n].reshape(2, 128).T
        biases[:, BDEC0 + n * 8:BDEC0 + n * 8 + 8] = bd[n].reshape(C, 128).T
        biases[0:64, BDOWN0 + n] = bdn[n]
        biases[:, BUP0 + n * 8:BUP0 + n * 8 + 8] = bup[n].reshape(C, 128).T

    consts = np.zeros((128, NCONST), np.float32)
    consts[:, 0:128] = 1.0
    consts[:, CEPS] = 1e-5
    consts[:, CINV] = 1.0 / 1024.0
    consts[:, C30] = 30.0
    consts[:, CHALF] = 0.5
    consts[:, CZERO] = 0.0
    consts[:, CNEG1] = -1.0
    for j in range(8):
        consts[:, CJ0 + j] = float(j)
    consts[:, CM05] = -0.5

    shared = dict(wenc=wenc, wdec=wdec, wbase=wbase, w1=w1, w2=w2, w3=w3,
                  wdown=wdown, wup=wupl, biases=biases, consts=consts, ci=ci)
    in_maps = []
    for core in range(NCORES):
        xc = x[core * ttot:(core + 1) * ttot].T
        m = dict(shared)
        m["x"] = _fm(xc)
        in_maps.append(m)
    return in_maps


def assemble_out(results, ttot=2048):
    parts = []
    for r in results:
        y = np.asarray(r["y"])
        a = y.reshape(128, C, ttot).transpose(1, 0, 2).reshape(D, ttot).T
        parts.append(a)
    out = np.concatenate(parts, axis=0)
    return np.ascontiguousarray(out.reshape(4, 4096, D), dtype=np.float32)


def kernel(**inputs):
    ttot = 2048
    nc, _ = _get_program(ttot=ttot, nt=4, dbg=False)
    in_maps = make_in_maps(inputs, ttot=ttot)
    res = run_bass_kernel_spmd(nc, in_maps, core_ids=list(range(NCORES)))
    return assemble_out(res.results, ttot=ttot)


# version 16
# speedup vs baseline: 1.0612x; 1.0612x over previous
import contextlib
import numpy as np
import concourse.bass as bass
import concourse.mybir as mybir
from concourse.bass_utils import run_bass_kernel_spmd
from concourse.alu_op_type import AluOpType as AO

F32 = mybir.dt.float32
BF16 = mybir.dt.bfloat16
I32 = mybir.dt.int32
AF = mybir.ActivationFunctionType
AX = mybir.AxisListType

D = 1024
C = 8
NE = 8
HC = 2
NCORES = 8

CEPS, CINV, C30, CHALF, CZERO, CNEG1, CJ0, CM05 = 128, 129, 130, 131, 132, 133, 134, 142
NCONST = 144

B1_0, BBASE0, B2_0, B3_0, BENC0, BDEC0, BDOWN0, BUP0 = 0, 8, 16, 17, 18, 34, 98, 106
NBIAS = 176


class Pipe:
    ENGS = ("sync", "gpsimd", "tensor", "scalar", "vector")
    COMPUTE = ("t", "s", "v", "q")

    def __init__(self, sems):
        self.sems = sems
        self.ops = []
        self.cnt = {k: 0 for k in sems}
        self.chain = {}

    def is_dma(self, sem):
        return sem is not None and sem not in self.COMPUTE and sem != "c"

    def _flat(self, waits):
        for w in waits:
            if w is None:
                continue
            if isinstance(w, list):
                yield from self._flat(w)
            else:
                yield w

    def op(self, eng, emit, waits=(), sem=None, n=1):
        waits = list(waits)
        if sem in self.chain and self.chain[sem] is not None:
            waits.append(self.chain[sem])
        wmax = {}
        for (s, v, peng) in self._flat(waits):
            if v <= 0:
                continue
            assert v <= self.cnt[s], f"wait on future event {s}={v} (cnt={self.cnt[s]})"
            if v > wmax.get(s, 0):
                wmax[s] = v
        ws = list(wmax.items())
        post = None
        if sem is not None:
            self.cnt[sem] += 16 * n if self.is_dma(sem) else n
            post = (sem, self.cnt[sem], eng)
            if sem in self.chain:
                self.chain[sem] = post
        self.ops.append((eng, tuple(ws), emit, sem, n))
        return post

    def wait_only(self, eng, waits):
        self.op(eng, None, waits=waits)

    def emit_into(self, block):
        for eng_name in self.ENGS:
            def mk(eng_name=eng_name):
                def f(eng):
                    lw = {k: 0 for k in self.sems}
                    for (e, ws, emit, sem, n) in self.ops:
                        if e != eng_name:
                            continue
                        for (s, v) in ws:
                            if v > lw[s]:
                                eng.wait_ge(self.sems[s], v)
                                lw[s] = v
                        if emit is None:
                            continue
                        last = emit(eng)
                        if sem is not None:
                            last.then_inc(self.sems[sem],
                                          16 * n if self.is_dma(sem) else n)
                return f
            getattr(block, eng_name)(mk())


class Ctx:
    pass


def F(z, key):
    return z.free.get(key, [])



def plan_prelude(P, z):
    e_con = P.op("gpsimd", lambda e: e.dma_start(out=z.consts[:], in_=z.d_consts), sem="g")
    e_bias = P.op("gpsimd", lambda e: e.dma_start(out=z.bias[:], in_=z.d_biases), sem="g")
    e_ci = P.op("gpsimd", lambda e: e.dma_start(out=z.ci[:], in_=z.d_ci), sem="g")
    nt, ttot = z.nt, z.ttot
    TT = ttot // nt
    xd3 = z.d_x.rearrange("p (c t) -> p c t", c=C)
    xs3 = z.x[:].rearrange("p (c t) -> p c t", c=C)
    z.ev_xtile = []
    for i in range(nt):
        z.ev_xtile.append(P.op(
            "gpsimd", lambda e, i=i: e.dma_start(
                out=xs3[:, :, i * TT:(i + 1) * TT], in_=xd3[:, :, i * TT:(i + 1) * TT]),
            sem="g"))
    z.ev_x = z.ev_xtile[-1]
    z.ev_con = e_con
    z.ev_bias = e_bias
    z.ev_cif = P.op("vector", lambda e: e.tensor_copy(z.cif[:], z.ci[:]),
                    waits=[e_ci], sem="v")
    z.ev_halfbase = P.op(
        "vector", lambda e: e.tensor_scalar(
            z.halfbase[:, 0:8], z.bias[:, BBASE0:BBASE0 + 8],
            z.consts[:, CHALF:CHALF + 1], None, op0=AO.mult),
        waits=[e_bias, e_con], sem="v")
    z.ev_rb0 = P.op(
        "vector", lambda e: e.tensor_scalar(
            z.rowbuf[0:1, 0:64], z.consts[0:1, 0:64],
            z.consts[0:1, CZERO:CZERO + 1], None, op0=AO.mult),
        waits=[e_con], sem="v")


def plan_stats_xn(P, z, gate):
    ttot, nt = z.ttot, z.nt
    TT = ttot // nt
    x = z.x
    free_p0 = F(z, "p0")
    free_p1 = F(z, "p1")
    ev_tile = []
    e6 = None
    ev_Q_prev = None
    ev_qmm = {}
    sc3_guard = F(z, "sc3")
    per_tile = bool(gate) and isinstance(gate[0], list) and len(gate) == nt
    gates = gate if per_tile else [gate] * nt
    for i in range(nt):
        gate = gates[i]
        t0 = i * TT

        def emit_S(e, t0=t0, TT=TT):
            last = None
            for c in range(C):
                last = e.matmul(z.p[0][:, 0:TT], z.consts[:, 0:128],
                                x[:, c * ttot + t0: c * ttot + t0 + TT],
                                start=(c == 0), stop=(c == C - 1))
            return last
        ev_S = P.op("tensor", emit_S,
                    waits=[gate, z.ev_xtile[i], z.ev_con, free_p0], sem="t")

        ev_sq = [None, None]
        for c in range(C):
            scr = (z.sc0, z.sc1)[c % 2]
            if c >= 2:
                w_extra = [ev_qmm.get(c % 2)]
            else:
                w_extra = [ev_qmm.get(c % 2), F(z, ("sc0", "sc1")[c % 2]), free_p1]
            ev_sq[c % 2] = P.op(
                "scalar",
                lambda e, c=c, scr=scr, t0=t0, TT=TT: e.activation(
                    out=scr[:, 0:TT], in_=x[:, c * ttot + t0: c * ttot + t0 + TT],
                    func=AF.Square),
                waits=[gate, z.ev_xtile[i]] + w_extra, sem="s")
            ev_qmm[c % 2] = P.op(
                "tensor",
                lambda e, c=c, scr=scr, TT=TT: e.matmul(
                    z.p[1][:, 0:TT], z.consts[:, 0:128], scr[:, 0:TT],
                    start=(c == 0), stop=(c == C - 1)),
                waits=[ev_sq[c % 2]], sem="t")
        ev_Q = ev_qmm[(C - 1) % 2]
        ev_Q_prev = ev_Q

        e1 = P.op("vector", lambda e, TT=TT: e.tensor_scalar(
            z.scrM[:, 0:TT], z.p[0][:, 0:TT], z.consts[:, CINV:CINV + 1], None,
            op0=AO.mult), waits=[ev_S, ev_tile[-1] if ev_tile else None], sem="v")
        e2 = P.op("vector", lambda e, TT=TT: e.tensor_scalar(
            z.sc2[:, 0:TT], z.p[1][:, 0:TT], z.consts[:, CINV:CINV + 1], None,
            op0=AO.mult), waits=[ev_Q, F(z, "sc2")], sem="v")
        free_p0, free_p1 = [e1], [e2]
        e3 = P.op("vector", lambda e, TT=TT: e.tensor_tensor(
            z.scrR[:, 0:TT], z.scrM[:, 0:TT], z.scrM[:, 0:TT], op=AO.mult),
            waits=[e1, e6], sem="v")
        e4 = P.op("vector", lambda e, TT=TT: e.tensor_tensor(
            z.sc2[:, 0:TT], z.sc2[:, 0:TT], z.scrR[:, 0:TT], op=AO.subtract),
            waits=[e3], sem="v")
        e5 = P.op("scalar", lambda e, TT=TT: e.activation(
            out=z.sc3[:, 0:TT], in_=z.sc2[:, 0:TT], func=AF.Ln,
            bias=z.consts[:, CEPS:CEPS + 1]),
            waits=[e4, sc3_guard, ev_tile[-1] if ev_tile else None], sem="s")
        e6 = P.op("scalar", lambda e, TT=TT: e.activation(
            out=z.scrR[:, 0:TT], in_=z.sc3[:, 0:TT], func=AF.Exp,
            scale=z.consts[:, CM05:CM05 + 1]), waits=[e5], sem="s")

        elast = None
        for c in range(C):
            off = c * ttot + t0
            e7 = P.op("vector", lambda e, off=off, TT=TT: e.tensor_tensor(
                z.sc3[:, 0:TT], x[:, off: off + TT], z.scrM[:, 0:TT],
                op=AO.subtract), waits=[e1, elast, e6], sem="v")
            e8 = P.op("vector", lambda e, TT=TT: e.tensor_tensor(
                z.sc3[:, 0:TT], z.sc3[:, 0:TT], z.scrR[:, 0:TT],
                op=AO.mult), waits=[e6, e7], sem="v")
            e9 = P.op("vector", lambda e, off=off, TT=TT: e.tensor_copy(
                z.xnh[:, off: off + TT], z.sc3[:, 0:TT]), waits=[e8], sem="v")
            elast = P.op("vector", lambda e, off=off, TT=TT: e.tensor_tensor(
                z.xnl[:, off: off + TT], z.sc3[:, 0:TT], z.xnh[:, off: off + TT],
                op=AO.subtract), waits=[e9], sem="v")
        ev_tile.append(elast)
    z.free["p0"] = free_p0
    z.free["p1"] = free_p1
    z.free["sc0"] = [ev_qmm[0]]
    z.free["sc1"] = [ev_qmm[1]]
    z.free["sc2"] = [e6]
    z.free["sc3"] = [ev_tile[-1]]
    return ev_tile


def _mm3(e, psum_ap, wsp, whoff, wloff, wslice, xh, xl, xslices, first):
    last = None
    passes = [(whoff, xh), (whoff, xl), (wloff, xh)]
    for pi, (woff, xs) in enumerate(passes):
        for c in range(C):
            w0, w1 = wslice(c)
            x0, x1 = xslices(c)
            last = e.matmul(psum_ap, wsp[:, woff + w0: woff + w1],
                            xs[:, x0:x1],
                            start=(pi == 0 and c == 0 and first),
                            stop=(pi == 2 and c == C - 1))
    return last


def plan_wstream(P, z, dram_src, target, tgt_guard):
    ev_dma = P.op("sync", lambda e: e.dma_start(out=z.wtrans[:], in_=dram_src),
                  waits=[F(z, "wtrans")], sem="we0")
    ev_hi = P.op("vector", lambda e: e.tensor_copy(target[:, 0:2048], z.wtrans[:]),
                 waits=[ev_dma, tgt_guard], sem="v")
    ev_lo = P.op("vector", lambda e: e.tensor_tensor(
        target[:, 2048:4096], z.wtrans[:], target[:, 0:2048], op=AO.subtract),
        waits=[ev_hi], sem="v")
    z.free["wtrans"] = [ev_lo]
    return ev_lo


def plan_experts(P, z, ev_xn, tail_hook=None, pre=None):
    ttot, nt = z.ttot, z.nt
    TT = ttot // nt
    ev_enc_sp = [None, None]
    ev_dec_sp = None
    enc_guard = [F(z, "wencsp0"), F(z, "wencsp1")]
    dec_guard = F(z, "wdecsp")
    ev_enc_mm_last = [None, None]
    ev_dec_mm_last = None

    def load_enc(n):
        b = n % 2
        ev_enc_sp[b] = plan_wstream(P, z, z.d_wenc[n], z.wencsp[b],
                                    [enc_guard[b], ev_enc_mm_last[b]])

    def load_dec(n):
        nonlocal ev_dec_sp
        ev_dec_sp = plan_wstream(P, z, z.d_wdec[n], z.wdecsp,
                                 [dec_guard, ev_dec_mm_last])

    if pre is not None:
        ev_enc_sp[0], ev_dec_sp, ev_enc_sp[1] = pre
    else:
        load_enc(0)
        load_dec(0)
        load_enc(1)

    hp_free = {(0, 0): F(z, "p2"), (0, 1): F(z, "p3"),
               (1, 0): F(z, "p6"), (1, 1): F(z, "p7")}
    rbank = (z.p[4], z.p[5], z.p[0], z.p[1])
    rc_free = {0: F(z, "p4"), 1: F(z, "p5"), 2: F(z, "p0"), 3: F(z, "p1")}
    h32_guard = {0: F(z, "sc4"), 1: F(z, "sc5")}
    hsp_guard = {0: F(z, "hsp0"), 1: F(z, "hsp1")}
    ev_d = {0: F(z, "sc0"), 1: F(z, "sc1")}
    ev_sq = {0: F(z, "sc2"), 1: F(z, "sc3")}
    slots_guard = F(z, "slots")
    ev_hsplit = {}

    state = {"last_enc_mm": None, "last_dec_mm": None}

    def enc_tile(n, b, i, my_enc_sp):
        t0 = i * TT
        hspb = (z.hsp0, z.hsp1)[i % 2]
        pbank = (2, 6)[i % 2]
        for hc in range(HC):
            def emit_enc(e, b=b, hc=hc, t0=t0, pbank=pbank, TT=TT):
                return _mm3(
                    e, z.p[pbank + hc][:, 0:TT], z.wencsp[b], 0, 2048,
                    lambda c, hc=hc: (c * 256 + hc * 128, c * 256 + hc * 128 + 128),
                    z.xnh, z.xnl,
                    lambda c, t0=t0: (c * ttot + t0, c * ttot + t0 + TT),
                    first=True)
            ev_mm = P.op("tensor", emit_enc,
                         waits=[ev_xn[i], my_enc_sp, hp_free[(i % 2, hc)]],
                         sem="t")
            state["last_enc_mm"] = ev_mm
            h32 = (z.sc4, z.sc5)[hc]
            ev_relu = P.op(
                "scalar",
                lambda e, n=n, hc=hc, h32=h32, pbank=pbank, TT=TT: e.activation(
                    out=h32[:, 0:TT], in_=z.p[pbank + hc][:, 0:TT],
                    func=AF.Relu,
                    bias=z.bias[:, BENC0 + n * 2 + hc: BENC0 + n * 2 + hc + 1]),
                waits=[ev_mm, z.ev_bias, h32_guard[hc]], sem="s")
            hp_free[(i % 2, hc)] = [ev_relu]
            ev_hi = P.op("vector", lambda e, hc=hc, h32=h32, hspb=hspb, TT=TT:
                         e.tensor_copy(hspb[:, hc * TT: hc * TT + TT],
                                       h32[:, 0:TT]),
                         waits=[ev_relu, hsp_guard[i % 2]], sem="v")
            ev_lo = P.op("vector", lambda e, hc=hc, h32=h32, hspb=hspb, TT=TT:
                         e.tensor_tensor(
                             hspb[:, 2 * TT + hc * TT: 2 * TT + hc * TT + TT],
                             h32[:, 0:TT], hspb[:, hc * TT: hc * TT + TT],
                             op=AO.subtract),
                         waits=[ev_hi], sem="v")
            h32_guard[hc] = [ev_lo]
            ev_hsplit[(i % 2, hc)] = ev_lo

    pend = [None]

    def dec_tile(n, i, my_dec_sp):
        t0 = i * TT
        hspb = (z.hsp0, z.hsp1)[i % 2]
        for c in range(C):
            cb = c % 2
            cb4 = c % 4
            bank = rbank[cb4]
            scrRb = (z.sc0, z.sc1)[cb]
            scrDb = (z.sc2, z.sc3)[cb]

            def emit_dec(e, c=c, bank=bank, hspb=hspb, TT=TT):
                last = None
                first = True
                for (woff, hoff) in ((0, 0), (0, 2 * TT), (2048, 0)):
                    for hc in range(HC):
                        last = e.matmul(
                            bank[:, 0:TT],
                            z.wdecsp[:, woff + hc * 1024 + c * 128:
                                     woff + hc * 1024 + c * 128 + 128],
                            hspb[:, hoff + hc * TT: hoff + hc * TT + TT],
                            start=first, stop=(woff == 2048 and hc == HC - 1))
                        first = False
                return last
            ev_dmm = P.op(
                "tensor", emit_dec,
                waits=[ev_hsplit[(i % 2, 0)], ev_hsplit[(i % 2, 1)], my_dec_sp,
                       rc_free[cb4]], sem="t")
            state["last_dec_mm"] = ev_dmm
            ev_rb = P.op(
                "scalar",
                lambda e, n=n, c=c, bank=bank, scrRb=scrRb, TT=TT: e.activation(
                    out=scrRb[:, 0:TT], in_=bank[:, 0:TT], func=AF.Identity,
                    bias=z.bias[:, BDEC0 + n * 8 + c: BDEC0 + n * 8 + c + 1]),
                waits=[ev_dmm, z.ev_bias, ev_d[cb], ev_sq[cb]], sem="s")
            rc_free[cb4] = [ev_rb]
            ev_d1 = P.op(
                "vector",
                lambda e, c=c, t0=t0, scrRb=scrRb, scrDb=scrDb, TT=TT:
                e.tensor_tensor(
                    scrDb[:, 0:TT], scrRb[:, 0:TT],
                    z.xnh[:, c * ttot + t0: c * ttot + t0 + TT], op=AO.subtract),
                waits=[ev_rb, ev_sq[cb]], sem="v")
            ev_d[cb] = P.op(
                "vector",
                lambda e, c=c, t0=t0, scrDb=scrDb, TT=TT: e.tensor_tensor(
                    scrDb[:, 0:TT], scrDb[:, 0:TT],
                    z.xnl[:, c * ttot + t0: c * ttot + t0 + TT], op=AO.subtract),
                waits=[ev_d1], sem="v")
            col = n * (nt * C) + i * C + c
            if pend[0] is not None:
                (pcb, pcol, pRb, pDb, pdev) = pend[0]
                ev_sq[pcb] = P.op(
                    "scalar",
                    lambda e, pRb=pRb, pDb=pDb, pcol=pcol, TT=TT: e.activation(
                        out=pRb[:, 0:TT], in_=pDb[:, 0:TT], func=AF.Square,
                        accum_out=z.slots[:, pcol:pcol + 1]),
                    waits=[pdev, slots_guard], sem="s")
            pend[0] = (cb, col, scrRb, scrDb, ev_d[cb])
        (pcb, pcol, pRb, pDb, pdev) = pend[0]
        ev_sq[pcb] = P.op(
            "scalar",
            lambda e, pRb=pRb, pDb=pDb, pcol=pcol, TT=TT: e.activation(
                out=pRb[:, 0:TT], in_=pDb[:, 0:TT], func=AF.Square,
                accum_out=z.slots[:, pcol:pcol + 1]),
            waits=[pdev, slots_guard], sem="s")
        pend[0] = None

    for n in range(NE):
        b = n % 2
        my_enc_sp = ev_enc_sp[b]
        my_dec_sp = ev_dec_sp
        for step in range(nt + 1):
            if n == NE - 1 and step == nt and tail_hook is not None:
                tail_hook(ev_enc_mm_last[0], state["last_enc_mm"])
            if step < nt:
                enc_tile(n, b, step, my_enc_sp)
            if step >= 1:
                dec_tile(n, step - 1, my_dec_sp)
        ev_enc_mm_last[b] = state["last_enc_mm"]
        ev_dec_mm_last = state["last_dec_mm"]
        if n + 1 < NE:
            load_dec(n + 1)
        if n + 2 < NE:
            load_enc(n + 2)
    last_dec_mm = state["last_dec_mm"]

    z.free["p2"] = hp_free[(0, 0)]
    z.free["p3"] = hp_free[(0, 1)]
    z.free["p6"] = hp_free[(1, 0)]
    z.free["p7"] = hp_free[(1, 1)]
    z.free["p4"] = rc_free[0]
    z.free["p5"] = rc_free[1]
    z.free["p0"] = rc_free[2]
    z.free["p1"] = rc_free[3]
    z.free["hsp0"] = [last_dec_mm]
    z.free["hsp1"] = [last_dec_mm]
    z.free["sc4"] = h32_guard[0]
    z.free["sc5"] = h32_guard[1]
    z.free["wencsp0"] = [ev_enc_mm_last[0]]
    z.free["wencsp1"] = [ev_enc_mm_last[1]]
    z.free["wdecsp"] = [ev_dec_mm_last]
    z.free["sc0"] = [ev_d[0], ev_sq[0]]
    z.free["sc1"] = [ev_d[1], ev_sq[1]]
    z.free["sc2"] = [ev_d[0], ev_sq[0]]
    z.free["sc3"] = [ev_d[1], ev_sq[1]]
    return [ev_sq[0], ev_sq[1]]


def plan_loss_ar_issue(P, z, call, ev_sq_pair):
    nslot = z.nt * C
    ev_red = None
    for n in range(NE):
        ev_red = P.op("vector", lambda e, n=n, nslot=nslot: e.tensor_reduce(
            z.lacc[:, n:n + 1], z.slots[:, n * nslot:(n + 1) * nslot], axis=AX.X,
            op=AO.add), waits=[ev_sq_pair, F(z, "lacc")], sem="v")
    z.free["slots"] = [ev_red]
    ev_cp = P.op("gpsimd", lambda e: e.tensor_reduce(
        z.arst[0:1, 0:8], z.lacc[:, 0:8], axis=AX.C, op=AO.add),
        waits=[ev_red, F(z, "arst")], sem="q")
    z.free["lacc"] = [ev_cp]
    ev_in = P.op("gpsimd", lambda e, call=call: e.dma_start(
        out=z.d_arin[call], in_=z.arst[0:1, 0:8]), waits=[ev_cp], sem="g")
    z.free["arst"] = [ev_in]
    ev_cc = P.op("gpsimd", lambda e, call=call: e.collective_compute(
        "AllReduce", AO.add,
        replica_groups=[[0, 1], [2, 3], [4, 5], [6, 7]],
        ins=[z.d_arin[call].opt()], outs=[z.d_arout[call].opt()]),
        waits=[ev_in], sem="c")
    ev_out = P.op("gpsimd", lambda e, call=call: e.dma_start(
        out=z.lrow[0:1, 0:8], in_=z.d_arout[call]),
        waits=[ev_cc, F(z, "lrow")], sem="g")
    z.ar_evcp = ev_cp
    return ev_out


def plan_mask(P, z, ev_out):
    ev_cp = z.ar_evcp
    rb = z.rowbuf
    ev_min = P.op("vector", lambda e: e.tensor_reduce(
        rb[0:1, 8:9], z.lrow[0:1, 0:8], axis=AX.X, op=AO.min),
        waits=[ev_out, z.ev_rb0, F(z, "rowbuf")], sem="v")
    ev_eq = P.op("vector", lambda e: e.tensor_scalar(
        rb[0:1, 16:24], z.lrow[0:1, 0:8], rb[0:1, 8:9], None, op0=AO.is_le),
        waits=[ev_min], sem="v")
    z.free["lrow"] = [ev_eq]
    ev_run = P.op("vector", lambda e: e.tensor_copy(
        rb[0:1, 24:25], z.consts[0:1, CZERO:CZERO + 1]), waits=[z.ev_con, ev_eq],
        sem="v")
    ev_nr = None
    for n in range(NE):
        ev_nr = P.op("vector", lambda e, n=n: e.tensor_scalar(
            rb[0:1, 56:57], rb[0:1, 24 + n:25 + n], z.consts[0:1, CNEG1:CNEG1 + 1],
            z.consts[0:1, 0:1], op0=AO.mult, op1=AO.add), waits=[ev_run, ev_eq], sem="v")
        ev_mk = P.op("vector", lambda e, n=n: e.tensor_tensor(
            rb[0:1, 33 + n:34 + n], rb[0:1, 16 + n:17 + n], rb[0:1, 56:57],
            op=AO.mult), waits=[ev_nr], sem="v")
        ev_run = P.op("vector", lambda e, n=n: e.tensor_tensor(
            rb[0:1, 25 + n:26 + n], rb[0:1, 24 + n:25 + n], rb[0:1, 33 + n:34 + n],
            op=AO.add), waits=[ev_mk], sem="v")
    ev_adp = None
    for j in range(NE):
        ev_e = P.op("vector", lambda e, j=j: e.tensor_scalar(
            rb[0:1, 48:56], z.cif[0:1, 0:8], z.consts[0:1, CJ0 + j:CJ0 + j + 1], None,
            op0=AO.is_equal), waits=[z.ev_cif, ev_run, ev_adp, ev_nr], sem="v")
        ev_p = P.op("vector", lambda e: e.tensor_tensor(
            rb[0:1, 48:56], rb[0:1, 48:56], rb[0:1, 33:41], op=AO.mult),
            waits=[ev_e], sem="v")
        ev_adp = P.op("vector", lambda e, j=j: e.tensor_reduce(
            rb[0:1, 41 + j:42 + j], rb[0:1, 48:56], axis=AX.X, op=AO.add),
            waits=[ev_p], sem="v")
    z.free["rowbuf"] = [ev_adp]
    z.ar_evadp = ev_adp
    return ev_adp


def plan_mask_bc(P, z):
    rb = z.rowbuf
    ev_bc = P.op("tensor", lambda e: e.matmul(
        z.p[6][:, 0:8], z.consts[0:1, 0:128], rb[0:1, 41:49], start=True, stop=True),
        waits=[z.ar_evadp, z.ar_evcp, F(z, "p6")], sem="t")
    ev_mskb = P.op("vector", lambda e: e.tensor_copy(z.mskb[:, 0:8], z.p[6][:, 0:8]),
                   waits=[ev_bc, F(z, "mskb")], sem="v")
    z.free["p6"] = [ev_mskb]
    return ev_mskb


def plan_select(P, z, ev_mskb):
    gd = F(z, "adapters")
    ev_wd = gd
    ev_wu = gd
    ev_bd = gd
    ev_bu = gd
    sc4_guard = F(z, "sc4")
    sc5_guard = F(z, "sc5")
    hsp0f = z.hsp0[:].bitcast(F32)
    hsp1f = z.hsp1[:].bitcast(F32)
    bufs = [(z.selbuf[:, 0:512], z.selbuf[0:64, 512:1536]),
            (hsp1f[:, 0:512], hsp0f[0:64, 0:1024])]
    ev_free = [F(z, "selbuf"), F(z, "selodd") + F(z, "hsp0") + F(z, "hsp1")]
    ev_loads = {}

    def load(n):
        b = n % 2
        wd_ap, wu_ap = bufs[b]
        sem = "ws" if b == 0 else "ws2"
        P.op("sync", lambda e, n=n, wd_ap=wd_ap: e.dma_start(
            out=wd_ap, in_=z.d_wdown[n]), waits=[ev_free[b]], sem=sem)
        ev_loads[n] = P.op("sync", lambda e, n=n, wu_ap=wu_ap: e.dma_start(
            out=wu_ap, in_=z.d_wup[n]), sem=sem)

    load(0)
    load(1)
    for n in range(NE):
        sb_wd, sb_wu = bufs[n % 2]
        ev_load = ev_loads[n]
        if n == 0:
            ev_wd = P.op("vector", lambda e, sb_wd=sb_wd: e.tensor_scalar(
                z.sc2[:, 0:512], sb_wd, z.mskb[:, 0:1], None, op0=AO.mult),
                waits=[ev_mskb, ev_load, ev_wd, F(z, "sc2")], sem="v")
            ev_wu = P.op("vector", lambda e, sb_wu=sb_wu: e.tensor_scalar(
                z.wupsel[0:64, :], sb_wu, z.mskb[0:64, 0:1], None,
                op0=AO.mult), waits=[ev_mskb, ev_load, ev_wu], sem="v")
            ev_bd = P.op("vector", lambda e: e.tensor_scalar(
                z.bsel[0:64, 0:1], z.bias[0:64, BDOWN0:BDOWN0 + 1],
                z.mskb[0:64, 0:1], None, op0=AO.mult),
                waits=[ev_mskb, z.ev_bias, ev_bd], sem="v")
            ev_bu = P.op("vector", lambda e: e.tensor_scalar(
                z.bsel[:, 2:10], z.bias[:, BUP0:BUP0 + 8],
                z.mskb[:, 0:1], None, op0=AO.mult),
                waits=[ev_mskb, z.ev_bias, ev_bu], sem="v")
        else:
            ev_s1 = P.op("vector", lambda e, sb_wd=sb_wd, n=n: e.tensor_scalar(
                z.sc4[:, 0:512], sb_wd, z.mskb[:, n:n + 1], None, op0=AO.mult),
                waits=[ev_mskb, ev_load, ev_wd, sc4_guard], sem="v")
            ev_wd = P.op("vector", lambda e: e.tensor_tensor(
                z.sc2[:, 0:512], z.sc2[:, 0:512], z.sc4[:, 0:512], op=AO.add),
                waits=[ev_s1], sem="v")
            for half in range(2):
                ev_s2 = P.op("vector", lambda e, sb_wu=sb_wu, n=n, half=half:
                             e.tensor_scalar(
                                 z.sc5[0:64, 0:512],
                                 sb_wu[0:64, half * 512:512 + half * 512],
                                 z.mskb[0:64, n:n + 1], None, op0=AO.mult),
                    waits=[ev_load, ev_wu, sc5_guard], sem="v")
                ev_wu = P.op("vector", lambda e, half=half: e.tensor_tensor(
                    z.wupsel[0:64, half * 512:512 + half * 512],
                    z.wupsel[0:64, half * 512:512 + half * 512],
                    z.sc5[0:64, 0:512], op=AO.add), waits=[ev_s2], sem="v")
            ev_b1 = P.op("vector", lambda e, n=n: e.tensor_scalar(
                z.bsel[0:64, 34:35], z.bias[0:64, BDOWN0 + n:BDOWN0 + n + 1],
                z.mskb[0:64, n:n + 1], None, op0=AO.mult), waits=[ev_bd], sem="v")
            ev_bd = P.op("vector", lambda e: e.tensor_tensor(
                z.bsel[0:64, 0:1], z.bsel[0:64, 0:1],
                z.bsel[0:64, 34:35], op=AO.add), waits=[ev_b1], sem="v")
            ev_b2 = P.op("vector", lambda e, n=n: e.tensor_scalar(
                z.bsel[:, 26:34], z.bias[:, BUP0 + n * 8:BUP0 + n * 8 + 8],
                z.mskb[:, n:n + 1], None, op0=AO.mult), waits=[ev_bu], sem="v")
            ev_bu = P.op("vector", lambda e: e.tensor_tensor(
                z.bsel[:, 2:10], z.bsel[:, 2:10],
                z.bsel[:, 26:34], op=AO.add), waits=[ev_b2], sem="v")
        ev_free[n % 2] = [ev_wu]
        if n + 2 < NE:
            load(n + 2)
    ev_bush = P.op("vector", lambda e: e.tensor_scalar(
        z.bsel[:, 18:26], z.bsel[:, 2:10],
        z.consts[:, CHALF:CHALF + 1], None, op0=AO.mult), waits=[ev_bu], sem="v")
    ev_wdh = P.op("vector", lambda e: e.tensor_copy(z.wdselh[:], z.sc2[:, 0:512]),
                  waits=[ev_wd, F(z, "wdselsp")], sem="v")
    ev_wdl = P.op("vector", lambda e: e.tensor_tensor(
        z.wdsell[:], z.sc2[:, 0:512], z.wdselh[:], op=AO.subtract),
        waits=[ev_wdh], sem="v")
    ev_wuh = P.op("vector", lambda e: e.tensor_copy(
        z.wupselh[0:64, :], z.wupsel[0:64, :]), waits=[ev_wu, F(z, "wupselsp")], sem="v")
    ev_wul = P.op("vector", lambda e: e.tensor_tensor(
        z.wupsell[0:64, :], z.wupsel[0:64, :], z.wupselh[0:64, :], op=AO.subtract),
        waits=[ev_wuh], sem="v")
    z.free["selbuf"] = ev_free[0]
    z.free["selodd"] = ev_free[1]
    z.free["hsp0"] = F(z, "hsp0") + ev_free[1]
    z.free["hsp1"] = F(z, "hsp1") + ev_free[1]
    z.free["sc2"] = [ev_wdl]
    z.free["sc4"] = [ev_wdh]
    z.free["sc5"] = [ev_wul]
    return ev_wdl, ev_wul, ev_bd, ev_bush


def plan_base(P, z, ev_xn, pre=None):
    ttot, nt = z.ttot, z.nt
    TT = ttot // nt
    ev_sp = [None, None]
    sp_guard = [F(z, "wencsp0"), F(z, "wencsp1")]
    ev_mm_free = [None, None]
    quarters = [(i, q) for i in range(nt) for q in range(4)]

    def load(k):
        i, q = quarters[k]
        b = k % 2
        ev_sp[b] = plan_wstream(
            P, z, z.d_wbase[:, q * 2048:(q + 1) * 2048], z.wencsp[b],
            [sp_guard[b], ev_mm_free[b]])
        sp_guard[b] = None

    if pre is not None:
        ev_sp[0], ev_sp[1] = pre
        sp_guard = [None, None]
    else:
        load(0)
        load(1)
    ev_evac = {0: F(z, "p2"), 1: F(z, "p3"), 2: F(z, "p4"), 3: F(z, "p5")}
    ev_add = {0: F(z, "sc0"), 1: F(z, "sc1")}
    ev_last_add = None
    tile_adds = []
    for k, (i, q) in enumerate(quarters):
        b = k % 2
        t0 = i * TT
        for eo2 in range(2):
            eo = q * 2 + eo2
            pb = 2 + 2 * (q % 2) + eo2

            def emit(e, b=b, eo2=eo2, t0=t0, pb=pb, TT=TT):
                return _mm3(
                    e, z.p[pb][:, 0:TT], z.wencsp[b], 0, 2048,
                    lambda c, eo2=eo2: (c * 256 + eo2 * 128, c * 256 + eo2 * 128 + 128),
                    z.xnh, z.xnl,
                    lambda c, t0=t0: (c * ttot + t0, c * ttot + t0 + TT),
                    first=True)
            ev_mm = P.op("tensor", emit,
                         waits=[ev_xn[i], ev_sp[b], ev_evac[pb - 2]], sem="t")
            if eo2 == 1:
                ev_mm_free[b] = ev_mm
            scr = (z.sc0, z.sc1)[eo % 2]
            ev_evac[pb - 2] = P.op(
                "scalar", lambda e, eo=eo, pb=pb, scr=scr, TT=TT: e.activation(
                    out=scr[:, 0:TT], in_=z.p[pb][:, 0:TT], func=AF.Identity,
                    scale=z.consts[:, CHALF:CHALF + 1],
                    bias=z.halfbase[:, eo:eo + 1]),
                waits=[ev_mm, z.ev_halfbase, ev_add[eo % 2]], sem="s")
            off = eo * ttot + t0
            ev_add[eo % 2] = P.op(
                "gpsimd", lambda e, off=off, scr=scr, pb=pb, TT=TT: e.tensor_tensor(
                    z.x[:, off:off + TT], z.x[:, off:off + TT], scr[:, 0:TT],
                    op=AO.add), waits=[ev_evac[pb - 2], ev_add[eo % 2]], sem="q")
            ev_last_add = ev_add[eo % 2]
        if q == 3:
            tile_adds.append([ev_add[0], ev_add[1]])
        if k + 2 < len(quarters):
            load(k + 2)
    z.free["p2"], z.free["p3"] = [ev_evac[0]], [ev_evac[1]]
    z.free["p4"], z.free["p5"] = [ev_evac[2]], [ev_evac[3]]
    z.free["wencsp0"] = [ev_mm_free[0]]
    z.free["wencsp1"] = [ev_mm_free[1]]
    z.free["sc0"] = [ev_add[0]]
    z.free["sc1"] = [ev_add[1]]
    return ev_last_add, [ev_add[0], ev_add[1]], tile_adds


def plan_adapter(P, z, ev_xn, ev_sel):
    ttot, nt = z.ttot, z.nt
    TT = ttot // nt
    ev_wdl, ev_wul, ev_bd, ev_bush = ev_sel
    mid_guard = F(z, "sc4")
    msp_guard = F(z, "msp")
    ubank = (z.p[3], z.p[4], z.p[5])
    ev_up_evac = {0: F(z, "p3"), 1: F(z, "p4"), 2: F(z, "p5")}
    ev_add = {0: F(z, "sc2"), 1: F(z, "sc3")}
    ev_mid_free = F(z, "p2")
    ev_up_last = None
    ev_msp = None
    ev_mid_relu = None
    ev_last = None
    tile_adds = []
    for i in range(nt):
        t0 = i * TT

        def emit_mid(e, t0=t0, TT=TT):
            last = None
            first = True
            for (wt, xt) in ((z.wdselh, z.xnh), (z.wdselh, z.xnl), (z.wdsell, z.xnh)):
                for c in range(C):
                    last = e.matmul(
                        z.p[2][0:64, 0:TT], wt[:, c * 64:(c + 1) * 64],
                        xt[:, c * ttot + t0: c * ttot + t0 + TT],
                        start=first, stop=(wt is z.wdsell and c == C - 1))
                    first = False
            return last
        ev_mid = P.op("tensor", emit_mid,
                      waits=[ev_xn[i], ev_wdl, ev_mid_free], sem="t")
        ev_mid_relu = P.op("scalar", lambda e, TT=TT: e.activation(
            out=z.sc4[0:64, 0:TT], in_=z.p[2][0:64, 0:TT], func=AF.Relu,
            bias=z.bsel[0:64, 0:1]),
            waits=[ev_mid, ev_bd, mid_guard, ev_msp], sem="s")
        ev_mid_free = [ev_mid_relu]
        ev_mh = P.op("vector", lambda e, TT=TT: e.tensor_copy(
            z.msph[0:64, 0:TT], z.sc4[0:64, 0:TT]),
            waits=[ev_mid_relu, msp_guard, ev_up_last], sem="v")
        ev_msp = P.op("vector", lambda e, TT=TT: e.tensor_tensor(
            z.mspl[0:64, 0:TT], z.sc4[0:64, 0:TT], z.msph[0:64, 0:TT],
            op=AO.subtract), waits=[ev_mh], sem="v")
        mid_guard = [ev_msp]
        for eo in range(C):
            ub = eo % 3
            bank = ubank[ub]

            def emit_up(e, eo=eo, bank=bank, TT=TT):
                last = None
                first = True
                for (wt, mt) in ((z.wupselh, z.msph), (z.wupselh, z.mspl),
                                 (z.wupsell, z.msph)):
                    last = e.matmul(
                        bank[:, 0:TT], wt[0:64, eo * 128:(eo + 1) * 128],
                        mt[0:64, 0:TT], start=first,
                        stop=(wt is z.wupsell))
                    first = False
                return last
            ev_up = P.op("tensor", emit_up,
                         waits=[ev_msp, ev_wul, ev_up_evac[ub]], sem="t")
            ev_up_last = ev_up
            scr = (z.sc2, z.sc3)[eo % 2]
            ev_evac_cur = P.op(
                "scalar", lambda e, eo=eo, bank=bank, scr=scr, TT=TT: e.activation(
                    out=scr[:, 0:TT], in_=bank[:, 0:TT], func=AF.Identity,
                    scale=z.consts[:, CHALF:CHALF + 1],
                    bias=z.bsel[:, 18 + eo:19 + eo]),
                waits=[ev_up, ev_bush, ev_add[eo % 2]], sem="s")
            ev_up_evac[ub] = ev_evac_cur
            off = eo * ttot + t0
            ev_add[eo % 2] = P.op(
                "gpsimd", lambda e, off=off, scr=scr, TT=TT: e.tensor_tensor(
                    z.x[:, off:off + TT], z.x[:, off:off + TT], scr[:, 0:TT],
                    op=AO.add), waits=[ev_evac_cur, ev_add[eo % 2]], sem="q")
            ev_last = ev_add[eo % 2]
        tile_adds.append([ev_add[0], ev_add[1]])
    z.free["p2"] = ev_mid_free
    z.free["p3"] = [ev_up_evac[0]]
    z.free["p4"] = [ev_up_evac[1]]
    z.free["p5"] = [ev_up_evac[2]]
    z.free["sc4"] = [ev_msp]
    z.free["msp"] = [ev_up_last]
    z.free["adapters"] = [ev_up_last, ev_up_evac[0], ev_up_evac[1], ev_mid_relu]
    z.free["wdselsp"] = [ev_mid]
    z.free["wupselsp"] = [ev_up_last]
    z.free["sc2"] = [ev_add[0]]
    z.free["sc3"] = [ev_add[1]]
    return ev_last, [ev_add[0], ev_add[1]], tile_adds


def plan_w23(P, z):
    ev_w2 = P.op("sync", lambda e: e.dma_start(out=z.wtrans[:, 0:128], in_=z.d_w2),
                 waits=[F(z, "wtrans")], sem="we0")
    ev_w3 = P.op("sync", lambda e: e.dma_start(out=z.wtrans[:, 128:256], in_=z.d_w3),
                 sem="we0")
    ev_w23h = P.op("vector", lambda e: e.tensor_copy(
        z.w23sp[:, 0:256], z.wtrans[:, 0:256]),
        waits=[ev_w2, ev_w3, F(z, "w23sp")], sem="v")
    ev_w23l = P.op("vector", lambda e: e.tensor_tensor(
        z.w23sp[:, 256:512], z.wtrans[:, 0:256], z.w23sp[:, 0:256], op=AO.subtract),
        waits=[ev_w23h], sem="v")
    z.free["wtrans"] = [ev_w23l]
    return ev_w23l


def plan_mixer(P, z, ev_xn, pre=None):
    ttot, nt = z.ttot, z.nt
    TT = ttot // nt
    ev_sp = [None, None]
    sp_guard = [F(z, "wencsp0"), F(z, "wencsp1")]
    ev_mm_free = [None, None]
    quarters = [(i, q) for i in range(nt) for q in range(4)]

    def load(k):
        i, q = quarters[k]
        b = k % 2
        ev_sp[b] = plan_wstream(
            P, z, z.d_w1[:, q * 2048:(q + 1) * 2048], z.wencsp[b],
            [sp_guard[b], ev_mm_free[b]])
        sp_guard[b] = None

    ev_w2 = P.op("sync", lambda e: e.dma_start(out=z.wtrans[:, 0:128], in_=z.d_w2),
                 waits=[F(z, "wtrans")], sem="we0")
    ev_w3 = P.op("sync", lambda e: e.dma_start(out=z.wtrans[:, 128:256], in_=z.d_w3),
                 sem="we0")
    ev_w23h = P.op("vector", lambda e: e.tensor_copy(
        z.w23sp[:, 0:256], z.wtrans[:, 0:256]),
        waits=[ev_w2, ev_w3, F(z, "w23sp")], sem="v")
    ev_w23l = P.op("vector", lambda e: e.tensor_tensor(
        z.w23sp[:, 256:512], z.wtrans[:, 0:256], z.w23sp[:, 0:256], op=AO.subtract),
        waits=[ev_w23h], sem="v")
    z.free["wtrans"] = [ev_w23l]

    load(0)
    load(1)

    ev_BCe = {0: F(z, "p4"), 1: F(z, "p5")}
    ev_sbc_used = F(z, "p6")
    ev_sbc = None
    ev_sbc_prev = None
    ev_evac = {0: F(z, "p2"), 1: F(z, "p3")}
    ev_u0_guard = {0: F(z, "sc0"), 1: F(z, "sc1")}
    ev_dd_guard = {0: F(z, "sc2"), 1: F(z, "sc3")}
    sc4_guard = F(z, "sc4")
    sc5_guard = F(z, "sc5")
    ev_BCp_prev = None
    ev_Bm = ev_Cm = None
    ev_last = None
    tile_adds = []
    for k, (i, q) in enumerate(quarters):
        b = k % 2
        t0 = i * TT
        if q == 0:
            def emit_B(e, t0=t0, TT=TT):
                return _mm3(
                    e, z.p[4][0:16, 0:TT], z.w23sp, 0, 256,
                    lambda c: (c * 16, (c + 1) * 16),
                    z.xnh, z.xnl,
                    lambda c, t0=t0: (c * ttot + t0, c * ttot + t0 + TT),
                    first=True)

            def emit_Cm(e, t0=t0, TT=TT):
                return _mm3(
                    e, z.p[5][0:16, 0:TT], z.w23sp, 128, 384,
                    lambda c: (c * 16, (c + 1) * 16),
                    z.xnh, z.xnl,
                    lambda c, t0=t0: (c * ttot + t0, c * ttot + t0 + TT),
                    first=True)
            ev_Bm = P.op("tensor", emit_B, waits=[ev_xn[i], ev_w23l, ev_BCe[0]],
                         sem="t")
            ev_Cm = P.op("tensor", emit_Cm, waits=[ev_w23l, ev_BCe[1]], sem="t")
            ev_Be = P.op("scalar", lambda e, TT=TT: e.activation(
                out=z.sc4[0:16, 0:TT], in_=z.p[4][0:16, 0:TT], func=AF.Identity,
                bias=z.bias[0:16, B2_0:B2_0 + 1]),
                waits=[ev_Bm, z.ev_bias, ev_sbc_prev, sc4_guard], sem="s")
            ev_Ce = P.op("scalar", lambda e, TT=TT: e.activation(
                out=z.sc5[0:16, 0:TT], in_=z.p[5][0:16, 0:TT], func=AF.Identity,
                bias=z.bias[0:16, B3_0:B3_0 + 1]),
                waits=[ev_Cm, z.ev_bias, ev_BCp_prev, sc5_guard], sem="s")
            ev_BCe[0], ev_BCe[1] = [ev_Be], [ev_Ce]
            ev_BCp = P.op("vector", lambda e, TT=TT: e.tensor_tensor(
                z.sc4[0:16, 0:TT], z.sc4[0:16, 0:TT], z.sc5[0:16, 0:TT], op=AO.mult),
                waits=[ev_Be, ev_Ce], sem="v")
            ev_BCp_prev = ev_BCp
            ev_sbc = P.op("tensor", lambda e, TT=TT: e.matmul(
                z.p[6][:, 0:TT], z.consts[0:16, 0:128], z.sc4[0:16, 0:TT],
                start=True, stop=True), waits=[ev_BCp, ev_sbc_used], sem="t")
            ev_sbc_prev = ev_sbc
        for eo2 in range(2):
            eo = q * 2 + eo2
            pb = 2 + eo2

            def emit_w1(e, b=b, eo2=eo2, t0=t0, pb=pb, TT=TT):
                return _mm3(
                    e, z.p[pb][:, 0:TT], z.wencsp[b], 0, 2048,
                    lambda c, eo2=eo2: (c * 256 + eo2 * 128, c * 256 + eo2 * 128 + 128),
                    z.xnh, z.xnl,
                    lambda c, t0=t0: (c * ttot + t0, c * ttot + t0 + TT),
                    first=True)
            ev_mm = P.op("tensor", emit_w1,
                         waits=[ev_xn[i], ev_sp[b], ev_evac[eo2]], sem="t")
            if eo2 == 1:
                ev_mm_free[b] = ev_mm
            u0 = (z.sc0, z.sc1)[eo2]
            dd = (z.sc2, z.sc3)[eo2]
            ev_u0 = P.op("scalar", lambda e, eo=eo, pb=pb, u0=u0, TT=TT: e.activation(
                out=u0[:, 0:TT], in_=z.p[pb][:, 0:TT], func=AF.Identity,
                bias=z.bias[:, B1_0 + eo:B1_0 + eo + 1]),
                waits=[ev_mm, z.ev_bias, ev_u0_guard[eo2]], sem="s")
            ev_evac[eo2] = [ev_u0]
            ev_uc = P.op("vector", lambda e, u0=u0, dd=dd, TT=TT: e.tensor_scalar(
                dd[:, 0:TT], u0[:, 0:TT], z.consts[:, C30:C30 + 1], None, op0=AO.min),
                waits=[ev_u0, ev_dd_guard[eo2]], sem="v")
            ev_ex = P.op("scalar", lambda e, dd=dd, TT=TT: e.activation(
                out=dd[:, 0:TT], in_=dd[:, 0:TT], func=AF.Exp), waits=[ev_uc], sem="s")
            ev_ln = P.op("scalar", lambda e, dd=dd, TT=TT: e.activation(
                out=dd[:, 0:TT], in_=dd[:, 0:TT], func=AF.Ln,
                bias=z.consts[:, 0:1]), waits=[ev_ex], sem="s")
            ev_dl = P.op("vector", lambda e, u0=u0, dd=dd, TT=TT: e.tensor_tensor(
                dd[:, 0:TT], u0[:, 0:TT], dd[:, 0:TT], op=AO.max),
                waits=[ev_ln], sem="v")
            off = eo * ttot + t0
            ev_xnf = P.op("vector", lambda e, off=off, u0=u0, TT=TT: e.tensor_tensor(
                u0[:, 0:TT], z.xnh[:, off:off + TT], z.xnl[:, off:off + TT],
                op=AO.add), waits=[ev_dl], sem="v")
            ev_p1 = P.op("vector", lambda e, u0=u0, dd=dd, TT=TT: e.tensor_tensor(
                dd[:, 0:TT], u0[:, 0:TT], dd[:, 0:TT], op=AO.mult),
                waits=[ev_xnf], sem="v")
            ev_u0_guard[eo2] = [ev_p1]
            ev_p2 = P.op("vector", lambda e, dd=dd, TT=TT: e.tensor_tensor(
                dd[:, 0:TT], dd[:, 0:TT], z.p[6][:, 0:TT], op=AO.mult),
                waits=[ev_p1, ev_sbc], sem="v")
            ev_add = P.op("vector", lambda e, off=off, dd=dd, TT=TT: e.tensor_tensor(
                z.x[:, off:off + TT], z.x[:, off:off + TT], dd[:, 0:TT], op=AO.add),
                waits=[ev_p2], sem="v")
            ev_dd_guard[eo2] = [ev_add]
            ev_last = ev_add
            if eo == 7:
                ev_sbc_used = [ev_p2]
        if q == 3:
            tile_adds.append([ev_dd_guard[0], ev_dd_guard[1]])
        if k + 2 < len(quarters):
            load(k + 2)
    z.free["p2"], z.free["p3"] = ev_evac[0], ev_evac[1]
    z.free["p4"], z.free["p5"] = ev_BCe[0], ev_BCe[1]
    z.free["p6"] = ev_sbc_used
    z.free["wencsp0"] = [ev_mm_free[0]]
    z.free["wencsp1"] = [ev_mm_free[1]]
    z.free["w23sp"] = [ev_Bm, ev_Cm]
    z.free["sc0"] = ev_u0_guard[0]
    z.free["sc1"] = ev_u0_guard[1]
    z.free["sc2"] = ev_dd_guard[0]
    z.free["sc3"] = ev_dd_guard[1]
    z.free["sc4"] = [ev_sbc]
    z.free["sc5"] = [ev_BCp_prev]
    return ev_last, tile_adds



def build_core_program(ttot=2048, nt=4, dbg=False):
    TT = ttot // nt
    assert TT <= 512
    nc = bass.Bass("TRN2", target_bir_lowering=False)
    z = Ctx()
    z.ttot, z.nt = ttot, nt
    z.free = {}
    z.d_x = nc.dram_tensor("x", [128, C * ttot], F32, kind="ExternalInput").ap()
    z.d_wenc = nc.dram_tensor("wenc", [NE, 128, 2048], F32, kind="ExternalInput").ap()
    z.d_wdec = nc.dram_tensor("wdec", [NE, 128, 2048], F32, kind="ExternalInput").ap()
    z.d_wbase = nc.dram_tensor("wbase", [128, 8192], F32, kind="ExternalInput").ap()
    z.d_w1 = nc.dram_tensor("w1", [128, 8192], F32, kind="ExternalInput").ap()
    z.d_w2 = nc.dram_tensor("w2", [128, 128], F32, kind="ExternalInput").ap()
    z.d_w3 = nc.dram_tensor("w3", [128, 128], F32, kind="ExternalInput").ap()
    z.d_wdown = nc.dram_tensor("wdown", [NE, 128, 512], F32, kind="ExternalInput").ap()
    z.d_wup = nc.dram_tensor("wup", [NE, 64, 1024], F32, kind="ExternalInput").ap()
    z.d_biases = nc.dram_tensor("biases", [128, NBIAS], F32, kind="ExternalInput").ap()
    z.d_consts = nc.dram_tensor("consts", [128, NCONST], F32, kind="ExternalInput").ap()
    z.d_ci = nc.dram_tensor("ci", [1, 8], I32, kind="ExternalInput").ap()
    z.d_y = nc.dram_tensor("y", [128, C * ttot], F32, kind="ExternalOutput").ap()
    z.d_arin = [nc.dram_tensor(f"arin{k}", [1, 8], F32).ap() for k in range(2)]
    z.d_arout = [nc.dram_tensor(f"arout{k}", [1, 8], F32).ap() for k in range(2)]
    if dbg:
        z.d_dbg_xn = nc.dram_tensor("dbg_xn", [128, C * ttot], BF16,
                                    kind="ExternalOutput").ap()
        z.d_dbg_xnl = nc.dram_tensor("dbg_xnl", [128, C * ttot], BF16,
                                     kind="ExternalOutput").ap()
        z.d_dbg_row = nc.dram_tensor("dbg_row", [1, 64], F32, kind="ExternalOutput").ap()
        z.d_dbg_lacc = nc.dram_tensor("dbg_lacc", [128, 8], F32,
                                      kind="ExternalOutput").ap()
        z.d_dbg_sel = nc.dram_tensor("dbg_sel", [128, 512], BF16,
                                     kind="ExternalOutput").ap()
        z.d_dbg_x1 = nc.dram_tensor("dbg_x1", [128, C * ttot], F32,
                                    kind="ExternalOutput").ap()
        z.d_dbg_x2 = nc.dram_tensor("dbg_x2", [128, C * ttot], F32,
                                    kind="ExternalOutput").ap()

    es = contextlib.ExitStack()
    sb = lambda name, shape, dt=F32: es.enter_context(nc.sbuf_tensor(name, shape, dt))
    with es:
        z.consts = sb("consts_sb", [128, NCONST])
        z.bias = sb("bias_sb", [128, NBIAS])
        z.halfbase = sb("halfbase", [128, 8])
        z.ci = sb("ci_sb", [1, 8], I32)
        z.cif = sb("cif", [1, 8])
        z.x = sb("x_sb", [128, C * ttot])
        z.xnh = sb("xnh", [128, C * ttot], BF16)
        z.xnl = sb("xnl", [128, C * ttot], BF16)
        z.wtrans = sb("wtrans", [128, 2048])
        z.wencsp = [sb("wencsp0", [128, 4096], BF16), sb("wencsp1", [128, 4096], BF16)]
        z.wdecsp = sb("wdecsp", [128, 4096], BF16)
        z.hsp0 = sb("hsp0", [128, 4 * TT], BF16)
        z.hsp1 = sb("hsp1", [128, 4 * TT], BF16)
        z.scrM = sb("scrM", [128, TT])
        z.scrR = sb("scrR", [128, TT])
        z.sc0 = sb("sc0", [128, 512])
        z.sc1 = sb("sc1", [128, 512])
        z.sc2 = sb("sc2", [128, 512])
        z.sc3 = sb("sc3", [128, 512])
        z.sc4 = sb("sc4", [128, 512])
        z.sc5 = sb("sc5", [128, 512])
        z.slots = sb("slots", [128, NE * nt * C])
        z.lacc = sb("lacc", [128, 8])
        z.rowbuf = sb("rowbuf", [1, 64])
        z.arst = sb("arst", [1, 8])
        z.lrow = sb("lrow", [1, 8])
        z.mskb = sb("mskb", [128, 8])
        z.selbuf = sb("selbuf", [128, 1536])
        z.wupsel = sb("wupsel", [64, 1024])
        z.wdselh = sb("wdselh", [128, 512], BF16)
        z.wdsell = sb("wdsell", [128, 512], BF16)
        z.wupselh = sb("wupselh", [64, 1024], BF16)
        z.wupsell = sb("wupsell", [64, 1024], BF16)
        z.msph = sb("msph", [64, 512], BF16)
        z.mspl = sb("mspl", [64, 512], BF16)
        z.bsel = sb("bsel", [128, 35])
        z.w23sp = sb("w23sp", [128, 512], BF16)
        z.p = [es.enter_context(nc.psum_tensor(f"pb{k}", [128, 512], F32))
               for k in range(8)]

        sems = {name: es.enter_context(nc.semaphore(name))
                for name in ("we0", "ws", "ws2", "g", "t", "s", "v", "q", "c")}
        P = Pipe(sems)
        P.chain["g"] = None

        plan_prelude(P, z)
        gate = []
        base_tiles = ad_tiles = None
        base_adds = ad_adds = []
        for call in range(2):
            exp_pre = (
                plan_wstream(P, z, z.d_wenc[0], z.wencsp[0], [F(z, "wencsp0")]),
                plan_wstream(P, z, z.d_wdec[0], z.wdecsp, [F(z, "wdecsp")]),
                plan_wstream(P, z, z.d_wenc[1], z.wencsp[1], [F(z, "wencsp1")]),
            )
            ev_xn = plan_stats_xn(P, z, gate)
            pre_box = {}

            def tail_hook(g0, g1, pre_box=pre_box):
                pre_box["sp0"] = plan_wstream(P, z, z.d_wbase[:, 0:2048],
                                              z.wencsp[0], [g0])
                pre_box["sp1"] = plan_wstream(P, z, z.d_wbase[:, 2048:4096],
                                              z.wencsp[1], [g1])
            ev_sq_pair = plan_experts(P, z, ev_xn, tail_hook, pre=exp_pre)
            ev_lrow = plan_loss_ar_issue(P, z, call, ev_sq_pair)
            plan_mask(P, z, ev_lrow)
            ev_base, base_adds, base_tiles = plan_base(
                P, z, ev_xn, pre=(pre_box["sp0"], pre_box["sp1"]))
            ev_mskb = plan_mask_bc(P, z)
            ev_sel = plan_select(P, z, ev_mskb)
            ev_ad, ad_adds, ad_tiles = plan_adapter(P, z, ev_xn, ev_sel)
            z.free["mskb"] = [ev_sel[3]]
            dbg_evs = []
            if dbg and call == 0:
                dbg_evs.append(P.op(
                    "gpsimd", lambda e: e.dma_start(out=z.d_dbg_xn, in_=z.xnh[:]),
                    waits=[ev_sq_pair], sem="g"))
                dbg_evs.append(P.op(
                    "gpsimd", lambda e: e.dma_start(out=z.d_dbg_xnl, in_=z.xnl[:]),
                    waits=[ev_sq_pair], sem="g"))
                dbg_evs.append(P.op(
                    "gpsimd", lambda e: e.dma_start(out=z.d_dbg_lacc, in_=z.lacc[:]),
                    waits=[z.free["lacc"]], sem="g"))
                dbg_evs.append(P.op(
                    "gpsimd", lambda e: e.dma_start(
                        out=z.d_dbg_row, in_=z.rowbuf[0:1, 0:64]),
                    waits=[ev_mskb], sem="g"))
                dbg_evs.append(P.op(
                    "gpsimd", lambda e: e.dma_start(out=z.d_dbg_sel, in_=z.wdselh[:]),
                    waits=[ev_sel[0]], sem="g"))
                dbg_evs.append(P.op(
                    "gpsimd", lambda e: e.dma_start(out=z.d_dbg_x1, in_=z.x[:]),
                    waits=[base_adds, ad_adds], sem="g"))
            if call == 0:
                mix_pre = (
                    plan_wstream(P, z, z.d_w1[:, 0:2048], z.wencsp[0],
                                 [F(z, "wencsp0")]),
                    plan_wstream(P, z, z.d_w1[:, 2048:4096], z.wencsp[1],
                                 [F(z, "wencsp1")]),
                    plan_w23(P, z),
                )
                gates = [base_tiles[i] + ad_tiles[i] + dbg_evs for i in range(nt)]
                ev_mix, mix_tiles = plan_mixer(P, z, plan_stats_xn(P, z, gates),
                                               pre=mix_pre)
                gate = [mix_tiles[i] for i in range(nt)]
                if dbg:
                    ev_dx2 = P.op(
                        "gpsimd", lambda e: e.dma_start(out=z.d_dbg_x2, in_=z.x[:]),
                        waits=[ev_mix], sem="g")
                    gate = [g + [ev_dx2] for g in gate]
        yd3 = z.d_y.rearrange("p (c t) -> p c t", c=C)
        ys3 = z.x[:].rearrange("p (c t) -> p c t", c=C)
        ev_out = None
        for i in range(nt):
            ev_out = P.op(
                "gpsimd", lambda e, i=i: e.dma_start(
                    out=yd3[:, :, i * TT:(i + 1) * TT],
                    in_=ys3[:, :, i * TT:(i + 1) * TT]),
                waits=[base_tiles[i], ad_tiles[i]], sem="g")
        P.wait_only("gpsimd", [ev_out])
        with nc.Block() as block:
            P.emit_into(block)
    return nc, z


_BUILD_CACHE = {}


def _get_program(ttot=2048, nt=4, dbg=False):
    key = (ttot, nt, dbg)
    if key not in _BUILD_CACHE:
        _BUILD_CACHE[key] = build_core_program(ttot, nt, dbg)
    return _BUILD_CACHE[key]



def _fm(a):
    Dd, T = a.shape
    return np.ascontiguousarray(
        a.reshape(Dd // 128, 128, T).transpose(1, 0, 2).reshape(128, (Dd // 128) * T))


def make_in_maps(inputs, ttot=2048):
    f32 = lambda v: np.asarray(v, dtype=np.float32)
    x = f32(inputs["L0_attn_x"]).reshape(-1, D)
    W1 = f32(inputs["L0_attn_W1"])
    b1 = f32(inputs["L0_attn_b1"])
    W2 = f32(inputs["L0_attn_W2"])
    b2 = f32(inputs["L0_attn_b2"])
    W3 = f32(inputs["L0_attn_W3"])
    b3 = f32(inputs["L0_attn_b3"])
    Wb = f32(inputs["L0_ffn_W_base"])
    bb = f32(inputs["L0_ffn_b_base"])
    We = f32(inputs["L0_ffn_W_enc"])
    be = f32(inputs["L0_ffn_b_enc"])
    Wd = f32(inputs["L0_ffn_W_dec"])
    bd = f32(inputs["L0_ffn_b_dec"])
    Wdn = f32(inputs["L0_ffn_W_down"])
    bdn = f32(inputs["L0_ffn_b_down"])
    Wup = f32(inputs["L0_ffn_W_up"])
    bup = f32(inputs["L0_ffn_b_up"])
    ci = np.asarray(inputs["L0_ffn_connected_idx"], dtype=np.int32).reshape(1, 8)

    wenc = np.stack([_fm(We[n]) for n in range(NE)])
    wdec = np.stack([_fm(Wd[n]) for n in range(NE)])

    def _quarter_layout(W):
        a = W.reshape(C, 128, 2, 2, 2, 128)
        a = a.transpose(1, 2, 3, 0, 4, 5)
        return np.ascontiguousarray(a.reshape(128, 8192))
    wbase = _quarter_layout(Wb)
    w1 = _quarter_layout(W1)
    w2 = _fm(W2)
    w3 = _fm(W3)
    wdown = np.stack([_fm(Wdn[n]) for n in range(NE)])
    wupl = np.ascontiguousarray(Wup)

    biases = np.zeros((128, NBIAS), np.float32)
    biases[:, B1_0:B1_0 + 8] = b1.reshape(C, 128).T
    biases[:, BBASE0:BBASE0 + 8] = bb.reshape(C, 128).T
    biases[0:16, B2_0] = b2
    biases[0:16, B3_0] = b3
    for n in range(NE):
        biases[:, BENC0 + n * 2:BENC0 + n * 2 + 2] = be[n].reshape(2, 128).T
        biases[:, BDEC0 + n * 8:BDEC0 + n * 8 + 8] = bd[n].reshape(C, 128).T
        biases[0:64, BDOWN0 + n] = bdn[n]
        biases[:, BUP0 + n * 8:BUP0 + n * 8 + 8] = bup[n].reshape(C, 128).T

    consts = np.zeros((128, NCONST), np.float32)
    consts[:, 0:128] = 1.0
    consts[:, CEPS] = 1e-5
    consts[:, CINV] = 1.0 / 1024.0
    consts[:, C30] = 30.0
    consts[:, CHALF] = 0.5
    consts[:, CZERO] = 0.0
    consts[:, CNEG1] = -1.0
    for j in range(8):
        consts[:, CJ0 + j] = float(j)
    consts[:, CM05] = -0.5

    shared = dict(wenc=wenc, wdec=wdec, wbase=wbase, w1=w1, w2=w2, w3=w3,
                  wdown=wdown, wup=wupl, biases=biases, consts=consts, ci=ci)
    in_maps = []
    for core in range(NCORES):
        xc = x[core * ttot:(core + 1) * ttot].T
        m = dict(shared)
        m["x"] = _fm(xc)
        in_maps.append(m)
    return in_maps


def assemble_out(results, ttot=2048):
    parts = []
    for r in results:
        y = np.asarray(r["y"])
        a = y.reshape(128, C, ttot).transpose(1, 0, 2).reshape(D, ttot).T
        parts.append(a)
    out = np.concatenate(parts, axis=0)
    return np.ascontiguousarray(out.reshape(4, 4096, D), dtype=np.float32)


def kernel(**inputs):
    ttot = 2048
    nc, _ = _get_program(ttot=ttot, nt=4, dbg=False)
    in_maps = make_in_maps(inputs, ttot=ttot)
    res = run_bass_kernel_spmd(nc, in_maps, core_ids=list(range(NCORES)))
    return assemble_out(res.results, ttot=ttot)


# version 18 (final)
# speedup vs baseline: 1.0688x; 1.0072x over previous
import contextlib
import numpy as np
import concourse.bass as bass
import concourse.mybir as mybir
from concourse.bass_utils import run_bass_kernel_spmd
from concourse.alu_op_type import AluOpType as AO

F32 = mybir.dt.float32
BF16 = mybir.dt.bfloat16
I32 = mybir.dt.int32
AF = mybir.ActivationFunctionType
AX = mybir.AxisListType

D = 1024
C = 8
NE = 8
HC = 2
NCORES = 8

CEPS, CINV, C30, CHALF, CZERO, CNEG1, CJ0, CM05 = 128, 129, 130, 131, 132, 133, 134, 142
NCONST = 144

B1_0, BBASE0, B2_0, B3_0, BENC0, BDEC0, BDOWN0, BUP0 = 0, 8, 16, 17, 18, 34, 98, 106
NBIAS = 176


class Pipe:
    ENGS = ("sync", "gpsimd", "tensor", "scalar", "vector")
    COMPUTE = ("t", "s", "v", "q")

    def __init__(self, sems):
        self.sems = sems
        self.ops = []
        self.cnt = {k: 0 for k in sems}
        self.chain = {}

    def is_dma(self, sem):
        return sem is not None and sem not in self.COMPUTE and sem != "c"

    def _flat(self, waits):
        for w in waits:
            if w is None:
                continue
            if isinstance(w, list):
                yield from self._flat(w)
            else:
                yield w

    def op(self, eng, emit, waits=(), sem=None, n=1):
        waits = list(waits)
        if sem in self.chain and self.chain[sem] is not None:
            waits.append(self.chain[sem])
        wmax = {}
        for (s, v, peng) in self._flat(waits):
            if v <= 0:
                continue
            assert v <= self.cnt[s], f"wait on future event {s}={v} (cnt={self.cnt[s]})"
            if v > wmax.get(s, 0):
                wmax[s] = v
        ws = list(wmax.items())
        post = None
        if sem is not None:
            self.cnt[sem] += 16 * n if self.is_dma(sem) else n
            post = (sem, self.cnt[sem], eng)
            if sem in self.chain:
                self.chain[sem] = post
        self.ops.append((eng, tuple(ws), emit, sem, n))
        return post

    def wait_only(self, eng, waits):
        self.op(eng, None, waits=waits)

    def emit_into(self, block):
        for eng_name in self.ENGS:
            def mk(eng_name=eng_name):
                def f(eng):
                    lw = {k: 0 for k in self.sems}
                    for (e, ws, emit, sem, n) in self.ops:
                        if e != eng_name:
                            continue
                        for (s, v) in ws:
                            if v > lw[s]:
                                eng.wait_ge(self.sems[s], v)
                                lw[s] = v
                        if emit is None:
                            continue
                        last = emit(eng)
                        if sem is not None:
                            last.then_inc(self.sems[sem],
                                          16 * n if self.is_dma(sem) else n)
                return f
            getattr(block, eng_name)(mk())


class Ctx:
    pass


def F(z, key):
    return z.free.get(key, [])



def plan_prelude(P, z):
    e_con = P.op("gpsimd", lambda e: e.dma_start(out=z.consts[:], in_=z.d_consts), sem="g")
    e_bias = P.op("gpsimd", lambda e: e.dma_start(out=z.bias[:], in_=z.d_biases), sem="g")
    e_ci = P.op("gpsimd", lambda e: e.dma_start(out=z.ci[:], in_=z.d_ci), sem="g")
    nt, ttot = z.nt, z.ttot
    TT = ttot // nt
    xd3 = z.d_x.rearrange("p (c t) -> p c t", c=C)
    xs3 = z.x[:].rearrange("p (c t) -> p c t", c=C)
    z.ev_xtile = []
    for i in range(nt):
        z.ev_xtile.append(P.op(
            "gpsimd", lambda e, i=i: e.dma_start(
                out=xs3[:, :, i * TT:(i + 1) * TT], in_=xd3[:, :, i * TT:(i + 1) * TT]),
            sem="g"))
    z.ev_x = z.ev_xtile[-1]
    z.ev_con = e_con
    z.ev_bias = e_bias
    z.ev_cif = P.op("vector", lambda e: e.tensor_copy(z.cif[:], z.ci[:]),
                    waits=[e_ci], sem="v")
    z.ev_halfbase = P.op(
        "vector", lambda e: e.tensor_scalar(
            z.halfbase[:, 0:8], z.bias[:, BBASE0:BBASE0 + 8],
            z.consts[:, CHALF:CHALF + 1], None, op0=AO.mult),
        waits=[e_bias, e_con], sem="v")
    z.ev_rb0 = P.op(
        "vector", lambda e: e.tensor_scalar(
            z.rowbuf[0:1, 0:64], z.consts[0:1, 0:64],
            z.consts[0:1, CZERO:CZERO + 1], None, op0=AO.mult),
        waits=[e_con], sem="v")


def plan_stats_xn(P, z, gate):
    ttot, nt = z.ttot, z.nt
    TT = ttot // nt
    x = z.x
    free_p0 = F(z, "p0")
    free_p1 = F(z, "p1")
    ev_tile = []
    e6 = None
    ev_Q_prev = None
    ev_qmm = {}
    sc3_guard = F(z, "sc3")
    per_tile = bool(gate) and isinstance(gate[0], list) and len(gate) == nt
    gates = gate if per_tile else [gate] * nt
    for i in range(nt):
        gate = gates[i]
        t0 = i * TT

        def emit_S(e, t0=t0, TT=TT):
            last = None
            for c in range(C):
                last = e.matmul(z.p[0][:, 0:TT], z.consts[:, 0:128],
                                x[:, c * ttot + t0: c * ttot + t0 + TT],
                                start=(c == 0), stop=(c == C - 1))
            return last
        ev_S = P.op("tensor", emit_S,
                    waits=[gate, z.ev_xtile[i], z.ev_con, free_p0], sem="t")

        ev_sq = [None, None]
        for c in range(C):
            scr = (z.sc0, z.sc1)[c % 2]
            if c >= 2:
                w_extra = [ev_qmm.get(c % 2)]
            else:
                w_extra = [ev_qmm.get(c % 2), F(z, ("sc0", "sc1")[c % 2]), free_p1]
            ev_sq[c % 2] = P.op(
                "scalar",
                lambda e, c=c, scr=scr, t0=t0, TT=TT: e.activation(
                    out=scr[:, 0:TT], in_=x[:, c * ttot + t0: c * ttot + t0 + TT],
                    func=AF.Square),
                waits=[gate, z.ev_xtile[i]] + w_extra, sem="s")
            ev_qmm[c % 2] = P.op(
                "tensor",
                lambda e, c=c, scr=scr, TT=TT: e.matmul(
                    z.p[1][:, 0:TT], z.consts[:, 0:128], scr[:, 0:TT],
                    start=(c == 0), stop=(c == C - 1)),
                waits=[ev_sq[c % 2]], sem="t")
        ev_Q = ev_qmm[(C - 1) % 2]
        ev_Q_prev = ev_Q

        e1 = P.op("vector", lambda e, TT=TT: e.tensor_scalar(
            z.scrM[:, 0:TT], z.p[0][:, 0:TT], z.consts[:, CINV:CINV + 1], None,
            op0=AO.mult), waits=[ev_S, ev_tile[-1] if ev_tile else None], sem="v")
        e2 = P.op("vector", lambda e, TT=TT: e.tensor_scalar(
            z.sc2[:, 0:TT], z.p[1][:, 0:TT], z.consts[:, CINV:CINV + 1], None,
            op0=AO.mult), waits=[ev_Q, F(z, "sc2")], sem="v")
        free_p0, free_p1 = [e1], [e2]
        e3 = P.op("vector", lambda e, TT=TT: e.tensor_tensor(
            z.scrR[:, 0:TT], z.scrM[:, 0:TT], z.scrM[:, 0:TT], op=AO.mult),
            waits=[e1, e6], sem="v")
        e4 = P.op("vector", lambda e, TT=TT: e.tensor_tensor(
            z.sc2[:, 0:TT], z.sc2[:, 0:TT], z.scrR[:, 0:TT], op=AO.subtract),
            waits=[e3], sem="v")
        e5 = P.op("scalar", lambda e, TT=TT: e.activation(
            out=z.sc3[:, 0:TT], in_=z.sc2[:, 0:TT], func=AF.Ln,
            bias=z.consts[:, CEPS:CEPS + 1]),
            waits=[e4, sc3_guard, ev_tile[-1] if ev_tile else None], sem="s")
        e6 = P.op("scalar", lambda e, TT=TT: e.activation(
            out=z.scrR[:, 0:TT], in_=z.sc3[:, 0:TT], func=AF.Exp,
            scale=z.consts[:, CM05:CM05 + 1]), waits=[e5], sem="s")

        elast = None
        for c in range(C):
            off = c * ttot + t0
            e7 = P.op("vector", lambda e, off=off, TT=TT: e.tensor_tensor(
                z.sc3[:, 0:TT], x[:, off: off + TT], z.scrM[:, 0:TT],
                op=AO.subtract), waits=[e1, elast, e6], sem="v")
            e8 = P.op("vector", lambda e, TT=TT: e.tensor_tensor(
                z.sc3[:, 0:TT], z.sc3[:, 0:TT], z.scrR[:, 0:TT],
                op=AO.mult), waits=[e6, e7], sem="v")
            e9 = P.op("vector", lambda e, off=off, TT=TT: e.tensor_copy(
                z.xnh[:, off: off + TT], z.sc3[:, 0:TT]), waits=[e8], sem="v")
            elast = P.op("vector", lambda e, off=off, TT=TT: e.tensor_tensor(
                z.xnl[:, off: off + TT], z.sc3[:, 0:TT], z.xnh[:, off: off + TT],
                op=AO.subtract), waits=[e9], sem="v")
        ev_tile.append(elast)
    z.free["p0"] = free_p0
    z.free["p1"] = free_p1
    z.free["sc0"] = [ev_qmm[0]]
    z.free["sc1"] = [ev_qmm[1]]
    z.free["sc2"] = [e6]
    z.free["sc3"] = [ev_tile[-1]]
    return ev_tile


def _mm3(e, psum_ap, wsp, whoff, wloff, wslice, xh, xl, xslices, first):
    last = None
    passes = [(whoff, xh), (whoff, xl), (wloff, xh)]
    for pi, (woff, xs) in enumerate(passes):
        for c in range(C):
            w0, w1 = wslice(c)
            x0, x1 = xslices(c)
            last = e.matmul(psum_ap, wsp[:, woff + w0: woff + w1],
                            xs[:, x0:x1],
                            start=(pi == 0 and c == 0 and first),
                            stop=(pi == 2 and c == C - 1))
    return last


def plan_wstream(P, z, dram_src, target, tgt_guard):
    ev_dma = P.op("sync", lambda e: e.dma_start(out=z.wtrans[:], in_=dram_src),
                  waits=[F(z, "wtrans")], sem="we0")
    ev_hi = P.op("vector", lambda e: e.tensor_copy(target[:, 0:2048], z.wtrans[:]),
                 waits=[ev_dma, tgt_guard], sem="v")
    ev_lo = P.op("vector", lambda e: e.tensor_tensor(
        target[:, 2048:4096], z.wtrans[:], target[:, 0:2048], op=AO.subtract),
        waits=[ev_hi], sem="v")
    z.free["wtrans"] = [ev_lo]
    return ev_lo


def plan_experts(P, z, ev_xn, tail_hook=None, pre=None):
    ttot, nt = z.ttot, z.nt
    TT = ttot // nt
    ev_enc_sp = [None, None]
    ev_dec_sp = None
    enc_guard = [F(z, "wencsp0"), F(z, "wencsp1")]
    dec_guard = F(z, "wdecsp")
    ev_enc_mm_last = [None, None]
    ev_dec_mm_last = None

    def load_enc(n):
        b = n % 2
        ev_enc_sp[b] = plan_wstream(P, z, z.d_wenc[n], z.wencsp[b],
                                    [enc_guard[b], ev_enc_mm_last[b]])

    def load_dec(n):
        nonlocal ev_dec_sp
        ev_dec_sp = plan_wstream(P, z, z.d_wdec[n], z.wdecsp,
                                 [dec_guard, ev_dec_mm_last])

    if pre is not None:
        ev_enc_sp[0], ev_dec_sp, ev_enc_sp[1] = pre
    else:
        load_enc(0)
        load_dec(0)
        load_enc(1)

    hp_free = {(0, 0): F(z, "p2"), (0, 1): F(z, "p3"),
               (1, 0): F(z, "p6"), (1, 1): F(z, "p7")}
    rbank = (z.p[4], z.p[5], z.p[0], z.p[1])
    rc_free = {0: F(z, "p4"), 1: F(z, "p5"), 2: F(z, "p0"), 3: F(z, "p1")}
    h32_guard = {0: F(z, "sc4"), 1: F(z, "sc5")}
    hsp_guard = {0: F(z, "hsp0"), 1: F(z, "hsp1")}
    ev_d = {0: F(z, "sc0"), 1: F(z, "sc1")}
    ev_sq = {0: F(z, "sc2"), 1: F(z, "sc3")}
    slots_guard = F(z, "slots")
    ev_hsplit = {}

    state = {"last_enc_mm": None, "last_dec_mm": None}

    def enc_tile(n, b, i, my_enc_sp):
        t0 = i * TT
        hspb = (z.hsp0, z.hsp1)[i % 2]
        pbank = (2, 6)[i % 2]
        for hc in range(HC):
            def emit_enc(e, b=b, hc=hc, t0=t0, pbank=pbank, TT=TT):
                return _mm3(
                    e, z.p[pbank + hc][:, 0:TT], z.wencsp[b], 0, 2048,
                    lambda c, hc=hc: (c * 256 + hc * 128, c * 256 + hc * 128 + 128),
                    z.xnh, z.xnl,
                    lambda c, t0=t0: (c * ttot + t0, c * ttot + t0 + TT),
                    first=True)
            ev_mm = P.op("tensor", emit_enc,
                         waits=[ev_xn[i], my_enc_sp, hp_free[(i % 2, hc)]],
                         sem="t")
            state["last_enc_mm"] = ev_mm
            h32 = (z.sc4, z.sc5)[hc]
            ev_relu = P.op(
                "scalar",
                lambda e, n=n, hc=hc, h32=h32, pbank=pbank, TT=TT: e.activation(
                    out=h32[:, 0:TT], in_=z.p[pbank + hc][:, 0:TT],
                    func=AF.Relu,
                    bias=z.bias[:, BENC0 + n * 2 + hc: BENC0 + n * 2 + hc + 1]),
                waits=[ev_mm, z.ev_bias, h32_guard[hc]], sem="s")
            hp_free[(i % 2, hc)] = [ev_relu]
            ev_hi = P.op("vector", lambda e, hc=hc, h32=h32, hspb=hspb, TT=TT:
                         e.tensor_copy(hspb[:, hc * TT: hc * TT + TT],
                                       h32[:, 0:TT]),
                         waits=[ev_relu, hsp_guard[i % 2]], sem="v")
            ev_lo = P.op("vector", lambda e, hc=hc, h32=h32, hspb=hspb, TT=TT:
                         e.tensor_tensor(
                             hspb[:, 2 * TT + hc * TT: 2 * TT + hc * TT + TT],
                             h32[:, 0:TT], hspb[:, hc * TT: hc * TT + TT],
                             op=AO.subtract),
                         waits=[ev_hi], sem="v")
            h32_guard[hc] = [ev_lo]
            ev_hsplit[(i % 2, hc)] = ev_lo

    pend = [None]

    def dec_tile(n, i, my_dec_sp):
        t0 = i * TT
        hspb = (z.hsp0, z.hsp1)[i % 2]
        for c in range(C):
            cb = c % 2
            cb4 = c % 4
            bank = rbank[cb4]
            scrRb = (z.sc0, z.sc1)[cb]
            scrDb = (z.sc2, z.sc3)[cb]

            def emit_dec(e, c=c, bank=bank, hspb=hspb, TT=TT):
                last = None
                first = True
                for (woff, hoff) in ((0, 0), (0, 2 * TT), (2048, 0)):
                    for hc in range(HC):
                        last = e.matmul(
                            bank[:, 0:TT],
                            z.wdecsp[:, woff + hc * 1024 + c * 128:
                                     woff + hc * 1024 + c * 128 + 128],
                            hspb[:, hoff + hc * TT: hoff + hc * TT + TT],
                            start=first, stop=(woff == 2048 and hc == HC - 1))
                        first = False
                return last
            ev_dmm = P.op(
                "tensor", emit_dec,
                waits=[ev_hsplit[(i % 2, 0)], ev_hsplit[(i % 2, 1)], my_dec_sp,
                       rc_free[cb4]], sem="t")
            state["last_dec_mm"] = ev_dmm
            ev_rb = P.op(
                "scalar",
                lambda e, n=n, c=c, bank=bank, scrRb=scrRb, TT=TT: e.activation(
                    out=scrRb[:, 0:TT], in_=bank[:, 0:TT], func=AF.Identity,
                    bias=z.bias[:, BDEC0 + n * 8 + c: BDEC0 + n * 8 + c + 1]),
                waits=[ev_dmm, z.ev_bias, ev_d[cb], ev_sq[cb]], sem="s")
            rc_free[cb4] = [ev_rb]
            ev_d1 = P.op(
                "vector",
                lambda e, c=c, t0=t0, scrRb=scrRb, scrDb=scrDb, TT=TT:
                e.tensor_tensor(
                    scrDb[:, 0:TT], scrRb[:, 0:TT],
                    z.xnh[:, c * ttot + t0: c * ttot + t0 + TT], op=AO.subtract),
                waits=[ev_rb, ev_sq[cb]], sem="v")
            ev_d[cb] = P.op(
                "vector",
                lambda e, c=c, t0=t0, scrDb=scrDb, TT=TT: e.tensor_tensor(
                    scrDb[:, 0:TT], scrDb[:, 0:TT],
                    z.xnl[:, c * ttot + t0: c * ttot + t0 + TT], op=AO.subtract),
                waits=[ev_d1], sem="v")
            col = n * (nt * C) + i * C + c
            if pend[0] is not None:
                (pcb, pcol, pRb, pDb, pdev) = pend[0]
                ev_sq[pcb] = P.op(
                    "scalar",
                    lambda e, pRb=pRb, pDb=pDb, pcol=pcol, TT=TT: e.activation(
                        out=pRb[:, 0:TT], in_=pDb[:, 0:TT], func=AF.Square,
                        accum_out=z.slots[:, pcol:pcol + 1]),
                    waits=[pdev, slots_guard], sem="s")
            pend[0] = (cb, col, scrRb, scrDb, ev_d[cb])
        (pcb, pcol, pRb, pDb, pdev) = pend[0]
        ev_sq[pcb] = P.op(
            "scalar",
            lambda e, pRb=pRb, pDb=pDb, pcol=pcol, TT=TT: e.activation(
                out=pRb[:, 0:TT], in_=pDb[:, 0:TT], func=AF.Square,
                accum_out=z.slots[:, pcol:pcol + 1]),
            waits=[pdev, slots_guard], sem="s")
        pend[0] = None

    for n in range(NE):
        b = n % 2
        my_enc_sp = ev_enc_sp[b]
        my_dec_sp = ev_dec_sp
        for step in range(nt + 1):
            if n == NE - 1 and step == nt and tail_hook is not None:
                tail_hook(ev_enc_mm_last[0], state["last_enc_mm"])
            if step < nt:
                enc_tile(n, b, step, my_enc_sp)
            if step >= 1:
                dec_tile(n, step - 1, my_dec_sp)
        ev_enc_mm_last[b] = state["last_enc_mm"]
        ev_dec_mm_last = state["last_dec_mm"]
        if n + 1 < NE:
            load_dec(n + 1)
        if n + 2 < NE:
            load_enc(n + 2)
    last_dec_mm = state["last_dec_mm"]

    z.free["p2"] = hp_free[(0, 0)]
    z.free["p3"] = hp_free[(0, 1)]
    z.free["p6"] = hp_free[(1, 0)]
    z.free["p7"] = hp_free[(1, 1)]
    z.free["p4"] = rc_free[0]
    z.free["p5"] = rc_free[1]
    z.free["p0"] = rc_free[2]
    z.free["p1"] = rc_free[3]
    z.free["hsp0"] = [last_dec_mm]
    z.free["hsp1"] = [last_dec_mm]
    z.free["sc4"] = h32_guard[0]
    z.free["sc5"] = h32_guard[1]
    z.free["wencsp0"] = [ev_enc_mm_last[0]]
    z.free["wencsp1"] = [ev_enc_mm_last[1]]
    z.free["wdecsp"] = [ev_dec_mm_last]
    z.free["sc0"] = [ev_d[0], ev_sq[0]]
    z.free["sc1"] = [ev_d[1], ev_sq[1]]
    z.free["sc2"] = [ev_d[0], ev_sq[0]]
    z.free["sc3"] = [ev_d[1], ev_sq[1]]
    return [ev_sq[0], ev_sq[1]]


def plan_loss_ar_issue(P, z, call, ev_sq_pair):
    nslot = z.nt * C
    ev_red = None
    for n in range(NE):
        ev_red = P.op("vector", lambda e, n=n, nslot=nslot: e.tensor_reduce(
            z.lacc[:, n:n + 1], z.slots[:, n * nslot:(n + 1) * nslot], axis=AX.X,
            op=AO.add), waits=[ev_sq_pair, F(z, "lacc")], sem="v")
    z.free["slots"] = [ev_red]
    ev_cp = P.op("gpsimd", lambda e: e.tensor_reduce(
        z.arst[0:1, 0:8], z.lacc[:, 0:8], axis=AX.C, op=AO.add),
        waits=[ev_red, F(z, "arst")], sem="q")
    z.free["lacc"] = [ev_cp]
    ev_in = P.op("gpsimd", lambda e, call=call: e.dma_start(
        out=z.d_arin[call], in_=z.arst[0:1, 0:8]), waits=[ev_cp], sem="g")
    z.free["arst"] = [ev_in]
    ev_cc = P.op("gpsimd", lambda e, call=call: e.collective_compute(
        "AllReduce", AO.add,
        replica_groups=[[0, 1], [2, 3], [4, 5], [6, 7]],
        ins=[z.d_arin[call].opt()], outs=[z.d_arout[call].opt()]),
        waits=[ev_in], sem="c")
    ev_out = P.op("gpsimd", lambda e, call=call: e.dma_start(
        out=z.lrow[0:1, 0:8], in_=z.d_arout[call]),
        waits=[ev_cc, F(z, "lrow")], sem="g")
    z.ar_evcp = ev_cp
    return ev_out


def plan_mask(P, z, ev_out):
    ev_cp = z.ar_evcp
    rb = z.rowbuf
    ev_min = P.op("vector", lambda e: e.tensor_reduce(
        rb[0:1, 8:9], z.lrow[0:1, 0:8], axis=AX.X, op=AO.min),
        waits=[ev_out, z.ev_rb0, F(z, "rowbuf")], sem="v")
    ev_eq = P.op("vector", lambda e: e.tensor_scalar(
        rb[0:1, 16:24], z.lrow[0:1, 0:8], rb[0:1, 8:9], None, op0=AO.is_le),
        waits=[ev_min], sem="v")
    z.free["lrow"] = [ev_eq]
    ev_run = P.op("vector", lambda e: e.tensor_copy(
        rb[0:1, 24:25], z.consts[0:1, CZERO:CZERO + 1]), waits=[z.ev_con, ev_eq],
        sem="v")
    ev_nr = None
    for n in range(NE):
        ev_nr = P.op("vector", lambda e, n=n: e.tensor_scalar(
            rb[0:1, 56:57], rb[0:1, 24 + n:25 + n], z.consts[0:1, CNEG1:CNEG1 + 1],
            z.consts[0:1, 0:1], op0=AO.mult, op1=AO.add), waits=[ev_run, ev_eq], sem="v")
        ev_mk = P.op("vector", lambda e, n=n: e.tensor_tensor(
            rb[0:1, 33 + n:34 + n], rb[0:1, 16 + n:17 + n], rb[0:1, 56:57],
            op=AO.mult), waits=[ev_nr], sem="v")
        ev_run = P.op("vector", lambda e, n=n: e.tensor_tensor(
            rb[0:1, 25 + n:26 + n], rb[0:1, 24 + n:25 + n], rb[0:1, 33 + n:34 + n],
            op=AO.add), waits=[ev_mk], sem="v")
    ev_adp = None
    for j in range(NE):
        ev_e = P.op("vector", lambda e, j=j: e.tensor_scalar(
            rb[0:1, 48:56], z.cif[0:1, 0:8], z.consts[0:1, CJ0 + j:CJ0 + j + 1], None,
            op0=AO.is_equal), waits=[z.ev_cif, ev_run, ev_adp, ev_nr], sem="v")
        ev_p = P.op("vector", lambda e: e.tensor_tensor(
            rb[0:1, 48:56], rb[0:1, 48:56], rb[0:1, 33:41], op=AO.mult),
            waits=[ev_e], sem="v")
        ev_adp = P.op("vector", lambda e, j=j: e.tensor_reduce(
            rb[0:1, 41 + j:42 + j], rb[0:1, 48:56], axis=AX.X, op=AO.add),
            waits=[ev_p], sem="v")
    z.free["rowbuf"] = [ev_adp]
    z.ar_evadp = ev_adp
    return ev_adp


def plan_mask_bc(P, z):
    rb = z.rowbuf
    ev_bc = P.op("tensor", lambda e: e.matmul(
        z.p[6][:, 0:8], z.consts[0:1, 0:128], rb[0:1, 41:49], start=True, stop=True),
        waits=[z.ar_evadp, z.ar_evcp, F(z, "p6")], sem="t")
    ev_mskb = P.op("vector", lambda e: e.tensor_copy(z.mskb[:, 0:8], z.p[6][:, 0:8]),
                   waits=[ev_bc, F(z, "mskb")], sem="v")
    z.free["p6"] = [ev_mskb]
    return ev_mskb


def plan_select(P, z, ev_mskb):
    gd = F(z, "adapters")
    ev_wd = gd
    ev_wu = gd
    ev_bd = gd
    ev_bu = gd
    sc4_guard = F(z, "sc4")
    sc5_guard = F(z, "sc5")
    hsp0f = z.hsp0[:].bitcast(F32)
    hsp1f = z.hsp1[:].bitcast(F32)
    bufs = [(z.selbuf[:, 0:512], z.selbuf[0:64, 512:1536]),
            (hsp1f[:, 0:512], hsp0f[0:64, 0:1024])]
    ev_free = [F(z, "selbuf"), F(z, "selodd") + F(z, "hsp0") + F(z, "hsp1")]
    ev_loads = {}

    def load(n):
        b = n % 2
        wd_ap, wu_ap = bufs[b]
        sem = "ws" if b == 0 else "ws2"
        P.op("sync", lambda e, n=n, wd_ap=wd_ap: e.dma_start(
            out=wd_ap, in_=z.d_wdown[n]), waits=[ev_free[b]], sem=sem)
        ev_loads[n] = P.op("sync", lambda e, n=n, wu_ap=wu_ap: e.dma_start(
            out=wu_ap, in_=z.d_wup[n]), sem=sem)

    load(0)
    load(1)
    for n in range(NE):
        sb_wd, sb_wu = bufs[n % 2]
        ev_load = ev_loads[n]
        if n == 0:
            ev_wd = P.op("vector", lambda e, sb_wd=sb_wd: e.tensor_scalar(
                z.sc2[:, 0:512], sb_wd, z.mskb[:, 0:1], None, op0=AO.mult),
                waits=[ev_mskb, ev_load, ev_wd, F(z, "sc2")], sem="v")
            ev_wu = P.op("vector", lambda e, sb_wu=sb_wu: e.tensor_scalar(
                z.wupsel[0:64, :], sb_wu, z.mskb[0:64, 0:1], None,
                op0=AO.mult), waits=[ev_mskb, ev_load, ev_wu], sem="v")
            ev_bd = P.op("vector", lambda e: e.tensor_scalar(
                z.bsel[0:64, 0:1], z.bias[0:64, BDOWN0:BDOWN0 + 1],
                z.mskb[0:64, 0:1], None, op0=AO.mult),
                waits=[ev_mskb, z.ev_bias, ev_bd], sem="v")
            ev_bu = P.op("vector", lambda e: e.tensor_scalar(
                z.bsel[:, 2:10], z.bias[:, BUP0:BUP0 + 8],
                z.mskb[:, 0:1], None, op0=AO.mult),
                waits=[ev_mskb, z.ev_bias, ev_bu], sem="v")
        else:
            ev_s1 = P.op("vector", lambda e, sb_wd=sb_wd, n=n: e.tensor_scalar(
                z.sc4[:, 0:512], sb_wd, z.mskb[:, n:n + 1], None, op0=AO.mult),
                waits=[ev_mskb, ev_load, ev_wd, sc4_guard], sem="v")
            ev_wd = P.op("vector", lambda e: e.tensor_tensor(
                z.sc2[:, 0:512], z.sc2[:, 0:512], z.sc4[:, 0:512], op=AO.add),
                waits=[ev_s1], sem="v")
            for half in range(2):
                ev_s2 = P.op("vector", lambda e, sb_wu=sb_wu, n=n, half=half:
                             e.tensor_scalar(
                                 z.sc5[0:64, 0:512],
                                 sb_wu[0:64, half * 512:512 + half * 512],
                                 z.mskb[0:64, n:n + 1], None, op0=AO.mult),
                    waits=[ev_load, ev_wu, sc5_guard], sem="v")
                ev_wu = P.op("vector", lambda e, half=half: e.tensor_tensor(
                    z.wupsel[0:64, half * 512:512 + half * 512],
                    z.wupsel[0:64, half * 512:512 + half * 512],
                    z.sc5[0:64, 0:512], op=AO.add), waits=[ev_s2], sem="v")
            ev_b1 = P.op("vector", lambda e, n=n: e.tensor_scalar(
                z.bsel[0:64, 34:35], z.bias[0:64, BDOWN0 + n:BDOWN0 + n + 1],
                z.mskb[0:64, n:n + 1], None, op0=AO.mult), waits=[ev_bd], sem="v")
            ev_bd = P.op("vector", lambda e: e.tensor_tensor(
                z.bsel[0:64, 0:1], z.bsel[0:64, 0:1],
                z.bsel[0:64, 34:35], op=AO.add), waits=[ev_b1], sem="v")
            ev_b2 = P.op("vector", lambda e, n=n: e.tensor_scalar(
                z.bsel[:, 26:34], z.bias[:, BUP0 + n * 8:BUP0 + n * 8 + 8],
                z.mskb[:, n:n + 1], None, op0=AO.mult), waits=[ev_bu], sem="v")
            ev_bu = P.op("vector", lambda e: e.tensor_tensor(
                z.bsel[:, 2:10], z.bsel[:, 2:10],
                z.bsel[:, 26:34], op=AO.add), waits=[ev_b2], sem="v")
        ev_free[n % 2] = [ev_wu]
        if n + 2 < NE:
            load(n + 2)
    ev_bush = P.op("vector", lambda e: e.tensor_scalar(
        z.bsel[:, 18:26], z.bsel[:, 2:10],
        z.consts[:, CHALF:CHALF + 1], None, op0=AO.mult), waits=[ev_bu], sem="v")
    ev_wdh = P.op("vector", lambda e: e.tensor_copy(z.wdselh[:], z.sc2[:, 0:512]),
                  waits=[ev_wd, F(z, "wdselsp")], sem="v")
    ev_wdl = P.op("vector", lambda e: e.tensor_tensor(
        z.wdsell[:], z.sc2[:, 0:512], z.wdselh[:], op=AO.subtract),
        waits=[ev_wdh], sem="v")
    ev_wuh = P.op("vector", lambda e: e.tensor_copy(
        z.wupselh[0:64, :], z.wupsel[0:64, :]), waits=[ev_wu, F(z, "wupselsp")], sem="v")
    ev_wul = P.op("vector", lambda e: e.tensor_tensor(
        z.wupsell[0:64, :], z.wupsel[0:64, :], z.wupselh[0:64, :], op=AO.subtract),
        waits=[ev_wuh], sem="v")
    z.free["selbuf"] = ev_free[0]
    z.free["selodd"] = ev_free[1]
    z.free["hsp0"] = F(z, "hsp0") + ev_free[1]
    z.free["hsp1"] = F(z, "hsp1") + ev_free[1]
    z.free["sc2"] = [ev_wdl]
    z.free["sc4"] = [ev_wdh]
    z.free["sc5"] = [ev_wul]
    return ev_wdl, ev_wul, ev_bd, ev_bush


def plan_base(P, z, ev_xn, pre=None):
    ttot, nt = z.ttot, z.nt
    TT = ttot // nt
    ev_sp = [None, None]
    sp_guard = [F(z, "wencsp0"), F(z, "wencsp1")]
    ev_mm_free = [None, None]
    quarters = [(i, q) for i in range(nt) for q in range(4)]

    def load(k):
        i, q = quarters[k]
        b = k % 2
        ev_sp[b] = plan_wstream(
            P, z, z.d_wbase[:, q * 2048:(q + 1) * 2048], z.wencsp[b],
            [sp_guard[b], ev_mm_free[b]])
        sp_guard[b] = None

    if pre is not None:
        ev_sp[0], ev_sp[1] = pre
        sp_guard = [None, None]
    else:
        load(0)
        load(1)
    ev_evac = {0: F(z, "p2"), 1: F(z, "p3"), 2: F(z, "p4"), 3: F(z, "p5")}
    ev_add = {0: F(z, "sc0"), 1: F(z, "sc1")}
    ev_last_add = None
    tile_adds = []
    for k, (i, q) in enumerate(quarters):
        b = k % 2
        t0 = i * TT
        for eo2 in range(2):
            eo = q * 2 + eo2
            pb = 2 + 2 * (q % 2) + eo2

            def emit(e, b=b, eo2=eo2, t0=t0, pb=pb, TT=TT):
                return _mm3(
                    e, z.p[pb][:, 0:TT], z.wencsp[b], 0, 2048,
                    lambda c, eo2=eo2: (c * 256 + eo2 * 128, c * 256 + eo2 * 128 + 128),
                    z.xnh, z.xnl,
                    lambda c, t0=t0: (c * ttot + t0, c * ttot + t0 + TT),
                    first=True)
            ev_mm = P.op("tensor", emit,
                         waits=[ev_xn[i], ev_sp[b], ev_evac[pb - 2]], sem="t")
            if eo2 == 1:
                ev_mm_free[b] = ev_mm
            scr = (z.sc0, z.sc1)[eo % 2]
            ev_evac[pb - 2] = P.op(
                "scalar", lambda e, eo=eo, pb=pb, scr=scr, TT=TT: e.activation(
                    out=scr[:, 0:TT], in_=z.p[pb][:, 0:TT], func=AF.Identity,
                    scale=z.consts[:, CHALF:CHALF + 1],
                    bias=z.halfbase[:, eo:eo + 1]),
                waits=[ev_mm, z.ev_halfbase, ev_add[eo % 2]], sem="s")
            off = eo * ttot + t0
            ev_add[eo % 2] = P.op(
                "gpsimd", lambda e, off=off, scr=scr, pb=pb, TT=TT: e.tensor_tensor(
                    z.x[:, off:off + TT], z.x[:, off:off + TT], scr[:, 0:TT],
                    op=AO.add), waits=[ev_evac[pb - 2], ev_add[eo % 2]], sem="q")
            ev_last_add = ev_add[eo % 2]
        if q == 3:
            tile_adds.append([ev_add[0], ev_add[1]])
        if k + 2 < len(quarters):
            load(k + 2)
    z.free["p2"], z.free["p3"] = [ev_evac[0]], [ev_evac[1]]
    z.free["p4"], z.free["p5"] = [ev_evac[2]], [ev_evac[3]]
    z.free["wencsp0"] = [ev_mm_free[0]]
    z.free["wencsp1"] = [ev_mm_free[1]]
    z.free["sc0"] = [ev_add[0]]
    z.free["sc1"] = [ev_add[1]]
    return ev_last_add, [ev_add[0], ev_add[1]], tile_adds


def plan_adapter(P, z, ev_xn, ev_sel):
    ttot, nt = z.ttot, z.nt
    TT = ttot // nt
    ev_wdl, ev_wul, ev_bd, ev_bush = ev_sel
    mid_guard = F(z, "sc4")
    ubank = (z.p[3], z.p[4], z.p[5])
    mbank = (z.p[2], z.p[6])
    msph = (z.msph[:, :], z.hsp0[0:64, 0:512])
    mspl = (z.mspl[:, :], z.hsp0[0:64, 512:1024])
    ev_up_evac = {0: F(z, "p3"), 1: F(z, "p4"), 2: F(z, "p5")}
    ev_add = {0: F(z, "sc2"), 1: F(z, "sc3")}
    mid_free = {0: F(z, "p2"), 1: F(z, "p6")}
    msp_free = {0: F(z, "msp"), 1: F(z, "msp") + F(z, "hsp0") + F(z, "selodd")}
    ev_msp_par = {}
    st = {"up_last": None, "mid_relu": None, "mid": None, "msp": None,
          "last": None}
    tile_adds = []

    def mid_part(i):
        t0 = i * TT
        par = i % 2
        mb = mbank[par]

        def emit_mid(e, t0=t0, TT=TT, mb=mb):
            last = None
            first = True
            for (wt, xt) in ((z.wdselh, z.xnh), (z.wdselh, z.xnl), (z.wdsell, z.xnh)):
                for c in range(C):
                    last = e.matmul(
                        mb[0:64, 0:TT], wt[:, c * 64:(c + 1) * 64],
                        xt[:, c * ttot + t0: c * ttot + t0 + TT],
                        start=first, stop=(wt is z.wdsell and c == C - 1))
                    first = False
            return last
        ev_mid = P.op("tensor", emit_mid,
                      waits=[ev_xn[i], ev_wdl, mid_free[par]], sem="t")
        st["mid"] = ev_mid
        ev_mid_relu = P.op("scalar", lambda e, TT=TT, mb=mb: e.activation(
            out=z.sc4[0:64, 0:TT], in_=mb[0:64, 0:TT], func=AF.Relu,
            bias=z.bsel[0:64, 0:1]),
            waits=[ev_mid, ev_bd, mid_guard, st["msp"]], sem="s")
        st["mid_relu"] = ev_mid_relu
        mid_free[par] = [ev_mid_relu]
        ev_mh = P.op("vector", lambda e, TT=TT, par=par: e.tensor_copy(
            msph[par][0:64, 0:TT], z.sc4[0:64, 0:TT]),
            waits=[ev_mid_relu, msp_free[par]], sem="v")
        ev_msp = P.op("vector", lambda e, TT=TT, par=par: e.tensor_tensor(
            mspl[par][0:64, 0:TT], z.sc4[0:64, 0:TT], msph[par][0:64, 0:TT],
            op=AO.subtract), waits=[ev_mh], sem="v")
        st["msp"] = ev_msp
        ev_msp_par[par] = ev_msp
        mid_guard[:] = [ev_msp]

    def up_part(i):
        t0 = i * TT
        par = i % 2
        for eo in range(C):
            ub = eo % 3
            bank = ubank[ub]

            def emit_up(e, eo=eo, bank=bank, TT=TT, par=par):
                last = None
                first = True
                for (wt, mt) in ((z.wupselh, msph[par]), (z.wupselh, mspl[par]),
                                 (z.wupsell, msph[par])):
                    last = e.matmul(
                        bank[:, 0:TT], wt[0:64, eo * 128:(eo + 1) * 128],
                        mt[0:64, 0:TT], start=first,
                        stop=(wt is z.wupsell))
                    first = False
                return last
            ev_up = P.op("tensor", emit_up,
                         waits=[ev_msp_par[par], ev_wul, ev_up_evac[ub]], sem="t")
            st["up_last"] = ev_up
            if eo == C - 1:
                msp_free[par] = [ev_up]
            scr = (z.sc2, z.sc3)[eo % 2]
            ev_evac_cur = P.op(
                "scalar", lambda e, eo=eo, bank=bank, scr=scr, TT=TT: e.activation(
                    out=scr[:, 0:TT], in_=bank[:, 0:TT], func=AF.Identity,
                    scale=z.consts[:, CHALF:CHALF + 1],
                    bias=z.bsel[:, 18 + eo:19 + eo]),
                waits=[ev_up, ev_bush, ev_add[eo % 2]], sem="s")
            ev_up_evac[ub] = ev_evac_cur
            off = eo * ttot + t0
            ev_add[eo % 2] = P.op(
                "gpsimd", lambda e, off=off, scr=scr, TT=TT: e.tensor_tensor(
                    z.x[:, off:off + TT], z.x[:, off:off + TT], scr[:, 0:TT],
                    op=AO.add), waits=[ev_evac_cur, ev_add[eo % 2]], sem="q")
            st["last"] = ev_add[eo % 2]
        tile_adds.append([ev_add[0], ev_add[1]])

    mid_guard = list(mid_guard)
    for step in range(nt + 1):
        if step < nt:
            mid_part(step)
        if step >= 1:
            up_part(step - 1)
    z.free["p2"] = mid_free[0]
    z.free["p6"] = mid_free[1]
    z.free["p3"] = [ev_up_evac[0]]
    z.free["p4"] = [ev_up_evac[1]]
    z.free["p5"] = [ev_up_evac[2]]
    z.free["sc4"] = [st["msp"]]
    z.free["msp"] = msp_free[0] + msp_free[1]
    z.free["hsp0"] = F(z, "hsp0") + msp_free[1] + [ev_msp_par.get(1)]
    z.free["adapters"] = [st["up_last"], ev_up_evac[0], ev_up_evac[1],
                          st["mid_relu"]]
    z.free["wdselsp"] = [st["mid"]]
    z.free["wupselsp"] = [st["up_last"]]
    z.free["sc2"] = [ev_add[0]]
    z.free["sc3"] = [ev_add[1]]
    return st["last"], [ev_add[0], ev_add[1]], tile_adds


def plan_w23(P, z):
    ev_w2 = P.op("sync", lambda e: e.dma_start(out=z.wtrans[:, 0:128], in_=z.d_w2),
                 waits=[F(z, "wtrans")], sem="we0")
    ev_w3 = P.op("sync", lambda e: e.dma_start(out=z.wtrans[:, 128:256], in_=z.d_w3),
                 sem="we0")
    ev_w23h = P.op("vector", lambda e: e.tensor_copy(
        z.w23sp[:, 0:256], z.wtrans[:, 0:256]),
        waits=[ev_w2, ev_w3, F(z, "w23sp")], sem="v")
    ev_w23l = P.op("vector", lambda e: e.tensor_tensor(
        z.w23sp[:, 256:512], z.wtrans[:, 0:256], z.w23sp[:, 0:256], op=AO.subtract),
        waits=[ev_w23h], sem="v")
    z.free["wtrans"] = [ev_w23l]
    return ev_w23l


def plan_mixer(P, z, ev_xn, pre=None):
    ttot, nt = z.ttot, z.nt
    TT = ttot // nt
    ev_sp = [None, None]
    sp_guard = [F(z, "wencsp0"), F(z, "wencsp1")]
    ev_mm_free = [None, None]
    quarters = [(i, q) for i in range(nt) for q in range(4)]

    def load(k):
        i, q = quarters[k]
        b = k % 2
        ev_sp[b] = plan_wstream(
            P, z, z.d_w1[:, q * 2048:(q + 1) * 2048], z.wencsp[b],
            [sp_guard[b], ev_mm_free[b]])
        sp_guard[b] = None

    ev_w2 = P.op("sync", lambda e: e.dma_start(out=z.wtrans[:, 0:128], in_=z.d_w2),
                 waits=[F(z, "wtrans")], sem="we0")
    ev_w3 = P.op("sync", lambda e: e.dma_start(out=z.wtrans[:, 128:256], in_=z.d_w3),
                 sem="we0")
    ev_w23h = P.op("vector", lambda e: e.tensor_copy(
        z.w23sp[:, 0:256], z.wtrans[:, 0:256]),
        waits=[ev_w2, ev_w3, F(z, "w23sp")], sem="v")
    ev_w23l = P.op("vector", lambda e: e.tensor_tensor(
        z.w23sp[:, 256:512], z.wtrans[:, 0:256], z.w23sp[:, 0:256], op=AO.subtract),
        waits=[ev_w23h], sem="v")
    z.free["wtrans"] = [ev_w23l]

    load(0)
    load(1)

    ev_BCe = {0: F(z, "p4"), 1: F(z, "p5")}
    ev_sbc_used = F(z, "p6")
    ev_sbc = None
    ev_sbc_prev = None
    ev_evac = {0: F(z, "p2"), 1: F(z, "p3")}
    ev_u0_guard = {0: F(z, "sc0"), 1: F(z, "sc1")}
    ev_dd_guard = {0: F(z, "sc2"), 1: F(z, "sc3")}
    sc4_guard = F(z, "sc4")
    sc5_guard = F(z, "sc5")
    ev_BCp_prev = None
    ev_Bm = ev_Cm = None
    ev_last = None
    tile_adds = []
    for k, (i, q) in enumerate(quarters):
        b = k % 2
        t0 = i * TT
        if q == 0:
            def emit_B(e, t0=t0, TT=TT):
                return _mm3(
                    e, z.p[4][0:16, 0:TT], z.w23sp, 0, 256,
                    lambda c: (c * 16, (c + 1) * 16),
                    z.xnh, z.xnl,
                    lambda c, t0=t0: (c * ttot + t0, c * ttot + t0 + TT),
                    first=True)

            def emit_Cm(e, t0=t0, TT=TT):
                return _mm3(
                    e, z.p[5][0:16, 0:TT], z.w23sp, 128, 384,
                    lambda c: (c * 16, (c + 1) * 16),
                    z.xnh, z.xnl,
                    lambda c, t0=t0: (c * ttot + t0, c * ttot + t0 + TT),
                    first=True)
            ev_Bm = P.op("tensor", emit_B, waits=[ev_xn[i], ev_w23l, ev_BCe[0]],
                         sem="t")
            ev_Cm = P.op("tensor", emit_Cm, waits=[ev_w23l, ev_BCe[1]], sem="t")
            ev_Be = P.op("scalar", lambda e, TT=TT: e.activation(
                out=z.sc4[0:16, 0:TT], in_=z.p[4][0:16, 0:TT], func=AF.Identity,
                bias=z.bias[0:16, B2_0:B2_0 + 1]),
                waits=[ev_Bm, z.ev_bias, ev_sbc_prev, sc4_guard], sem="s")
            ev_Ce = P.op("scalar", lambda e, TT=TT: e.activation(
                out=z.sc5[0:16, 0:TT], in_=z.p[5][0:16, 0:TT], func=AF.Identity,
                bias=z.bias[0:16, B3_0:B3_0 + 1]),
                waits=[ev_Cm, z.ev_bias, ev_BCp_prev, sc5_guard], sem="s")
            ev_BCe[0], ev_BCe[1] = [ev_Be], [ev_Ce]
            ev_BCp = P.op("vector", lambda e, TT=TT: e.tensor_tensor(
                z.sc4[0:16, 0:TT], z.sc4[0:16, 0:TT], z.sc5[0:16, 0:TT], op=AO.mult),
                waits=[ev_Be, ev_Ce], sem="v")
            ev_BCp_prev = ev_BCp
            ev_sbc = P.op("tensor", lambda e, TT=TT: e.matmul(
                z.p[6][:, 0:TT], z.consts[0:16, 0:128], z.sc4[0:16, 0:TT],
                start=True, stop=True), waits=[ev_BCp, ev_sbc_used], sem="t")
            ev_sbc_prev = ev_sbc
        for eo2 in range(2):
            eo = q * 2 + eo2
            pb = 2 + eo2

            def emit_w1(e, b=b, eo2=eo2, t0=t0, pb=pb, TT=TT):
                return _mm3(
                    e, z.p[pb][:, 0:TT], z.wencsp[b], 0, 2048,
                    lambda c, eo2=eo2: (c * 256 + eo2 * 128, c * 256 + eo2 * 128 + 128),
                    z.xnh, z.xnl,
                    lambda c, t0=t0: (c * ttot + t0, c * ttot + t0 + TT),
                    first=True)
            ev_mm = P.op("tensor", emit_w1,
                         waits=[ev_xn[i], ev_sp[b], ev_evac[eo2]], sem="t")
            if eo2 == 1:
                ev_mm_free[b] = ev_mm
            u0 = (z.sc0, z.sc1)[eo2]
            dd = (z.sc2, z.sc3)[eo2]
            ev_u0 = P.op("scalar", lambda e, eo=eo, pb=pb, u0=u0, TT=TT: e.activation(
                out=u0[:, 0:TT], in_=z.p[pb][:, 0:TT], func=AF.Identity,
                bias=z.bias[:, B1_0 + eo:B1_0 + eo + 1]),
                waits=[ev_mm, z.ev_bias, ev_u0_guard[eo2]], sem="s")
            ev_evac[eo2] = [ev_u0]
            ev_uc = P.op("vector", lambda e, u0=u0, dd=dd, TT=TT: e.tensor_scalar(
                dd[:, 0:TT], u0[:, 0:TT], z.consts[:, C30:C30 + 1], None, op0=AO.min),
                waits=[ev_u0, ev_dd_guard[eo2]], sem="v")
            ev_ex = P.op("scalar", lambda e, dd=dd, TT=TT: e.activation(
                out=dd[:, 0:TT], in_=dd[:, 0:TT], func=AF.Exp), waits=[ev_uc], sem="s")
            ev_ln = P.op("scalar", lambda e, dd=dd, TT=TT: e.activation(
                out=dd[:, 0:TT], in_=dd[:, 0:TT], func=AF.Ln,
                bias=z.consts[:, 0:1]), waits=[ev_ex], sem="s")
            ev_dl = P.op("vector", lambda e, u0=u0, dd=dd, TT=TT: e.tensor_tensor(
                dd[:, 0:TT], u0[:, 0:TT], dd[:, 0:TT], op=AO.max),
                waits=[ev_ln], sem="v")
            off = eo * ttot + t0
            ev_xnf = P.op("vector", lambda e, off=off, u0=u0, TT=TT: e.tensor_tensor(
                u0[:, 0:TT], z.xnh[:, off:off + TT], z.xnl[:, off:off + TT],
                op=AO.add), waits=[ev_dl], sem="v")
            ev_p1 = P.op("vector", lambda e, u0=u0, dd=dd, TT=TT: e.tensor_tensor(
                dd[:, 0:TT], u0[:, 0:TT], dd[:, 0:TT], op=AO.mult),
                waits=[ev_xnf], sem="v")
            ev_u0_guard[eo2] = [ev_p1]
            ev_p2 = P.op("vector", lambda e, dd=dd, TT=TT: e.tensor_tensor(
                dd[:, 0:TT], dd[:, 0:TT], z.p[6][:, 0:TT], op=AO.mult),
                waits=[ev_p1, ev_sbc], sem="v")
            ev_add = P.op("vector", lambda e, off=off, dd=dd, TT=TT: e.tensor_tensor(
                z.x[:, off:off + TT], z.x[:, off:off + TT], dd[:, 0:TT], op=AO.add),
                waits=[ev_p2], sem="v")
            ev_dd_guard[eo2] = [ev_add]
            ev_last = ev_add
            if eo == 7:
                ev_sbc_used = [ev_p2]
        if q == 3:
            tile_adds.append([ev_dd_guard[0], ev_dd_guard[1]])
        if k + 2 < len(quarters):
            load(k + 2)
    z.free["p2"], z.free["p3"] = ev_evac[0], ev_evac[1]
    z.free["p4"], z.free["p5"] = ev_BCe[0], ev_BCe[1]
    z.free["p6"] = ev_sbc_used
    z.free["wencsp0"] = [ev_mm_free[0]]
    z.free["wencsp1"] = [ev_mm_free[1]]
    z.free["w23sp"] = [ev_Bm, ev_Cm]
    z.free["sc0"] = ev_u0_guard[0]
    z.free["sc1"] = ev_u0_guard[1]
    z.free["sc2"] = ev_dd_guard[0]
    z.free["sc3"] = ev_dd_guard[1]
    z.free["sc4"] = [ev_sbc]
    z.free["sc5"] = [ev_BCp_prev]
    return ev_last, tile_adds



def build_core_program(ttot=2048, nt=4, dbg=False):
    TT = ttot // nt
    assert TT <= 512
    nc = bass.Bass("TRN2", target_bir_lowering=False)
    z = Ctx()
    z.ttot, z.nt = ttot, nt
    z.free = {}
    z.d_x = nc.dram_tensor("x", [128, C * ttot], F32, kind="ExternalInput").ap()
    z.d_wenc = nc.dram_tensor("wenc", [NE, 128, 2048], F32, kind="ExternalInput").ap()
    z.d_wdec = nc.dram_tensor("wdec", [NE, 128, 2048], F32, kind="ExternalInput").ap()
    z.d_wbase = nc.dram_tensor("wbase", [128, 8192], F32, kind="ExternalInput").ap()
    z.d_w1 = nc.dram_tensor("w1", [128, 8192], F32, kind="ExternalInput").ap()
    z.d_w2 = nc.dram_tensor("w2", [128, 128], F32, kind="ExternalInput").ap()
    z.d_w3 = nc.dram_tensor("w3", [128, 128], F32, kind="ExternalInput").ap()
    z.d_wdown = nc.dram_tensor("wdown", [NE, 128, 512], F32, kind="ExternalInput").ap()
    z.d_wup = nc.dram_tensor("wup", [NE, 64, 1024], F32, kind="ExternalInput").ap()
    z.d_biases = nc.dram_tensor("biases", [128, NBIAS], F32, kind="ExternalInput").ap()
    z.d_consts = nc.dram_tensor("consts", [128, NCONST], F32, kind="ExternalInput").ap()
    z.d_ci = nc.dram_tensor("ci", [1, 8], I32, kind="ExternalInput").ap()
    z.d_y = nc.dram_tensor("y", [128, C * ttot], F32, kind="ExternalOutput").ap()
    z.d_arin = [nc.dram_tensor(f"arin{k}", [1, 8], F32).ap() for k in range(2)]
    z.d_arout = [nc.dram_tensor(f"arout{k}", [1, 8], F32).ap() for k in range(2)]
    if dbg:
        z.d_dbg_xn = nc.dram_tensor("dbg_xn", [128, C * ttot], BF16,
                                    kind="ExternalOutput").ap()
        z.d_dbg_xnl = nc.dram_tensor("dbg_xnl", [128, C * ttot], BF16,
                                     kind="ExternalOutput").ap()
        z.d_dbg_row = nc.dram_tensor("dbg_row", [1, 64], F32, kind="ExternalOutput").ap()
        z.d_dbg_lacc = nc.dram_tensor("dbg_lacc", [128, 8], F32,
                                      kind="ExternalOutput").ap()
        z.d_dbg_sel = nc.dram_tensor("dbg_sel", [128, 512], BF16,
                                     kind="ExternalOutput").ap()
        z.d_dbg_x1 = nc.dram_tensor("dbg_x1", [128, C * ttot], F32,
                                    kind="ExternalOutput").ap()
        z.d_dbg_x2 = nc.dram_tensor("dbg_x2", [128, C * ttot], F32,
                                    kind="ExternalOutput").ap()

    es = contextlib.ExitStack()
    sb = lambda name, shape, dt=F32: es.enter_context(nc.sbuf_tensor(name, shape, dt))
    with es:
        z.consts = sb("consts_sb", [128, NCONST])
        z.bias = sb("bias_sb", [128, NBIAS])
        z.halfbase = sb("halfbase", [128, 8])
        z.ci = sb("ci_sb", [1, 8], I32)
        z.cif = sb("cif", [1, 8])
        z.x = sb("x_sb", [128, C * ttot])
        z.xnh = sb("xnh", [128, C * ttot], BF16)
        z.xnl = sb("xnl", [128, C * ttot], BF16)
        z.wtrans = sb("wtrans", [128, 2048])
        z.wencsp = [sb("wencsp0", [128, 4096], BF16), sb("wencsp1", [128, 4096], BF16)]
        z.wdecsp = sb("wdecsp", [128, 4096], BF16)
        z.hsp0 = sb("hsp0", [128, 4 * TT], BF16)
        z.hsp1 = sb("hsp1", [128, 4 * TT], BF16)
        z.scrM = sb("scrM", [128, TT])
        z.scrR = sb("scrR", [128, TT])
        z.sc0 = sb("sc0", [128, 512])
        z.sc1 = sb("sc1", [128, 512])
        z.sc2 = sb("sc2", [128, 512])
        z.sc3 = sb("sc3", [128, 512])
        z.sc4 = sb("sc4", [128, 512])
        z.sc5 = sb("sc5", [128, 512])
        z.slots = sb("slots", [128, NE * nt * C])
        z.lacc = sb("lacc", [128, 8])
        z.rowbuf = sb("rowbuf", [1, 64])
        z.arst = sb("arst", [1, 8])
        z.lrow = sb("lrow", [1, 8])
        z.mskb = sb("mskb", [128, 8])
        z.selbuf = sb("selbuf", [128, 1536])
        z.wupsel = sb("wupsel", [64, 1024])
        z.wdselh = sb("wdselh", [128, 512], BF16)
        z.wdsell = sb("wdsell", [128, 512], BF16)
        z.wupselh = sb("wupselh", [64, 1024], BF16)
        z.wupsell = sb("wupsell", [64, 1024], BF16)
        z.msph = sb("msph", [64, 512], BF16)
        z.mspl = sb("mspl", [64, 512], BF16)
        z.bsel = sb("bsel", [128, 35])
        z.w23sp = sb("w23sp", [128, 512], BF16)
        z.p = [es.enter_context(nc.psum_tensor(f"pb{k}", [128, 512], F32))
               for k in range(8)]

        sems = {name: es.enter_context(nc.semaphore(name))
                for name in ("we0", "ws", "ws2", "g", "t", "s", "v", "q", "c")}
        P = Pipe(sems)
        P.chain["g"] = None

        plan_prelude(P, z)
        gate = []
        base_tiles = ad_tiles = None
        base_adds = ad_adds = []
        for call in range(2):
            exp_pre = (
                plan_wstream(P, z, z.d_wenc[0], z.wencsp[0], [F(z, "wencsp0")]),
                plan_wstream(P, z, z.d_wdec[0], z.wdecsp, [F(z, "wdecsp")]),
                plan_wstream(P, z, z.d_wenc[1], z.wencsp[1], [F(z, "wencsp1")]),
            )
            ev_xn = plan_stats_xn(P, z, gate)
            pre_box = {}

            def tail_hook(g0, g1, pre_box=pre_box):
                pre_box["sp0"] = plan_wstream(P, z, z.d_wbase[:, 0:2048],
                                              z.wencsp[0], [g0])
                pre_box["sp1"] = plan_wstream(P, z, z.d_wbase[:, 2048:4096],
                                              z.wencsp[1], [g1])
            ev_sq_pair = plan_experts(P, z, ev_xn, tail_hook, pre=exp_pre)
            ev_lrow = plan_loss_ar_issue(P, z, call, ev_sq_pair)
            plan_mask(P, z, ev_lrow)
            ev_base, base_adds, base_tiles = plan_base(
                P, z, ev_xn, pre=(pre_box["sp0"], pre_box["sp1"]))
            ev_mskb = plan_mask_bc(P, z)
            ev_sel = plan_select(P, z, ev_mskb)
            ev_ad, ad_adds, ad_tiles = plan_adapter(P, z, ev_xn, ev_sel)
            z.free["mskb"] = [ev_sel[3]]
            dbg_evs = []
            if dbg and call == 0:
                dbg_evs.append(P.op(
                    "gpsimd", lambda e: e.dma_start(out=z.d_dbg_xn, in_=z.xnh[:]),
                    waits=[ev_sq_pair], sem="g"))
                dbg_evs.append(P.op(
                    "gpsimd", lambda e: e.dma_start(out=z.d_dbg_xnl, in_=z.xnl[:]),
                    waits=[ev_sq_pair], sem="g"))
                dbg_evs.append(P.op(
                    "gpsimd", lambda e: e.dma_start(out=z.d_dbg_lacc, in_=z.lacc[:]),
                    waits=[z.free["lacc"]], sem="g"))
                dbg_evs.append(P.op(
                    "gpsimd", lambda e: e.dma_start(
                        out=z.d_dbg_row, in_=z.rowbuf[0:1, 0:64]),
                    waits=[ev_mskb], sem="g"))
                dbg_evs.append(P.op(
                    "gpsimd", lambda e: e.dma_start(out=z.d_dbg_sel, in_=z.wdselh[:]),
                    waits=[ev_sel[0]], sem="g"))
                dbg_evs.append(P.op(
                    "gpsimd", lambda e: e.dma_start(out=z.d_dbg_x1, in_=z.x[:]),
                    waits=[base_adds, ad_adds], sem="g"))
            if call == 0:
                mix_pre = (
                    plan_wstream(P, z, z.d_w1[:, 0:2048], z.wencsp[0],
                                 [F(z, "wencsp0")]),
                    plan_wstream(P, z, z.d_w1[:, 2048:4096], z.wencsp[1],
                                 [F(z, "wencsp1")]),
                    plan_w23(P, z),
                )
                gates = [base_tiles[i] + ad_tiles[i] + dbg_evs for i in range(nt)]
                ev_mix, mix_tiles = plan_mixer(P, z, plan_stats_xn(P, z, gates),
                                               pre=mix_pre)
                gate = [mix_tiles[i] for i in range(nt)]
                if dbg:
                    ev_dx2 = P.op(
                        "gpsimd", lambda e: e.dma_start(out=z.d_dbg_x2, in_=z.x[:]),
                        waits=[ev_mix], sem="g")
                    gate = [g + [ev_dx2] for g in gate]
        yd3 = z.d_y.rearrange("p (c t) -> p c t", c=C)
        ys3 = z.x[:].rearrange("p (c t) -> p c t", c=C)
        ev_out = None
        for i in range(nt):
            ev_out = P.op(
                "gpsimd", lambda e, i=i: e.dma_start(
                    out=yd3[:, :, i * TT:(i + 1) * TT],
                    in_=ys3[:, :, i * TT:(i + 1) * TT]),
                waits=[base_tiles[i], ad_tiles[i]], sem="g")
        P.wait_only("gpsimd", [ev_out])
        with nc.Block() as block:
            P.emit_into(block)
    return nc, z


_BUILD_CACHE = {}


def _get_program(ttot=2048, nt=4, dbg=False):
    key = (ttot, nt, dbg)
    if key not in _BUILD_CACHE:
        _BUILD_CACHE[key] = build_core_program(ttot, nt, dbg)
    return _BUILD_CACHE[key]



def _fm(a):
    Dd, T = a.shape
    return np.ascontiguousarray(
        a.reshape(Dd // 128, 128, T).transpose(1, 0, 2).reshape(128, (Dd // 128) * T))


def make_in_maps(inputs, ttot=2048):
    f32 = lambda v: np.asarray(v, dtype=np.float32)
    x = f32(inputs["L0_attn_x"]).reshape(-1, D)
    W1 = f32(inputs["L0_attn_W1"])
    b1 = f32(inputs["L0_attn_b1"])
    W2 = f32(inputs["L0_attn_W2"])
    b2 = f32(inputs["L0_attn_b2"])
    W3 = f32(inputs["L0_attn_W3"])
    b3 = f32(inputs["L0_attn_b3"])
    Wb = f32(inputs["L0_ffn_W_base"])
    bb = f32(inputs["L0_ffn_b_base"])
    We = f32(inputs["L0_ffn_W_enc"])
    be = f32(inputs["L0_ffn_b_enc"])
    Wd = f32(inputs["L0_ffn_W_dec"])
    bd = f32(inputs["L0_ffn_b_dec"])
    Wdn = f32(inputs["L0_ffn_W_down"])
    bdn = f32(inputs["L0_ffn_b_down"])
    Wup = f32(inputs["L0_ffn_W_up"])
    bup = f32(inputs["L0_ffn_b_up"])
    ci = np.asarray(inputs["L0_ffn_connected_idx"], dtype=np.int32).reshape(1, 8)

    wenc = np.stack([_fm(We[n]) for n in range(NE)])
    wdec = np.stack([_fm(Wd[n]) for n in range(NE)])

    def _quarter_layout(W):
        a = W.reshape(C, 128, 2, 2, 2, 128)
        a = a.transpose(1, 2, 3, 0, 4, 5)
        return np.ascontiguousarray(a.reshape(128, 8192))
    wbase = _quarter_layout(Wb)
    w1 = _quarter_layout(W1)
    w2 = _fm(W2)
    w3 = _fm(W3)
    wdown = np.stack([_fm(Wdn[n]) for n in range(NE)])
    wupl = np.ascontiguousarray(Wup)

    biases = np.zeros((128, NBIAS), np.float32)
    biases[:, B1_0:B1_0 + 8] = b1.reshape(C, 128).T
    biases[:, BBASE0:BBASE0 + 8] = bb.reshape(C, 128).T
    biases[0:16, B2_0] = b2
    biases[0:16, B3_0] = b3
    for n in range(NE):
        biases[:, BENC0 + n * 2:BENC0 + n * 2 + 2] = be[n].reshape(2, 128).T
        biases[:, BDEC0 + n * 8:BDEC0 + n * 8 + 8] = bd[n].reshape(C, 128).T
        biases[0:64, BDOWN0 + n] = bdn[n]
        biases[:, BUP0 + n * 8:BUP0 + n * 8 + 8] = bup[n].reshape(C, 128).T

    consts = np.zeros((128, NCONST), np.float32)
    consts[:, 0:128] = 1.0
    consts[:, CEPS] = 1e-5
    consts[:, CINV] = 1.0 / 1024.0
    consts[:, C30] = 30.0
    consts[:, CHALF] = 0.5
    consts[:, CZERO] = 0.0
    consts[:, CNEG1] = -1.0
    for j in range(8):
        consts[:, CJ0 + j] = float(j)
    consts[:, CM05] = -0.5

    shared = dict(wenc=wenc, wdec=wdec, wbase=wbase, w1=w1, w2=w2, w3=w3,
                  wdown=wdown, wup=wupl, biases=biases, consts=consts, ci=ci)
    in_maps = []
    for core in range(NCORES):
        xc = x[core * ttot:(core + 1) * ttot].T
        m = dict(shared)
        m["x"] = _fm(xc)
        in_maps.append(m)
    return in_maps


def assemble_out(results, ttot=2048):
    parts = []
    for r in results:
        y = np.asarray(r["y"])
        a = y.reshape(128, C, ttot).transpose(1, 0, 2).reshape(D, ttot).T
        parts.append(a)
    out = np.concatenate(parts, axis=0)
    return np.ascontiguousarray(out.reshape(4, 4096, D), dtype=np.float32)


def kernel(**inputs):
    ttot = 2048
    nc, _ = _get_program(ttot=ttot, nt=4, dbg=False)
    in_maps = make_in_maps(inputs, ttot=ttot)
    res = run_bass_kernel_spmd(nc, in_maps, core_ids=list(range(NCORES)))
    return assemble_out(res.results, ttot=ttot)


# version 19
# speedup vs baseline: 1.0752x; 1.0060x over previous
import contextlib
import numpy as np
import concourse.bass as bass
import concourse.mybir as mybir
from concourse.bass_utils import run_bass_kernel_spmd
from concourse.alu_op_type import AluOpType as AO

F32 = mybir.dt.float32
BF16 = mybir.dt.bfloat16
I32 = mybir.dt.int32
AF = mybir.ActivationFunctionType
AX = mybir.AxisListType

D = 1024
C = 8
NE = 8
HC = 2
NCORES = 8

CEPS, CINV, C30, CHALF, CZERO, CNEG1, CJ0, CM05 = 128, 129, 130, 131, 132, 133, 134, 142
NCONST = 144

B1_0, BBASE0, B2_0, B3_0, BENC0, BDEC0, BDOWN0, BUP0 = 0, 8, 16, 17, 18, 34, 98, 106
NBIAS = 176


class Pipe:
    ENGS = ("sync", "gpsimd", "tensor", "scalar", "vector")
    COMPUTE = ("t", "s", "v", "q")

    def __init__(self, sems):
        self.sems = sems
        self.ops = []
        self.cnt = {k: 0 for k in sems}
        self.chain = {}

    def is_dma(self, sem):
        return sem is not None and sem not in self.COMPUTE and sem != "c"

    def _flat(self, waits):
        for w in waits:
            if w is None:
                continue
            if isinstance(w, list):
                yield from self._flat(w)
            else:
                yield w

    def op(self, eng, emit, waits=(), sem=None, n=1):
        waits = list(waits)
        if sem in self.chain and self.chain[sem] is not None:
            waits.append(self.chain[sem])
        wmax = {}
        for (s, v, peng) in self._flat(waits):
            if v <= 0:
                continue
            assert v <= self.cnt[s], f"wait on future event {s}={v} (cnt={self.cnt[s]})"
            if v > wmax.get(s, 0):
                wmax[s] = v
        ws = list(wmax.items())
        post = None
        if sem is not None:
            self.cnt[sem] += 16 * n if self.is_dma(sem) else n
            post = (sem, self.cnt[sem], eng)
            if sem in self.chain:
                self.chain[sem] = post
        self.ops.append((eng, tuple(ws), emit, sem, n))
        return post

    def wait_only(self, eng, waits):
        self.op(eng, None, waits=waits)

    def emit_into(self, block):
        for eng_name in self.ENGS:
            def mk(eng_name=eng_name):
                def f(eng):
                    lw = {k: 0 for k in self.sems}
                    for (e, ws, emit, sem, n) in self.ops:
                        if e != eng_name:
                            continue
                        for (s, v) in ws:
                            if v > lw[s]:
                                eng.wait_ge(self.sems[s], v)
                                lw[s] = v
                        if emit is None:
                            continue
                        last = emit(eng)
                        if sem is not None:
                            last.then_inc(self.sems[sem],
                                          16 * n if self.is_dma(sem) else n)
                return f
            getattr(block, eng_name)(mk())


class Ctx:
    pass


def F(z, key):
    return z.free.get(key, [])



def plan_prelude(P, z):
    e_con = P.op("gpsimd", lambda e: e.dma_start(out=z.consts[:], in_=z.d_consts), sem="g")
    e_bias = P.op("gpsimd", lambda e: e.dma_start(out=z.bias[:], in_=z.d_biases), sem="g")
    e_ci = P.op("gpsimd", lambda e: e.dma_start(out=z.ci[:], in_=z.d_ci), sem="g")
    nt, ttot = z.nt, z.ttot
    TT = ttot // nt
    xd3 = z.d_x.rearrange("p (c t) -> p c t", c=C)
    xs3 = z.x[:].rearrange("p (c t) -> p c t", c=C)
    z.ev_xtile = []
    for i in range(nt):
        z.ev_xtile.append(P.op(
            "gpsimd", lambda e, i=i: e.dma_start(
                out=xs3[:, :, i * TT:(i + 1) * TT], in_=xd3[:, :, i * TT:(i + 1) * TT]),
            sem="g"))
    z.ev_x = z.ev_xtile[-1]
    z.ev_con = e_con
    z.ev_bias = e_bias
    z.ev_cif = P.op("vector", lambda e: e.tensor_copy(z.cif[:], z.ci[:]),
                    waits=[e_ci], sem="v")
    z.ev_halfbase = P.op(
        "vector", lambda e: e.tensor_scalar(
            z.halfbase[:, 0:8], z.bias[:, BBASE0:BBASE0 + 8],
            z.consts[:, CHALF:CHALF + 1], None, op0=AO.mult),
        waits=[e_bias, e_con], sem="v")
    z.ev_rb0 = P.op(
        "vector", lambda e: e.tensor_scalar(
            z.rowbuf[0:1, 0:64], z.consts[0:1, 0:64],
            z.consts[0:1, CZERO:CZERO + 1], None, op0=AO.mult),
        waits=[e_con], sem="v")


def plan_stats_xn(P, z, gate):
    ttot, nt = z.ttot, z.nt
    TT = ttot // nt
    x = z.x
    free_p0 = F(z, "p0")
    free_p1 = F(z, "p1")
    ev_tile = []
    e6 = None
    ev_Q_prev = None
    ev_qmm = {}
    sc3_guard = F(z, "sc3")
    per_tile = bool(gate) and isinstance(gate[0], list) and len(gate) == nt
    gates = gate if per_tile else [gate] * nt
    for i in range(nt):
        gate = gates[i]
        t0 = i * TT

        def emit_S(e, t0=t0, TT=TT):
            last = None
            for c in range(C):
                last = e.matmul(z.p[0][:, 0:TT], z.consts[:, 0:128],
                                x[:, c * ttot + t0: c * ttot + t0 + TT],
                                start=(c == 0), stop=(c == C - 1))
            return last
        ev_S = P.op("tensor", emit_S,
                    waits=[gate, z.ev_xtile[i], z.ev_con, free_p0], sem="t")

        ev_sq = [None, None]
        for c in range(C):
            scr = (z.sc0, z.sc1)[c % 2]
            if c >= 2:
                w_extra = [ev_qmm.get(c % 2)]
            else:
                w_extra = [ev_qmm.get(c % 2), F(z, ("sc0", "sc1")[c % 2]), free_p1]
            ev_sq[c % 2] = P.op(
                "scalar",
                lambda e, c=c, scr=scr, t0=t0, TT=TT: e.activation(
                    out=scr[:, 0:TT], in_=x[:, c * ttot + t0: c * ttot + t0 + TT],
                    func=AF.Square),
                waits=[gate, z.ev_xtile[i]] + w_extra, sem="s")
            ev_qmm[c % 2] = P.op(
                "tensor",
                lambda e, c=c, scr=scr, TT=TT: e.matmul(
                    z.p[1][:, 0:TT], z.consts[:, 0:128], scr[:, 0:TT],
                    start=(c == 0), stop=(c == C - 1)),
                waits=[ev_sq[c % 2]], sem="t")
        ev_Q = ev_qmm[(C - 1) % 2]
        ev_Q_prev = ev_Q

        e1 = P.op("vector", lambda e, TT=TT: e.tensor_scalar(
            z.scrM[:, 0:TT], z.p[0][:, 0:TT], z.consts[:, CINV:CINV + 1], None,
            op0=AO.mult), waits=[ev_S, ev_tile[-1] if ev_tile else None], sem="v")
        e2 = P.op("vector", lambda e, TT=TT: e.tensor_scalar(
            z.sc2[:, 0:TT], z.p[1][:, 0:TT], z.consts[:, CINV:CINV + 1], None,
            op0=AO.mult), waits=[ev_Q, F(z, "sc2")], sem="v")
        free_p0, free_p1 = [e1], [e2]
        e3 = P.op("vector", lambda e, TT=TT: e.tensor_tensor(
            z.scrR[:, 0:TT], z.scrM[:, 0:TT], z.scrM[:, 0:TT], op=AO.mult),
            waits=[e1, e6], sem="v")
        e4 = P.op("vector", lambda e, TT=TT: e.tensor_tensor(
            z.sc2[:, 0:TT], z.sc2[:, 0:TT], z.scrR[:, 0:TT], op=AO.subtract),
            waits=[e3], sem="v")
        e5 = P.op("scalar", lambda e, TT=TT: e.activation(
            out=z.sc3[:, 0:TT], in_=z.sc2[:, 0:TT], func=AF.Ln,
            bias=z.consts[:, CEPS:CEPS + 1]),
            waits=[e4, sc3_guard, ev_tile[-1] if ev_tile else None], sem="s")
        e6 = P.op("scalar", lambda e, TT=TT: e.activation(
            out=z.scrR[:, 0:TT], in_=z.sc3[:, 0:TT], func=AF.Exp,
            scale=z.consts[:, CM05:CM05 + 1]), waits=[e5], sem="s")

        elast = None
        for c in range(C):
            off = c * ttot + t0
            e7 = P.op("vector", lambda e, off=off, TT=TT: e.tensor_tensor(
                z.sc3[:, 0:TT], x[:, off: off + TT], z.scrM[:, 0:TT],
                op=AO.subtract), waits=[e1, elast, e6], sem="v")
            e8 = P.op("vector", lambda e, TT=TT: e.tensor_tensor(
                z.sc3[:, 0:TT], z.sc3[:, 0:TT], z.scrR[:, 0:TT],
                op=AO.mult), waits=[e6, e7], sem="v")
            e9 = P.op("vector", lambda e, off=off, TT=TT: e.tensor_copy(
                z.xnh[:, off: off + TT], z.sc3[:, 0:TT]), waits=[e8], sem="v")
            elast = P.op("vector", lambda e, off=off, TT=TT: e.tensor_tensor(
                z.xnl[:, off: off + TT], z.sc3[:, 0:TT], z.xnh[:, off: off + TT],
                op=AO.subtract), waits=[e9], sem="v")
        ev_tile.append(elast)
    z.free["p0"] = free_p0
    z.free["p1"] = free_p1
    z.free["sc0"] = [ev_qmm[0]]
    z.free["sc1"] = [ev_qmm[1]]
    z.free["sc2"] = [e6]
    z.free["sc3"] = [ev_tile[-1]]
    return ev_tile


def _mm3(e, psum_ap, wsp, whoff, wloff, wslice, xh, xl, xslices, first):
    last = None
    passes = [(whoff, xh), (whoff, xl), (wloff, xh)]
    for pi, (woff, xs) in enumerate(passes):
        for c in range(C):
            w0, w1 = wslice(c)
            x0, x1 = xslices(c)
            last = e.matmul(psum_ap, wsp[:, woff + w0: woff + w1],
                            xs[:, x0:x1],
                            start=(pi == 0 and c == 0 and first),
                            stop=(pi == 2 and c == C - 1))
    return last


def plan_wstream(P, z, dram_src, target, tgt_guard):
    ev_dma = P.op("sync", lambda e: e.dma_start(out=z.wtrans[:], in_=dram_src),
                  waits=[F(z, "wtrans")], sem="we0")
    ev_hi = P.op("vector", lambda e: e.tensor_copy(target[:, 0:2048], z.wtrans[:]),
                 waits=[ev_dma, tgt_guard], sem="v")
    ev_lo = P.op("vector", lambda e: e.tensor_tensor(
        target[:, 2048:4096], z.wtrans[:], target[:, 0:2048], op=AO.subtract),
        waits=[ev_hi], sem="v")
    z.free["wtrans"] = [ev_lo]
    return ev_lo


def plan_experts(P, z, ev_xn, tail_hook=None, pre=None):
    ttot, nt = z.ttot, z.nt
    TT = ttot // nt
    ev_enc_sp = [None, None]
    ev_dec_sp = None
    enc_guard = [F(z, "wencsp0"), F(z, "wencsp1")]
    dec_guard = F(z, "wdecsp")
    ev_enc_mm_last = [None, None]
    ev_dec_mm_last = None

    def load_enc(n):
        b = n % 2
        ev_enc_sp[b] = plan_wstream(P, z, z.d_wenc[n], z.wencsp[b],
                                    [enc_guard[b], ev_enc_mm_last[b]])

    def load_dec(n):
        nonlocal ev_dec_sp
        ev_dec_sp = plan_wstream(P, z, z.d_wdec[n], z.wdecsp,
                                 [dec_guard, ev_dec_mm_last])

    if pre is not None:
        ev_enc_sp[0], ev_dec_sp, ev_enc_sp[1] = pre
    else:
        load_enc(0)
        load_dec(0)
        load_enc(1)

    hp_free = {(0, 0): F(z, "p2"), (0, 1): F(z, "p3"),
               (1, 0): F(z, "p6"), (1, 1): F(z, "p7")}
    rbank = (z.p[4], z.p[5], z.p[0], z.p[1])
    rc_free = {0: F(z, "p4"), 1: F(z, "p5"), 2: F(z, "p0"), 3: F(z, "p1")}
    h32_guard = {0: F(z, "sc4"), 1: F(z, "sc5")}
    hsp_guard = {0: F(z, "hsp0"), 1: F(z, "hsp1")}
    ev_d = {0: F(z, "sc0"), 1: F(z, "sc1")}
    ev_sq = {0: F(z, "sc2"), 1: F(z, "sc3")}
    slots_guard = F(z, "slots")
    ev_hsplit = {}

    state = {"last_enc_mm": None, "last_dec_mm": None}

    def enc_tile(n, b, i, my_enc_sp):
        t0 = i * TT
        hspb = (z.hsp0, z.hsp1)[i % 2]
        pbank = (2, 6)[i % 2]
        for hc in range(HC):
            def emit_enc(e, b=b, hc=hc, t0=t0, pbank=pbank, TT=TT):
                return _mm3(
                    e, z.p[pbank + hc][:, 0:TT], z.wencsp[b], 0, 2048,
                    lambda c, hc=hc: (c * 256 + hc * 128, c * 256 + hc * 128 + 128),
                    z.xnh, z.xnl,
                    lambda c, t0=t0: (c * ttot + t0, c * ttot + t0 + TT),
                    first=True)
            ev_mm = P.op("tensor", emit_enc,
                         waits=[ev_xn[i], my_enc_sp, hp_free[(i % 2, hc)]],
                         sem="t")
            state["last_enc_mm"] = ev_mm
            h32 = (z.sc4, z.sc5)[hc]
            ev_relu = P.op(
                "scalar",
                lambda e, n=n, hc=hc, h32=h32, pbank=pbank, TT=TT: e.activation(
                    out=h32[:, 0:TT], in_=z.p[pbank + hc][:, 0:TT],
                    func=AF.Relu,
                    bias=z.bias[:, BENC0 + n * 2 + hc: BENC0 + n * 2 + hc + 1]),
                waits=[ev_mm, z.ev_bias, h32_guard[hc]], sem="s")
            hp_free[(i % 2, hc)] = [ev_relu]
            ev_hi = P.op("vector", lambda e, hc=hc, h32=h32, hspb=hspb, TT=TT:
                         e.tensor_copy(hspb[:, hc * TT: hc * TT + TT],
                                       h32[:, 0:TT]),
                         waits=[ev_relu, hsp_guard[i % 2]], sem="v")
            ev_lo = P.op("vector", lambda e, hc=hc, h32=h32, hspb=hspb, TT=TT:
                         e.tensor_tensor(
                             hspb[:, 2 * TT + hc * TT: 2 * TT + hc * TT + TT],
                             h32[:, 0:TT], hspb[:, hc * TT: hc * TT + TT],
                             op=AO.subtract),
                         waits=[ev_hi], sem="v")
            h32_guard[hc] = [ev_lo]
            ev_hsplit[(i % 2, hc)] = ev_lo

    pend = [None]

    def dec_tile(n, i, my_dec_sp):
        t0 = i * TT
        hspb = (z.hsp0, z.hsp1)[i % 2]
        for c in range(C):
            cb = c % 2
            cb4 = c % 4
            bank = rbank[cb4]
            scrRb = (z.sc0, z.sc1)[cb]
            scrDb = (z.sc2, z.sc3)[cb]

            def emit_dec(e, c=c, bank=bank, hspb=hspb, TT=TT):
                last = None
                first = True
                for (woff, hoff) in ((0, 0), (0, 2 * TT), (2048, 0)):
                    for hc in range(HC):
                        last = e.matmul(
                            bank[:, 0:TT],
                            z.wdecsp[:, woff + hc * 1024 + c * 128:
                                     woff + hc * 1024 + c * 128 + 128],
                            hspb[:, hoff + hc * TT: hoff + hc * TT + TT],
                            start=first, stop=(woff == 2048 and hc == HC - 1))
                        first = False
                return last
            ev_dmm = P.op(
                "tensor", emit_dec,
                waits=[ev_hsplit[(i % 2, 0)], ev_hsplit[(i % 2, 1)], my_dec_sp,
                       rc_free[cb4]], sem="t")
            state["last_dec_mm"] = ev_dmm
            ev_rb = P.op(
                "scalar",
                lambda e, n=n, c=c, bank=bank, scrRb=scrRb, TT=TT: e.activation(
                    out=scrRb[:, 0:TT], in_=bank[:, 0:TT], func=AF.Identity,
                    bias=z.bias[:, BDEC0 + n * 8 + c: BDEC0 + n * 8 + c + 1]),
                waits=[ev_dmm, z.ev_bias, ev_d[cb], ev_sq[cb]], sem="s")
            rc_free[cb4] = [ev_rb]
            ev_d1 = P.op(
                "vector",
                lambda e, c=c, t0=t0, scrRb=scrRb, scrDb=scrDb, TT=TT:
                e.tensor_tensor(
                    scrDb[:, 0:TT], scrRb[:, 0:TT],
                    z.xnh[:, c * ttot + t0: c * ttot + t0 + TT], op=AO.subtract),
                waits=[ev_rb, ev_sq[cb]], sem="v")
            ev_d[cb] = P.op(
                "vector",
                lambda e, c=c, t0=t0, scrDb=scrDb, TT=TT: e.tensor_tensor(
                    scrDb[:, 0:TT], scrDb[:, 0:TT],
                    z.xnl[:, c * ttot + t0: c * ttot + t0 + TT], op=AO.subtract),
                waits=[ev_d1], sem="v")
            col = n * (nt * C) + i * C + c
            if pend[0] is not None:
                (pcb, pcol, pRb, pDb, pdev) = pend[0]
                ev_sq[pcb] = P.op(
                    "scalar",
                    lambda e, pRb=pRb, pDb=pDb, pcol=pcol, TT=TT: e.activation(
                        out=pRb[:, 0:TT], in_=pDb[:, 0:TT], func=AF.Square,
                        accum_out=z.slots[:, pcol:pcol + 1]),
                    waits=[pdev, slots_guard], sem="s")
            pend[0] = (cb, col, scrRb, scrDb, ev_d[cb])
        (pcb, pcol, pRb, pDb, pdev) = pend[0]
        ev_sq[pcb] = P.op(
            "scalar",
            lambda e, pRb=pRb, pDb=pDb, pcol=pcol, TT=TT: e.activation(
                out=pRb[:, 0:TT], in_=pDb[:, 0:TT], func=AF.Square,
                accum_out=z.slots[:, pcol:pcol + 1]),
            waits=[pdev, slots_guard], sem="s")
        pend[0] = None

    for n in range(NE):
        b = n % 2
        my_enc_sp = ev_enc_sp[b]
        my_dec_sp = ev_dec_sp
        for step in range(nt + 1):
            if n == NE - 1 and step == nt and tail_hook is not None:
                tail_hook(ev_enc_mm_last[0], state["last_enc_mm"])
            if step < nt:
                enc_tile(n, b, step, my_enc_sp)
            if step >= 1:
                dec_tile(n, step - 1, my_dec_sp)
        ev_enc_mm_last[b] = state["last_enc_mm"]
        ev_dec_mm_last = state["last_dec_mm"]
        if n + 1 < NE:
            load_dec(n + 1)
        if n + 2 < NE:
            load_enc(n + 2)
    last_dec_mm = state["last_dec_mm"]

    z.free["p2"] = hp_free[(0, 0)]
    z.free["p3"] = hp_free[(0, 1)]
    z.free["p6"] = hp_free[(1, 0)]
    z.free["p7"] = hp_free[(1, 1)]
    z.free["p4"] = rc_free[0]
    z.free["p5"] = rc_free[1]
    z.free["p0"] = rc_free[2]
    z.free["p1"] = rc_free[3]
    z.free["hsp0"] = [last_dec_mm]
    z.free["hsp1"] = [last_dec_mm]
    z.free["sc4"] = h32_guard[0]
    z.free["sc5"] = h32_guard[1]
    z.free["wencsp0"] = [ev_enc_mm_last[0]]
    z.free["wencsp1"] = [ev_enc_mm_last[1]]
    z.free["wdecsp"] = [ev_dec_mm_last]
    z.free["sc0"] = [ev_d[0], ev_sq[0]]
    z.free["sc1"] = [ev_d[1], ev_sq[1]]
    z.free["sc2"] = [ev_d[0], ev_sq[0]]
    z.free["sc3"] = [ev_d[1], ev_sq[1]]
    return [ev_sq[0], ev_sq[1]]


def plan_loss_ar_issue(P, z, call, ev_sq_pair):
    nslot = z.nt * C
    ev_red = None
    for n in range(NE):
        ev_red = P.op("vector", lambda e, n=n, nslot=nslot: e.tensor_reduce(
            z.lacc[:, n:n + 1], z.slots[:, n * nslot:(n + 1) * nslot], axis=AX.X,
            op=AO.add), waits=[ev_sq_pair, F(z, "lacc")], sem="v")
    z.free["slots"] = [ev_red]
    ev_cp = P.op("gpsimd", lambda e: e.tensor_reduce(
        z.arst[0:1, 0:8], z.lacc[:, 0:8], axis=AX.C, op=AO.add),
        waits=[ev_red, F(z, "arst")], sem="q")
    z.free["lacc"] = [ev_cp]
    ev_in = P.op("gpsimd", lambda e, call=call: e.dma_start(
        out=z.d_arin[call], in_=z.arst[0:1, 0:8]), waits=[ev_cp], sem="g")
    z.free["arst"] = [ev_in]
    ev_cc = P.op("gpsimd", lambda e, call=call: e.collective_compute(
        "AllReduce", AO.add,
        replica_groups=[[0, 1], [2, 3], [4, 5], [6, 7]],
        ins=[z.d_arin[call].opt()], outs=[z.d_arout[call].opt()]),
        waits=[ev_in], sem="c")
    ev_out = P.op("gpsimd", lambda e, call=call: e.dma_start(
        out=z.lrow[0:1, 0:8], in_=z.d_arout[call]),
        waits=[ev_cc, F(z, "lrow")], sem="g")
    z.ar_evcp = ev_cp
    return ev_out


def plan_mask(P, z, ev_out):
    ev_cp = z.ar_evcp
    rb = z.rowbuf
    ev_min = P.op("vector", lambda e: e.tensor_reduce(
        rb[0:1, 8:9], z.lrow[0:1, 0:8], axis=AX.X, op=AO.min),
        waits=[ev_out, z.ev_rb0, F(z, "rowbuf")], sem="v")
    ev_eq = P.op("vector", lambda e: e.tensor_scalar(
        rb[0:1, 16:24], z.lrow[0:1, 0:8], rb[0:1, 8:9], None, op0=AO.is_le),
        waits=[ev_min], sem="v")
    z.free["lrow"] = [ev_eq]
    ev_run = P.op("vector", lambda e: e.tensor_copy(
        rb[0:1, 24:25], z.consts[0:1, CZERO:CZERO + 1]), waits=[z.ev_con, ev_eq],
        sem="v")
    ev_nr = None
    for n in range(NE):
        ev_nr = P.op("vector", lambda e, n=n: e.tensor_scalar(
            rb[0:1, 56:57], rb[0:1, 24 + n:25 + n], z.consts[0:1, CNEG1:CNEG1 + 1],
            z.consts[0:1, 0:1], op0=AO.mult, op1=AO.add), waits=[ev_run, ev_eq], sem="v")
        ev_mk = P.op("vector", lambda e, n=n: e.tensor_tensor(
            rb[0:1, 33 + n:34 + n], rb[0:1, 16 + n:17 + n], rb[0:1, 56:57],
            op=AO.mult), waits=[ev_nr], sem="v")
        ev_run = P.op("vector", lambda e, n=n: e.tensor_tensor(
            rb[0:1, 25 + n:26 + n], rb[0:1, 24 + n:25 + n], rb[0:1, 33 + n:34 + n],
            op=AO.add), waits=[ev_mk], sem="v")
    ev_adp = None
    for j in range(NE):
        ev_e = P.op("vector", lambda e, j=j: e.tensor_scalar(
            rb[0:1, 48:56], z.cif[0:1, 0:8], z.consts[0:1, CJ0 + j:CJ0 + j + 1], None,
            op0=AO.is_equal), waits=[z.ev_cif, ev_run, ev_adp, ev_nr], sem="v")
        ev_p = P.op("vector", lambda e: e.tensor_tensor(
            rb[0:1, 48:56], rb[0:1, 48:56], rb[0:1, 33:41], op=AO.mult),
            waits=[ev_e], sem="v")
        ev_adp = P.op("vector", lambda e, j=j: e.tensor_reduce(
            rb[0:1, 41 + j:42 + j], rb[0:1, 48:56], axis=AX.X, op=AO.add),
            waits=[ev_p], sem="v")
    z.free["rowbuf"] = [ev_adp]
    z.ar_evadp = ev_adp
    return ev_adp


def plan_mask_bc(P, z):
    rb = z.rowbuf
    ev_bc = P.op("tensor", lambda e: e.matmul(
        z.p[6][:, 0:8], z.consts[0:1, 0:128], rb[0:1, 41:49], start=True, stop=True),
        waits=[z.ar_evadp, z.ar_evcp, F(z, "p6")], sem="t")
    ev_mskb = P.op("vector", lambda e: e.tensor_copy(z.mskb[:, 0:8], z.p[6][:, 0:8]),
                   waits=[ev_bc, F(z, "mskb")], sem="v")
    z.free["p6"] = [ev_mskb]
    return ev_mskb


def plan_select(P, z, ev_mskb):
    gd = F(z, "adapters")
    ev_wd = gd
    ev_wu = gd
    ev_bd = gd
    ev_bu = gd
    sc4_guard = F(z, "sc4")
    sc5_guard = F(z, "sc5")
    hsp0f = z.hsp0[:].bitcast(F32)
    hsp1f = z.hsp1[:].bitcast(F32)
    bufs = [(z.selbuf[:, 0:512], z.selbuf[0:64, 512:1536]),
            (hsp1f[:, 0:512], hsp0f[0:64, 0:1024])]
    ev_free = [F(z, "selbuf"), F(z, "selodd") + F(z, "hsp0") + F(z, "hsp1")]
    ev_loads = {}

    def load(n):
        b = n % 2
        wd_ap, wu_ap = bufs[b]
        sem = "ws" if b == 0 else "ws2"
        P.op("sync", lambda e, n=n, wd_ap=wd_ap: e.dma_start(
            out=wd_ap, in_=z.d_wdown[n]), waits=[ev_free[b]], sem=sem)
        ev_loads[n] = P.op("sync", lambda e, n=n, wu_ap=wu_ap: e.dma_start(
            out=wu_ap, in_=z.d_wup[n]), sem=sem)

    load(0)
    load(1)
    for n in range(NE):
        sb_wd, sb_wu = bufs[n % 2]
        ev_load = ev_loads[n]
        if n == 0:
            ev_wd = P.op("vector", lambda e, sb_wd=sb_wd: e.tensor_scalar(
                z.sc2[:, 0:512], sb_wd, z.mskb[:, 0:1], None, op0=AO.mult),
                waits=[ev_mskb, ev_load, ev_wd, F(z, "sc2")], sem="v")
            ev_wu = P.op("vector", lambda e, sb_wu=sb_wu: e.tensor_scalar(
                z.wupsel[0:64, :], sb_wu, z.mskb[0:64, 0:1], None,
                op0=AO.mult), waits=[ev_mskb, ev_load, ev_wu], sem="v")
            ev_bd = P.op("vector", lambda e: e.tensor_scalar(
                z.bsel[0:64, 0:1], z.bias[0:64, BDOWN0:BDOWN0 + 1],
                z.mskb[0:64, 0:1], None, op0=AO.mult),
                waits=[ev_mskb, z.ev_bias, ev_bd], sem="v")
            ev_bu = P.op("vector", lambda e: e.tensor_scalar(
                z.bsel[:, 2:10], z.bias[:, BUP0:BUP0 + 8],
                z.mskb[:, 0:1], None, op0=AO.mult),
                waits=[ev_mskb, z.ev_bias, ev_bu], sem="v")
        else:
            ev_s1 = P.op("vector", lambda e, sb_wd=sb_wd, n=n: e.tensor_scalar(
                z.sc4[:, 0:512], sb_wd, z.mskb[:, n:n + 1], None, op0=AO.mult),
                waits=[ev_mskb, ev_load, ev_wd, sc4_guard], sem="v")
            ev_wd = P.op("vector", lambda e: e.tensor_tensor(
                z.sc2[:, 0:512], z.sc2[:, 0:512], z.sc4[:, 0:512], op=AO.add),
                waits=[ev_s1], sem="v")
            for half in range(2):
                ev_s2 = P.op("vector", lambda e, sb_wu=sb_wu, n=n, half=half:
                             e.tensor_scalar(
                                 z.sc5[0:64, 0:512],
                                 sb_wu[0:64, half * 512:512 + half * 512],
                                 z.mskb[0:64, n:n + 1], None, op0=AO.mult),
                    waits=[ev_load, ev_wu, sc5_guard], sem="v")
                ev_wu = P.op("vector", lambda e, half=half: e.tensor_tensor(
                    z.wupsel[0:64, half * 512:512 + half * 512],
                    z.wupsel[0:64, half * 512:512 + half * 512],
                    z.sc5[0:64, 0:512], op=AO.add), waits=[ev_s2], sem="v")
            ev_b1 = P.op("vector", lambda e, n=n: e.tensor_scalar(
                z.bsel[0:64, 34:35], z.bias[0:64, BDOWN0 + n:BDOWN0 + n + 1],
                z.mskb[0:64, n:n + 1], None, op0=AO.mult), waits=[ev_bd], sem="v")
            ev_bd = P.op("vector", lambda e: e.tensor_tensor(
                z.bsel[0:64, 0:1], z.bsel[0:64, 0:1],
                z.bsel[0:64, 34:35], op=AO.add), waits=[ev_b1], sem="v")
            ev_b2 = P.op("vector", lambda e, n=n: e.tensor_scalar(
                z.bsel[:, 26:34], z.bias[:, BUP0 + n * 8:BUP0 + n * 8 + 8],
                z.mskb[:, n:n + 1], None, op0=AO.mult), waits=[ev_bu], sem="v")
            ev_bu = P.op("vector", lambda e: e.tensor_tensor(
                z.bsel[:, 2:10], z.bsel[:, 2:10],
                z.bsel[:, 26:34], op=AO.add), waits=[ev_b2], sem="v")
        ev_free[n % 2] = [ev_wu]
        if n + 2 < NE:
            load(n + 2)
    ev_bush = P.op("vector", lambda e: e.tensor_scalar(
        z.bsel[:, 18:26], z.bsel[:, 2:10],
        z.consts[:, CHALF:CHALF + 1], None, op0=AO.mult), waits=[ev_bu], sem="v")
    ev_wdh = P.op("vector", lambda e: e.tensor_copy(z.wdselh[:], z.sc2[:, 0:512]),
                  waits=[ev_wd, F(z, "wdselsp")], sem="v")
    ev_wdl = P.op("vector", lambda e: e.tensor_tensor(
        z.wdsell[:], z.sc2[:, 0:512], z.wdselh[:], op=AO.subtract),
        waits=[ev_wdh], sem="v")
    ev_wuh = P.op("vector", lambda e: e.tensor_copy(
        z.wupselh[0:64, :], z.wupsel[0:64, :]), waits=[ev_wu, F(z, "wupselsp")], sem="v")
    ev_wul = P.op("vector", lambda e: e.tensor_tensor(
        z.wupsell[0:64, :], z.wupsel[0:64, :], z.wupselh[0:64, :], op=AO.subtract),
        waits=[ev_wuh], sem="v")
    z.free["selbuf"] = ev_free[0]
    z.free["selodd"] = ev_free[1]
    z.free["hsp0"] = F(z, "hsp0") + ev_free[1]
    z.free["hsp1"] = F(z, "hsp1") + ev_free[1]
    z.free["sc2"] = [ev_wdl]
    z.free["sc4"] = [ev_wdh]
    z.free["sc5"] = [ev_wul]
    return ev_wdl, ev_wul, ev_bd, ev_bush


def plan_base(P, z, ev_xn, pre=None):
    ttot, nt = z.ttot, z.nt
    TT = ttot // nt
    ev_sp = [None, None]
    sp_guard = [F(z, "wencsp0"), F(z, "wencsp1")]
    ev_mm_free = [None, None]
    quarters = [(i, q) for i in range(nt) for q in range(4)]

    def load(k):
        i, q = quarters[k]
        b = k % 2
        ev_sp[b] = plan_wstream(
            P, z, z.d_wbase[:, q * 2048:(q + 1) * 2048], z.wencsp[b],
            [sp_guard[b], ev_mm_free[b]])
        sp_guard[b] = None

    if pre is not None:
        ev_sp[0], ev_sp[1] = pre
        sp_guard = [None, None]
    else:
        load(0)
        load(1)
    ev_evac = {0: F(z, "p2"), 1: F(z, "p3"), 2: F(z, "p4"), 3: F(z, "p5")}
    ev_add = {0: F(z, "sc0"), 1: F(z, "sc1")}
    ev_last_add = None
    tile_adds = []
    for k, (i, q) in enumerate(quarters):
        b = k % 2
        t0 = i * TT
        for eo2 in range(2):
            eo = q * 2 + eo2
            pb = 2 + 2 * (q % 2) + eo2

            def emit(e, b=b, eo2=eo2, t0=t0, pb=pb, TT=TT):
                return _mm3(
                    e, z.p[pb][:, 0:TT], z.wencsp[b], 0, 2048,
                    lambda c, eo2=eo2: (c * 256 + eo2 * 128, c * 256 + eo2 * 128 + 128),
                    z.xnh, z.xnl,
                    lambda c, t0=t0: (c * ttot + t0, c * ttot + t0 + TT),
                    first=True)
            ev_mm = P.op("tensor", emit,
                         waits=[ev_xn[i], ev_sp[b], ev_evac[pb - 2]], sem="t")
            if eo2 == 1:
                ev_mm_free[b] = ev_mm
            scr = (z.sc0, z.sc1)[eo % 2]
            ev_evac[pb - 2] = P.op(
                "scalar", lambda e, eo=eo, pb=pb, scr=scr, TT=TT: e.activation(
                    out=scr[:, 0:TT], in_=z.p[pb][:, 0:TT], func=AF.Identity,
                    scale=z.consts[:, CHALF:CHALF + 1],
                    bias=z.halfbase[:, eo:eo + 1]),
                waits=[ev_mm, z.ev_halfbase, ev_add[eo % 2]], sem="s")
            off = eo * ttot + t0
            ev_add[eo % 2] = P.op(
                "gpsimd", lambda e, off=off, scr=scr, pb=pb, TT=TT: e.tensor_tensor(
                    z.x[:, off:off + TT], z.x[:, off:off + TT], scr[:, 0:TT],
                    op=AO.add), waits=[ev_evac[pb - 2], ev_add[eo % 2]], sem="q")
            ev_last_add = ev_add[eo % 2]
        if q == 3:
            tile_adds.append([ev_add[0], ev_add[1]])
        if k + 2 < len(quarters):
            load(k + 2)
    z.free["p2"], z.free["p3"] = [ev_evac[0]], [ev_evac[1]]
    z.free["p4"], z.free["p5"] = [ev_evac[2]], [ev_evac[3]]
    z.free["wencsp0"] = [ev_mm_free[0]]
    z.free["wencsp1"] = [ev_mm_free[1]]
    z.free["sc0"] = [ev_add[0]]
    z.free["sc1"] = [ev_add[1]]
    return ev_last_add, [ev_add[0], ev_add[1]], tile_adds


def plan_adapter(P, z, ev_xn, ev_sel):
    ttot, nt = z.ttot, z.nt
    TT = ttot // nt
    ev_wdl, ev_wul, ev_bd, ev_bush = ev_sel
    mid_guard = F(z, "sc4")
    ubank = (z.p[3], z.p[4], z.p[5])
    mbank = (z.p[2], z.p[6])
    msph = (z.msph[:, :], z.hsp0[0:64, 0:512])
    mspl = (z.mspl[:, :], z.hsp0[0:64, 512:1024])
    ev_up_evac = {0: F(z, "p3"), 1: F(z, "p4"), 2: F(z, "p5")}
    ev_add = {0: F(z, "sc2"), 1: F(z, "sc3")}
    mid_free = {0: F(z, "p2"), 1: F(z, "p6")}
    msp_free = {0: F(z, "msp"), 1: F(z, "msp") + F(z, "hsp0") + F(z, "selodd")}
    ev_msp_par = {}
    st = {"up_last": None, "mid_relu": None, "mid": None, "msp": None,
          "last": None}
    tile_adds = []

    def mid_part(i):
        t0 = i * TT
        par = i % 2
        mb = mbank[par]

        def emit_mid(e, t0=t0, TT=TT, mb=mb):
            last = None
            first = True
            for (wt, xt) in ((z.wdselh, z.xnh), (z.wdselh, z.xnl), (z.wdsell, z.xnh)):
                for c in range(C):
                    last = e.matmul(
                        mb[0:64, 0:TT], wt[:, c * 64:(c + 1) * 64],
                        xt[:, c * ttot + t0: c * ttot + t0 + TT],
                        start=first, stop=(wt is z.wdsell and c == C - 1))
                    first = False
            return last
        ev_mid = P.op("tensor", emit_mid,
                      waits=[ev_xn[i], ev_wdl, mid_free[par]], sem="t")
        st["mid"] = ev_mid
        ev_mid_relu = P.op("scalar", lambda e, TT=TT, mb=mb: e.activation(
            out=z.sc4[0:64, 0:TT], in_=mb[0:64, 0:TT], func=AF.Relu,
            bias=z.bsel[0:64, 0:1]),
            waits=[ev_mid, ev_bd, mid_guard, st["msp"]], sem="s")
        st["mid_relu"] = ev_mid_relu
        mid_free[par] = [ev_mid_relu]
        ev_mh = P.op("vector", lambda e, TT=TT, par=par: e.tensor_copy(
            msph[par][0:64, 0:TT], z.sc4[0:64, 0:TT]),
            waits=[ev_mid_relu, msp_free[par]], sem="v")
        ev_msp = P.op("vector", lambda e, TT=TT, par=par: e.tensor_tensor(
            mspl[par][0:64, 0:TT], z.sc4[0:64, 0:TT], msph[par][0:64, 0:TT],
            op=AO.subtract), waits=[ev_mh], sem="v")
        st["msp"] = ev_msp
        ev_msp_par[par] = ev_msp
        mid_guard[:] = [ev_msp]

    def up_part(i):
        t0 = i * TT
        par = i % 2
        for eo in range(C):
            ub = eo % 3
            bank = ubank[ub]

            def emit_up(e, eo=eo, bank=bank, TT=TT, par=par):
                last = None
                first = True
                for (wt, mt) in ((z.wupselh, msph[par]), (z.wupselh, mspl[par]),
                                 (z.wupsell, msph[par])):
                    last = e.matmul(
                        bank[:, 0:TT], wt[0:64, eo * 128:(eo + 1) * 128],
                        mt[0:64, 0:TT], start=first,
                        stop=(wt is z.wupsell))
                    first = False
                return last
            ev_up = P.op("tensor", emit_up,
                         waits=[ev_msp_par[par], ev_wul, ev_up_evac[ub]], sem="t")
            st["up_last"] = ev_up
            if eo == C - 1:
                msp_free[par] = [ev_up]
            scr = (z.sc2, z.sc3)[eo % 2]
            ev_evac_cur = P.op(
                "scalar", lambda e, eo=eo, bank=bank, scr=scr, TT=TT: e.activation(
                    out=scr[:, 0:TT], in_=bank[:, 0:TT], func=AF.Identity,
                    scale=z.consts[:, CHALF:CHALF + 1],
                    bias=z.bsel[:, 18 + eo:19 + eo]),
                waits=[ev_up, ev_bush, ev_add[eo % 2]], sem="s")
            ev_up_evac[ub] = ev_evac_cur
            off = eo * ttot + t0
            ev_add[eo % 2] = P.op(
                "gpsimd", lambda e, off=off, scr=scr, TT=TT: e.tensor_tensor(
                    z.x[:, off:off + TT], z.x[:, off:off + TT], scr[:, 0:TT],
                    op=AO.add), waits=[ev_evac_cur, ev_add[eo % 2]], sem="q")
            st["last"] = ev_add[eo % 2]
        tile_adds.append([ev_add[0], ev_add[1]])

    mid_guard = list(mid_guard)
    for step in range(nt + 1):
        if step < nt:
            mid_part(step)
        if step >= 1:
            up_part(step - 1)
    z.free["p2"] = mid_free[0]
    z.free["p6"] = mid_free[1]
    z.free["p3"] = [ev_up_evac[0]]
    z.free["p4"] = [ev_up_evac[1]]
    z.free["p5"] = [ev_up_evac[2]]
    z.free["sc4"] = [st["msp"]]
    z.free["msp"] = msp_free[0] + msp_free[1]
    z.free["hsp0"] = F(z, "hsp0") + msp_free[1] + [ev_msp_par.get(1)]
    z.free["adapters"] = [st["up_last"], ev_up_evac[0], ev_up_evac[1],
                          st["mid_relu"]]
    z.free["wdselsp"] = [st["mid"]]
    z.free["wupselsp"] = [st["up_last"]]
    z.free["sc2"] = [ev_add[0]]
    z.free["sc3"] = [ev_add[1]]
    return st["last"], [ev_add[0], ev_add[1]], tile_adds


def plan_w23(P, z):
    ev_w2 = P.op("sync", lambda e: e.dma_start(out=z.wtrans[:, 0:128], in_=z.d_w2),
                 waits=[F(z, "wtrans")], sem="we0")
    ev_w3 = P.op("sync", lambda e: e.dma_start(out=z.wtrans[:, 128:256], in_=z.d_w3),
                 sem="we0")
    ev_w23h = P.op("vector", lambda e: e.tensor_copy(
        z.w23sp[:, 0:256], z.wtrans[:, 0:256]),
        waits=[ev_w2, ev_w3, F(z, "w23sp")], sem="v")
    ev_w23l = P.op("vector", lambda e: e.tensor_tensor(
        z.w23sp[:, 256:512], z.wtrans[:, 0:256], z.w23sp[:, 0:256], op=AO.subtract),
        waits=[ev_w23h], sem="v")
    z.free["wtrans"] = [ev_w23l]
    return ev_w23l


def plan_mixer(P, z, ev_xn, pre=None):
    ttot, nt = z.ttot, z.nt
    TT = ttot // nt
    ev_sp = [None, None]
    sp_guard = [F(z, "wencsp0"), F(z, "wencsp1")]
    ev_mm_free = [None, None]
    quarters = [(i, q) for i in range(nt) for q in range(4)]

    def load(k):
        i, q = quarters[k]
        b = k % 2
        ev_sp[b] = plan_wstream(
            P, z, z.d_w1[:, q * 2048:(q + 1) * 2048], z.wencsp[b],
            [sp_guard[b], ev_mm_free[b]])
        sp_guard[b] = None

    ev_w2 = P.op("sync", lambda e: e.dma_start(out=z.wtrans[:, 0:128], in_=z.d_w2),
                 waits=[F(z, "wtrans")], sem="we0")
    ev_w3 = P.op("sync", lambda e: e.dma_start(out=z.wtrans[:, 128:256], in_=z.d_w3),
                 sem="we0")
    ev_w23h = P.op("vector", lambda e: e.tensor_copy(
        z.w23sp[:, 0:256], z.wtrans[:, 0:256]),
        waits=[ev_w2, ev_w3, F(z, "w23sp")], sem="v")
    ev_w23l = P.op("vector", lambda e: e.tensor_tensor(
        z.w23sp[:, 256:512], z.wtrans[:, 0:256], z.w23sp[:, 0:256], op=AO.subtract),
        waits=[ev_w23h], sem="v")
    z.free["wtrans"] = [ev_w23l]

    load(0)
    load(1)

    ev_BCe = {0: F(z, "p4"), 1: F(z, "p5")}
    ev_sbc_used = F(z, "p6")
    ev_sbc = None
    ev_sbc_prev = None
    ev_evac = {0: F(z, "p2"), 1: F(z, "p3"), 2: [], 3: []}
    ev_u0_guard = {0: F(z, "sc0"), 1: F(z, "sc1")}
    ev_dd_guard = {0: F(z, "sc2"), 1: F(z, "sc3")}
    sc4_guard = F(z, "sc4")
    sc5_guard = F(z, "sc5")
    ev_BCp_prev = None
    ev_Bm = ev_Cm = None
    ev_last = None
    tile_adds = []
    for k, (i, q) in enumerate(quarters):
        b = k % 2
        t0 = i * TT
        if q == 0:
            def emit_B(e, t0=t0, TT=TT):
                return _mm3(
                    e, z.p[4][0:16, 0:TT], z.w23sp, 0, 256,
                    lambda c: (c * 16, (c + 1) * 16),
                    z.xnh, z.xnl,
                    lambda c, t0=t0: (c * ttot + t0, c * ttot + t0 + TT),
                    first=True)

            def emit_Cm(e, t0=t0, TT=TT):
                return _mm3(
                    e, z.p[5][0:16, 0:TT], z.w23sp, 128, 384,
                    lambda c: (c * 16, (c + 1) * 16),
                    z.xnh, z.xnl,
                    lambda c, t0=t0: (c * ttot + t0, c * ttot + t0 + TT),
                    first=True)
            ev_Bm = P.op("tensor", emit_B,
                         waits=[ev_xn[i], ev_w23l, ev_BCe[0], ev_evac[2]], sem="t")
            ev_Cm = P.op("tensor", emit_Cm,
                         waits=[ev_w23l, ev_BCe[1], ev_evac[3]], sem="t")
            ev_Be = P.op("scalar", lambda e, TT=TT: e.activation(
                out=z.sc4[0:16, 0:TT], in_=z.p[4][0:16, 0:TT], func=AF.Identity,
                bias=z.bias[0:16, B2_0:B2_0 + 1]),
                waits=[ev_Bm, z.ev_bias, ev_sbc_prev, sc4_guard], sem="s")
            ev_Ce = P.op("scalar", lambda e, TT=TT: e.activation(
                out=z.sc5[0:16, 0:TT], in_=z.p[5][0:16, 0:TT], func=AF.Identity,
                bias=z.bias[0:16, B3_0:B3_0 + 1]),
                waits=[ev_Cm, z.ev_bias, ev_BCp_prev, sc5_guard], sem="s")
            ev_BCe[0], ev_BCe[1] = [ev_Be], [ev_Ce]
            ev_BCp = P.op("vector", lambda e, TT=TT: e.tensor_tensor(
                z.sc4[0:16, 0:TT], z.sc4[0:16, 0:TT], z.sc5[0:16, 0:TT], op=AO.mult),
                waits=[ev_Be, ev_Ce], sem="v")
            ev_BCp_prev = ev_BCp
            ev_sbc = P.op("tensor", lambda e, TT=TT: e.matmul(
                z.p[6][:, 0:TT], z.consts[0:16, 0:128], z.sc4[0:16, 0:TT],
                start=True, stop=True), waits=[ev_BCp, ev_sbc_used], sem="t")
            ev_sbc_prev = ev_sbc
        for eo2 in range(2):
            eo = q * 2 + eo2
            pb4 = eo % 4
            bank = (z.p[2], z.p[3], z.p[4], z.p[5])[pb4]

            def emit_w1(e, b=b, eo2=eo2, t0=t0, bank=bank, TT=TT):
                return _mm3(
                    e, bank[:, 0:TT], z.wencsp[b], 0, 2048,
                    lambda c, eo2=eo2: (c * 256 + eo2 * 128, c * 256 + eo2 * 128 + 128),
                    z.xnh, z.xnl,
                    lambda c, t0=t0: (c * ttot + t0, c * ttot + t0 + TT),
                    first=True)
            ev_mm = P.op("tensor", emit_w1,
                         waits=[ev_xn[i], ev_sp[b], ev_evac[pb4],
                                ev_BCe[0] if pb4 == 2 else None,
                                ev_BCe[1] if pb4 == 3 else None], sem="t")
            if eo2 == 1:
                ev_mm_free[b] = ev_mm
            u0 = (z.sc0, z.sc1)[eo2]
            dd = (z.sc2, z.sc3)[eo2]
            ev_u0 = P.op("scalar", lambda e, eo=eo, bank=bank, u0=u0, TT=TT: e.activation(
                out=u0[:, 0:TT], in_=bank[:, 0:TT], func=AF.Identity,
                bias=z.bias[:, B1_0 + eo:B1_0 + eo + 1]),
                waits=[ev_mm, z.ev_bias, ev_u0_guard[eo2]], sem="s")
            ev_evac[pb4] = [ev_u0]
            ev_uc = P.op("vector", lambda e, u0=u0, dd=dd, TT=TT: e.tensor_scalar(
                dd[:, 0:TT], u0[:, 0:TT], z.consts[:, C30:C30 + 1], None, op0=AO.min),
                waits=[ev_u0, ev_dd_guard[eo2]], sem="v")
            ev_ex = P.op("scalar", lambda e, dd=dd, TT=TT: e.activation(
                out=dd[:, 0:TT], in_=dd[:, 0:TT], func=AF.Exp), waits=[ev_uc], sem="s")
            ev_ln = P.op("scalar", lambda e, dd=dd, TT=TT: e.activation(
                out=dd[:, 0:TT], in_=dd[:, 0:TT], func=AF.Ln,
                bias=z.consts[:, 0:1]), waits=[ev_ex], sem="s")
            ev_dl = P.op("vector", lambda e, u0=u0, dd=dd, TT=TT: e.tensor_tensor(
                dd[:, 0:TT], u0[:, 0:TT], dd[:, 0:TT], op=AO.max),
                waits=[ev_ln], sem="v")
            off = eo * ttot + t0
            ev_xnf = P.op("vector", lambda e, off=off, u0=u0, TT=TT: e.tensor_tensor(
                u0[:, 0:TT], z.xnh[:, off:off + TT], z.xnl[:, off:off + TT],
                op=AO.add), waits=[ev_dl], sem="v")
            ev_p1 = P.op("vector", lambda e, u0=u0, dd=dd, TT=TT: e.tensor_tensor(
                dd[:, 0:TT], u0[:, 0:TT], dd[:, 0:TT], op=AO.mult),
                waits=[ev_xnf], sem="v")
            ev_u0_guard[eo2] = [ev_p1]
            ev_p2 = P.op("vector", lambda e, dd=dd, TT=TT: e.tensor_tensor(
                dd[:, 0:TT], dd[:, 0:TT], z.p[6][:, 0:TT], op=AO.mult),
                waits=[ev_p1, ev_sbc], sem="v")
            ev_add = P.op("vector", lambda e, off=off, dd=dd, TT=TT: e.tensor_tensor(
                z.x[:, off:off + TT], z.x[:, off:off + TT], dd[:, 0:TT], op=AO.add),
                waits=[ev_p2], sem="v")
            ev_dd_guard[eo2] = [ev_add]
            ev_last = ev_add
            if eo == 7:
                ev_sbc_used = [ev_p2]
        if q == 3:
            tile_adds.append([ev_dd_guard[0], ev_dd_guard[1]])
        if k + 2 < len(quarters):
            load(k + 2)
    z.free["p2"], z.free["p3"] = ev_evac[0], ev_evac[1]
    z.free["p4"] = ev_BCe[0] + ev_evac[2]
    z.free["p5"] = ev_BCe[1] + ev_evac[3]
    z.free["p6"] = ev_sbc_used
    z.free["wencsp0"] = [ev_mm_free[0]]
    z.free["wencsp1"] = [ev_mm_free[1]]
    z.free["w23sp"] = [ev_Bm, ev_Cm]
    z.free["sc0"] = ev_u0_guard[0]
    z.free["sc1"] = ev_u0_guard[1]
    z.free["sc2"] = ev_dd_guard[0]
    z.free["sc3"] = ev_dd_guard[1]
    z.free["sc4"] = [ev_sbc]
    z.free["sc5"] = [ev_BCp_prev]
    return ev_last, tile_adds



def build_core_program(ttot=2048, nt=4, dbg=False):
    TT = ttot // nt
    assert TT <= 512
    nc = bass.Bass("TRN2", target_bir_lowering=False)
    z = Ctx()
    z.ttot, z.nt = ttot, nt
    z.free = {}
    z.d_x = nc.dram_tensor("x", [128, C * ttot], F32, kind="ExternalInput").ap()
    z.d_wenc = nc.dram_tensor("wenc", [NE, 128, 2048], F32, kind="ExternalInput").ap()
    z.d_wdec = nc.dram_tensor("wdec", [NE, 128, 2048], F32, kind="ExternalInput").ap()
    z.d_wbase = nc.dram_tensor("wbase", [128, 8192], F32, kind="ExternalInput").ap()
    z.d_w1 = nc.dram_tensor("w1", [128, 8192], F32, kind="ExternalInput").ap()
    z.d_w2 = nc.dram_tensor("w2", [128, 128], F32, kind="ExternalInput").ap()
    z.d_w3 = nc.dram_tensor("w3", [128, 128], F32, kind="ExternalInput").ap()
    z.d_wdown = nc.dram_tensor("wdown", [NE, 128, 512], F32, kind="ExternalInput").ap()
    z.d_wup = nc.dram_tensor("wup", [NE, 64, 1024], F32, kind="ExternalInput").ap()
    z.d_biases = nc.dram_tensor("biases", [128, NBIAS], F32, kind="ExternalInput").ap()
    z.d_consts = nc.dram_tensor("consts", [128, NCONST], F32, kind="ExternalInput").ap()
    z.d_ci = nc.dram_tensor("ci", [1, 8], I32, kind="ExternalInput").ap()
    z.d_y = nc.dram_tensor("y", [128, C * ttot], F32, kind="ExternalOutput").ap()
    z.d_arin = [nc.dram_tensor(f"arin{k}", [1, 8], F32).ap() for k in range(2)]
    z.d_arout = [nc.dram_tensor(f"arout{k}", [1, 8], F32).ap() for k in range(2)]
    if dbg:
        z.d_dbg_xn = nc.dram_tensor("dbg_xn", [128, C * ttot], BF16,
                                    kind="ExternalOutput").ap()
        z.d_dbg_xnl = nc.dram_tensor("dbg_xnl", [128, C * ttot], BF16,
                                     kind="ExternalOutput").ap()
        z.d_dbg_row = nc.dram_tensor("dbg_row", [1, 64], F32, kind="ExternalOutput").ap()
        z.d_dbg_lacc = nc.dram_tensor("dbg_lacc", [128, 8], F32,
                                      kind="ExternalOutput").ap()
        z.d_dbg_sel = nc.dram_tensor("dbg_sel", [128, 512], BF16,
                                     kind="ExternalOutput").ap()
        z.d_dbg_x1 = nc.dram_tensor("dbg_x1", [128, C * ttot], F32,
                                    kind="ExternalOutput").ap()
        z.d_dbg_x2 = nc.dram_tensor("dbg_x2", [128, C * ttot], F32,
                                    kind="ExternalOutput").ap()

    es = contextlib.ExitStack()
    sb = lambda name, shape, dt=F32: es.enter_context(nc.sbuf_tensor(name, shape, dt))
    with es:
        z.consts = sb("consts_sb", [128, NCONST])
        z.bias = sb("bias_sb", [128, NBIAS])
        z.halfbase = sb("halfbase", [128, 8])
        z.ci = sb("ci_sb", [1, 8], I32)
        z.cif = sb("cif", [1, 8])
        z.x = sb("x_sb", [128, C * ttot])
        z.xnh = sb("xnh", [128, C * ttot], BF16)
        z.xnl = sb("xnl", [128, C * ttot], BF16)
        z.wtrans = sb("wtrans", [128, 2048])
        z.wencsp = [sb("wencsp0", [128, 4096], BF16), sb("wencsp1", [128, 4096], BF16)]
        z.wdecsp = sb("wdecsp", [128, 4096], BF16)
        z.hsp0 = sb("hsp0", [128, 4 * TT], BF16)
        z.hsp1 = sb("hsp1", [128, 4 * TT], BF16)
        z.scrM = sb("scrM", [128, TT])
        z.scrR = sb("scrR", [128, TT])
        z.sc0 = sb("sc0", [128, 512])
        z.sc1 = sb("sc1", [128, 512])
        z.sc2 = sb("sc2", [128, 512])
        z.sc3 = sb("sc3", [128, 512])
        z.sc4 = sb("sc4", [128, 512])
        z.sc5 = sb("sc5", [128, 512])
        z.slots = sb("slots", [128, NE * nt * C])
        z.lacc = sb("lacc", [128, 8])
        z.rowbuf = sb("rowbuf", [1, 64])
        z.arst = sb("arst", [1, 8])
        z.lrow = sb("lrow", [1, 8])
        z.mskb = sb("mskb", [128, 8])
        z.selbuf = sb("selbuf", [128, 1536])
        z.wupsel = sb("wupsel", [64, 1024])
        z.wdselh = sb("wdselh", [128, 512], BF16)
        z.wdsell = sb("wdsell", [128, 512], BF16)
        z.wupselh = sb("wupselh", [64, 1024], BF16)
        z.wupsell = sb("wupsell", [64, 1024], BF16)
        z.msph = sb("msph", [64, 512], BF16)
        z.mspl = sb("mspl", [64, 512], BF16)
        z.bsel = sb("bsel", [128, 35])
        z.w23sp = sb("w23sp", [128, 512], BF16)
        z.p = [es.enter_context(nc.psum_tensor(f"pb{k}", [128, 512], F32))
               for k in range(8)]

        sems = {name: es.enter_context(nc.semaphore(name))
                for name in ("we0", "ws", "ws2", "g", "t", "s", "v", "q", "c")}
        P = Pipe(sems)
        P.chain["g"] = None

        plan_prelude(P, z)
        gate = []
        base_tiles = ad_tiles = None
        base_adds = ad_adds = []
        for call in range(2):
            exp_pre = (
                plan_wstream(P, z, z.d_wenc[0], z.wencsp[0], [F(z, "wencsp0")]),
                plan_wstream(P, z, z.d_wdec[0], z.wdecsp, [F(z, "wdecsp")]),
                plan_wstream(P, z, z.d_wenc[1], z.wencsp[1], [F(z, "wencsp1")]),
            )
            ev_xn = plan_stats_xn(P, z, gate)
            pre_box = {}

            def tail_hook(g0, g1, pre_box=pre_box):
                pre_box["sp0"] = plan_wstream(P, z, z.d_wbase[:, 0:2048],
                                              z.wencsp[0], [g0])
                pre_box["sp1"] = plan_wstream(P, z, z.d_wbase[:, 2048:4096],
                                              z.wencsp[1], [g1])
            ev_sq_pair = plan_experts(P, z, ev_xn, tail_hook, pre=exp_pre)
            ev_lrow = plan_loss_ar_issue(P, z, call, ev_sq_pair)
            plan_mask(P, z, ev_lrow)
            ev_base, base_adds, base_tiles = plan_base(
                P, z, ev_xn, pre=(pre_box["sp0"], pre_box["sp1"]))
            ev_mskb = plan_mask_bc(P, z)
            ev_sel = plan_select(P, z, ev_mskb)
            ev_ad, ad_adds, ad_tiles = plan_adapter(P, z, ev_xn, ev_sel)
            z.free["mskb"] = [ev_sel[3]]
            dbg_evs = []
            if dbg and call == 0:
                dbg_evs.append(P.op(
                    "gpsimd", lambda e: e.dma_start(out=z.d_dbg_xn, in_=z.xnh[:]),
                    waits=[ev_sq_pair], sem="g"))
                dbg_evs.append(P.op(
                    "gpsimd", lambda e: e.dma_start(out=z.d_dbg_xnl, in_=z.xnl[:]),
                    waits=[ev_sq_pair], sem="g"))
                dbg_evs.append(P.op(
                    "gpsimd", lambda e: e.dma_start(out=z.d_dbg_lacc, in_=z.lacc[:]),
                    waits=[z.free["lacc"]], sem="g"))
                dbg_evs.append(P.op(
                    "gpsimd", lambda e: e.dma_start(
                        out=z.d_dbg_row, in_=z.rowbuf[0:1, 0:64]),
                    waits=[ev_mskb], sem="g"))
                dbg_evs.append(P.op(
                    "gpsimd", lambda e: e.dma_start(out=z.d_dbg_sel, in_=z.wdselh[:]),
                    waits=[ev_sel[0]], sem="g"))
                dbg_evs.append(P.op(
                    "gpsimd", lambda e: e.dma_start(out=z.d_dbg_x1, in_=z.x[:]),
                    waits=[base_adds, ad_adds], sem="g"))
            if call == 0:
                mix_pre = (
                    plan_wstream(P, z, z.d_w1[:, 0:2048], z.wencsp[0],
                                 [F(z, "wencsp0")]),
                    plan_wstream(P, z, z.d_w1[:, 2048:4096], z.wencsp[1],
                                 [F(z, "wencsp1")]),
                    plan_w23(P, z),
                )
                gates = [base_tiles[i] + ad_tiles[i] + dbg_evs for i in range(nt)]
                ev_mix, mix_tiles = plan_mixer(P, z, plan_stats_xn(P, z, gates),
                                               pre=mix_pre)
                gate = [mix_tiles[i] for i in range(nt)]
                if dbg:
                    ev_dx2 = P.op(
                        "gpsimd", lambda e: e.dma_start(out=z.d_dbg_x2, in_=z.x[:]),
                        waits=[ev_mix], sem="g")
                    gate = [g + [ev_dx2] for g in gate]
        yd3 = z.d_y.rearrange("p (c t) -> p c t", c=C)
        ys3 = z.x[:].rearrange("p (c t) -> p c t", c=C)
        ev_out = None
        for i in range(nt):
            ev_out = P.op(
                "gpsimd", lambda e, i=i: e.dma_start(
                    out=yd3[:, :, i * TT:(i + 1) * TT],
                    in_=ys3[:, :, i * TT:(i + 1) * TT]),
                waits=[base_tiles[i], ad_tiles[i]], sem="g")
        P.wait_only("gpsimd", [ev_out])
        with nc.Block() as block:
            P.emit_into(block)
    return nc, z


_BUILD_CACHE = {}


def _get_program(ttot=2048, nt=4, dbg=False):
    key = (ttot, nt, dbg)
    if key not in _BUILD_CACHE:
        _BUILD_CACHE[key] = build_core_program(ttot, nt, dbg)
    return _BUILD_CACHE[key]



def _fm(a):
    Dd, T = a.shape
    return np.ascontiguousarray(
        a.reshape(Dd // 128, 128, T).transpose(1, 0, 2).reshape(128, (Dd // 128) * T))


def make_in_maps(inputs, ttot=2048):
    f32 = lambda v: np.asarray(v, dtype=np.float32)
    x = f32(inputs["L0_attn_x"]).reshape(-1, D)
    W1 = f32(inputs["L0_attn_W1"])
    b1 = f32(inputs["L0_attn_b1"])
    W2 = f32(inputs["L0_attn_W2"])
    b2 = f32(inputs["L0_attn_b2"])
    W3 = f32(inputs["L0_attn_W3"])
    b3 = f32(inputs["L0_attn_b3"])
    Wb = f32(inputs["L0_ffn_W_base"])
    bb = f32(inputs["L0_ffn_b_base"])
    We = f32(inputs["L0_ffn_W_enc"])
    be = f32(inputs["L0_ffn_b_enc"])
    Wd = f32(inputs["L0_ffn_W_dec"])
    bd = f32(inputs["L0_ffn_b_dec"])
    Wdn = f32(inputs["L0_ffn_W_down"])
    bdn = f32(inputs["L0_ffn_b_down"])
    Wup = f32(inputs["L0_ffn_W_up"])
    bup = f32(inputs["L0_ffn_b_up"])
    ci = np.asarray(inputs["L0_ffn_connected_idx"], dtype=np.int32).reshape(1, 8)

    wenc = np.stack([_fm(We[n]) for n in range(NE)])
    wdec = np.stack([_fm(Wd[n]) for n in range(NE)])

    def _quarter_layout(W):
        a = W.reshape(C, 128, 2, 2, 2, 128)
        a = a.transpose(1, 2, 3, 0, 4, 5)
        return np.ascontiguousarray(a.reshape(128, 8192))
    wbase = _quarter_layout(Wb)
    w1 = _quarter_layout(W1)
    w2 = _fm(W2)
    w3 = _fm(W3)
    wdown = np.stack([_fm(Wdn[n]) for n in range(NE)])
    wupl = np.ascontiguousarray(Wup)

    biases = np.zeros((128, NBIAS), np.float32)
    biases[:, B1_0:B1_0 + 8] = b1.reshape(C, 128).T
    biases[:, BBASE0:BBASE0 + 8] = bb.reshape(C, 128).T
    biases[0:16, B2_0] = b2
    biases[0:16, B3_0] = b3
    for n in range(NE):
        biases[:, BENC0 + n * 2:BENC0 + n * 2 + 2] = be[n].reshape(2, 128).T
        biases[:, BDEC0 + n * 8:BDEC0 + n * 8 + 8] = bd[n].reshape(C, 128).T
        biases[0:64, BDOWN0 + n] = bdn[n]
        biases[:, BUP0 + n * 8:BUP0 + n * 8 + 8] = bup[n].reshape(C, 128).T

    consts = np.zeros((128, NCONST), np.float32)
    consts[:, 0:128] = 1.0
    consts[:, CEPS] = 1e-5
    consts[:, CINV] = 1.0 / 1024.0
    consts[:, C30] = 30.0
    consts[:, CHALF] = 0.5
    consts[:, CZERO] = 0.0
    consts[:, CNEG1] = -1.0
    for j in range(8):
        consts[:, CJ0 + j] = float(j)
    consts[:, CM05] = -0.5

    shared = dict(wenc=wenc, wdec=wdec, wbase=wbase, w1=w1, w2=w2, w3=w3,
                  wdown=wdown, wup=wupl, biases=biases, consts=consts, ci=ci)
    in_maps = []
    for core in range(NCORES):
        xc = x[core * ttot:(core + 1) * ttot].T
        m = dict(shared)
        m["x"] = _fm(xc)
        in_maps.append(m)
    return in_maps


def assemble_out(results, ttot=2048):
    parts = []
    for r in results:
        y = np.asarray(r["y"])
        a = y.reshape(128, C, ttot).transpose(1, 0, 2).reshape(D, ttot).T
        parts.append(a)
    out = np.concatenate(parts, axis=0)
    return np.ascontiguousarray(out.reshape(4, 4096, D), dtype=np.float32)


def kernel(**inputs):
    ttot = 2048
    nc, _ = _get_program(ttot=ttot, nt=4, dbg=False)
    in_maps = make_in_maps(inputs, ttot=ttot)
    res = run_bass_kernel_spmd(nc, in_maps, core_ids=list(range(NCORES)))
    return assemble_out(res.results, ttot=ttot)
